# version 03
# baseline (speedup 1.0000x reference)
import numpy as np
import concourse.bass as bass
import concourse.mybir as mybir
from concourse.bass_utils import run_bass_kernel_spmd

F32 = mybir.dt.float32
BF16 = mybir.dt.bfloat16
AF = mybir.ActivationFunctionType
ALU = mybir.AluOpType

NCORES = 8
H = 2048
I = 4096
NH = 64
PD = 64
NS = 64
CH = 256
CONV = 4224
PROJ = 8384
T = 1024
TH = T + 3
NCHK = T // CH
KT = H // 128
EPS = 1e-5
INTER = 8192

ENGINES = ("tensor", "vector", "scalar", "gpsimd", "sync")
V, SC, TE, GP, SY = "vector", "scalar", "tensor", "gpsimd", "sync"

SLABS_W1 = ([(o, 512) for o in range(0, I, 512)]
            + [(o, 512) for o in range(I, 8192, 512)]
            + [(8192, 128), (8320, 64)])
SLABS_W2 = [(o, 256) for o in range(0, H, 256)]
SLABS_W3 = [x for gi in range(INTER // 512)
            for x in ((gi * 512, 512), (INTER + gi * 512, 512))]
SLABS_W4 = [(o, 128) for o in range(0, H, 128)]


def _slab_offsets(slabs, kb_n):
    offs = []
    t = 0
    for (o0, w) in slabs:
        offs.append(t)
        t += kb_n * w
    return offs, t


class Plan:
    def __init__(self):
        self.ops = {e: [] for e in ENGINES}
        self.cnt = {e: 0 for e in ENGINES}
        self.waited = {e: {} for e in ENGINES}
        self.dma_fills = {}
        self.sem_names = (["m_" + e for e in ENGINES]
                          + ["b_" + e for e in ENGINES])
        self.bcnt = {e: 0 for e in ENGINES}
        self.stopped = False

    def new_dma_sem(self, name):
        assert name not in self.dma_fills
        self.dma_fills[name] = 0
        self.sem_names.append(name)
        return name

    def _wait(self, eng, sem, val):
        if self.waited[eng].get(sem, 0) >= val:
            return
        self.waited[eng][sem] = val
        self.ops[eng].append(("wait", sem, val))

    def op(self, eng, fn, deps=(), inc=False):
        if self.stopped:
            return None
        for d in deps:
            if d is not None:
                self._wait(eng, d[0], d[1])
        tag = None
        if inc:
            self.cnt[eng] += 1
            tag = ("m_" + eng, self.cnt[eng])
        self.ops[eng].append(("op", fn, tag))
        return tag

    def dma(self, queue, fn, sem, deps=()):
        if self.stopped:
            return None
        for d in deps:
            if d is not None:
                self._wait(queue, d[0], d[1])
        self.dma_fills[sem] += 16
        tag = (sem, self.dma_fills[sem])
        self.ops[queue].append(("dma", fn, tag))
        return tag

    def barrier(self):
        if self.stopped:
            return
        marks = []
        for e in ENGINES:
            if e == SY:
                continue
            self.bcnt[e] += 1
            tag = ("b_" + e, self.bcnt[e])
            self.ops[e].append(("op", (lambda eng: eng.nop()), tag))
            marks.append(tag)
            if self.cnt[e] > 0:
                marks.append(("m_" + e, self.cnt[e]))
        for s, vv in self.dma_fills.items():
            if vv > 0:
                marks.append((s, vv))
        for e in ENGINES:
            for m in marks:
                self._wait(e, m[0], m[1])


def build_nc(stop_after=None):
    from contextlib import ExitStack
    nc = bass.Bass("TRN2", target_bir_lowering=False, debug=False)

    def din(name, shape, dt=F32):
        return nc.dram_tensor(name, shape, dt, kind="ExternalInput").ap()

    xt = din("xt", [128, KT * TH])
    _o1, _t1 = _slab_offsets(SLABS_W1, KT)
    _o2, _t2 = _slab_offsets(SLABS_W2, 32)
    _o3, _t3 = _slab_offsets(SLABS_W3, KT)
    _o4, _t4 = _slab_offsets(SLABS_W4, INTER // 128)
    w1t = din("w1t", [128, _t1])
    w2t = din("w2t", [128, _t2])
    w3t = din("w3t", [128, _t3])
    w4t = din("w4t", [128, _t4])
    W_OFFS = {"w1t": _o1, "w2t": _o2, "w3t": _o3, "w4t": _o4}
    lnw_c = din("lnw", [128, KT])
    nw_c = din("nw", [128, I // 128])
    dexp_c = din("dexp", [128, I // 128])
    cw_c = din("cw", [128, 33 * 4])
    cb_c = din("cb", [128, 33])
    dtb_c = din("dtb", [NH, 1])
    alog_c = din("alog", [NH, 1])
    tri_c = din("tri", [128, 2 * CH], BF16)
    identf_c = din("identf", [128, 128])
    identb_c = din("identb", [128, 128], BF16)
    rmask_c = din("rmask", [1, NCORES])

    out_hbm = nc.dram_tensor("out", [H, T], F32, kind="ExternalOutput").ap()

    def dint(name, shape, dt):
        return nc.dram_tensor(name, shape, dt, kind="Internal").ap()

    xbc_d = dint("xbc_d", [CONV, TH], BF16)
    sg_d = dint("sg_d", [I, T], BF16)
    xs_d = dint("xs_d", [128, NCHK * 33 * CH], BF16)
    r_d = dint("r_d", [H, T], F32)
    acum_d = dint("acum_dd", [NH, T], F32)
    st_d = dint("st_d", [NCHK * NH, I], BF16)
    ccin_d = dint("ccin_d", [NH, I + 1], F32)
    ccout_d = dint("ccout_d", [NCORES * NH, I + 1], F32)
    y_d = dint("y_d", [I, T], BF16)
    x1_d = dint("x1_d", [H, T], F32)
    h_d = dint("h_d", [128, 2 * 64 * 512], BF16)

    es0 = ExitStack()

    def sb(stack, name, shape, dt):
        return stack.enter_context(nc.sbuf_tensor(name, shape, dt))

    onesf = sb(es0, "onesf", [1, 128], F32)
    onescf = sb(es0, "onescf", [128, 1], F32)
    onesbc = sb(es0, "onesbc", [128, 1], BF16)
    lnw = sb(es0, "lnw_sb", [128, KT], F32)
    nw = sb(es0, "nw_sb", [128, I // 128], F32)
    dexp = sb(es0, "dexp_sb", [128, I // 128], F32)
    cw = sb(es0, "cw_sb", [128, 33 * 4], F32)
    cb = sb(es0, "cb_sb", [128, 33], F32)
    dtb = sb(es0, "dtb_sb", [NH, 1], F32)
    alog = sb(es0, "alog_sb", [NH, 1], F32)
    tri = sb(es0, "tri_sb", [128, 2 * CH], BF16)
    idf = sb(es0, "idf_sb", [128, 128], F32)
    idb = sb(es0, "idb_sb", [128, 128], BF16)
    rmask = sb(es0, "rmask_sb", [1, NCORES], F32)
    SLABW = 8192
    slab = [sb(es0, f"slab{i}", [128, SLABW], BF16) for i in range(2)]
    stage = [sb(es0, f"stage{i}", [128, 512], BF16) for i in range(4)]
    stagef = [sb(es0, f"stagef{i}", [128, 512], F32) for i in range(2)]
    dt_sb = sb(es0, "dt_sb", [NH, T], F32)
    rowtmp = sb(es0, "rowtmp", [1, 128], F32)

    ps_mm = [es0.enter_context(nc.psum_tensor(f"ps_mm{i}", [128, 512], F32))
             for i in range(4)]
    ps_e = es0.enter_context(nc.psum_tensor("ps_e", [128, 512], F32))
    ps_tpb = es0.enter_context(nc.psum_tensor("ps_tpb", [128, 1024], BF16))
    ps_tpf = es0.enter_context(nc.psum_tensor("ps_tpf", [128, 512], F32))
    ps_row = es0.enter_context(nc.psum_tensor("ps_row", [1, 512], F32))

    plan = Plan()
    _STOP = {'n': 0}

    def stage_end():
        _STOP['n'] += 1
        if stop_after is not None and _STOP['n'] >= stop_after:
            plan.stopped = True
    nds = plan.new_dma_sem
    d_const = nds("d_const"); d_xf = nds("d_xf")
    d_slab = [nds("d_slab0"), nds("d_slab1")]
    s_stg = [nds(f"s_stg{i}") for i in range(4)]
    s_sf = [nds(f"s_sf{i}") for i in range(2)]
    s_t2f = [nds(f"s_t2f{i}") for i in range(2)]
    s_acum = nds("s_acum"); s_st = nds("s_st")
    s_ccin = nds("s_ccin"); d_cc = nds("d_cc")
    d_xbct = [nds("d_xbct0"), nds("d_xbct1")]
    d_xsc = nds("d_xsc")
    d_row = [nds("d_row0"), nds("d_row1")]
    d_lq = nds("d_lq"); d_dt = nds("d_dt"); d_stl = nds("d_stl")
    s_y = [nds("s_y0"), nds("s_y1")]
    d_yt = [nds("d_yt0"), nds("d_yt1")]
    d_sgt = [nds("d_sgt0"), nds("d_sgt1")]
    d_x1t = [nds("d_x1t0"), nds("d_x1t1")]
    d_rt = [nds("d_rt0"), nds("d_rt1")]
    d_hblk = nds("d_hblk")

    def allw(sems_):
        return [(s, plan.dma_fills[s]) for s in sems_ if plan.dma_fills[s] > 0]

    MMB = {"i": 0}

    def next_bank():
        b = ps_mm[MMB["i"] % 4]
        MMB["i"] += 1
        return b

    bank_war = {}

    for dst, src in [(lnw, lnw_c), (nw, nw_c), (dexp, dexp_c), (cw, cw_c),
                     (cb, cb_c), (dtb, dtb_c), (alog, alog_c), (tri, tri_c),
                     (idf, identf_c), (idb, identb_c), (rmask, rmask_c)]:
        plan.dma(SY, (lambda e, d=dst, s=src: e.dma_start(out=d[:, :], in_=s[:, :])),
                 d_const)
    plan.op(V, lambda e: e.memset(onesf[:, :], 1.0), inc=True)
    plan.op(V, lambda e: e.memset(onescf[:, :], 1.0), inc=True)
    m_ones = plan.op(V, lambda e: e.memset(onesbc[:, :], 1.0), inc=True)

    es1 = ExitStack()
    xfull = sb(es1, "xfull", [128, KT * TH], F32)
    xsq = [sb(es1, f"xsq{i}", [128, TH], F32) for i in range(2)]
    srow = sb(es1, "srow", [1, TH], F32)
    sqrow = sb(es1, "sqrow", [1, TH], F32)
    mrow = sb(es1, "mrow", [1, TH], F32)
    varrow = sb(es1, "varrow", [1, TH], F32)
    arow = sb(es1, "arow", [1, TH], F32)
    aorow = sb(es1, "aorow", [1, TH], F32)
    mb = sb(es1, "mb", [128, TH], F32)
    ab = sb(es1, "ab", [128, TH], F32)
    aob = sb(es1, "aob", [128, TH], F32)
    t1f = sb(es1, "t1f", [128, TH], F32)
    t2f = [sb(es1, f"t2f{i}", [128, TH], F32) for i in range(2)]
    xb = sb(es1, "xb", [128, KT * TH], BF16)

    t_x = plan.dma(SY, lambda e: e.dma_start(out=xfull[:, :], in_=xt[:, :]), d_xf)

    TG = [(0, 512), (512, 512), (1024, 3)]
    sum_dst = [ps_row, ps_mm[0], ps_mm[1]]
    sq_dst = [ps_e, ps_mm[2], ps_mm[3]]
    sqmm_tag = {}
    last_mm = None
    for kt in range(KT):
        b = kt % 2
        sq_t = plan.op(
            SC, (lambda e, b=b, kt=kt: e.activation(
                xsq[b][:, :], xfull[:, kt * TH:(kt + 1) * TH], AF.Square)),
            deps=[t_x, sqmm_tag.get(kt - 2)], inc=True)
        for gi, (t0, tw) in enumerate(TG):
            plan.op(TE, (lambda e, kt=kt, gi=gi, t0=t0, tw=tw: e.matmul(
                sum_dst[gi][0:1, 0:tw], onescf[:, 0:1],
                xfull[:, kt * TH + t0: kt * TH + t0 + tw],
                start=(kt == 0), stop=(kt == KT - 1), skip_group_check=True)),
                deps=[t_x, m_ones])
            last_mm = plan.op(TE, (lambda e, b=b, gi=gi, t0=t0, tw=tw, kt=kt:
                e.matmul(
                    sq_dst[gi][0:1, 0:tw], onescf[:, 0:1], xsq[b][:, t0:t0 + tw],
                    start=(kt == 0), stop=(kt == KT - 1), skip_group_check=True)),
                deps=[sq_t, m_ones], inc=True)
        sqmm_tag[kt] = last_mm
    ev = []
    for gi, (t0, tw) in enumerate(TG):
        ev.append(plan.op(V, (lambda e, gi=gi, t0=t0, tw=tw: e.tensor_copy(
            srow[0:1, t0:t0 + tw], sum_dst[gi][0:1, 0:tw])), deps=[last_mm], inc=True))
        ev.append(plan.op(V, (lambda e, gi=gi, t0=t0, tw=tw: e.tensor_copy(
            sqrow[0:1, t0:t0 + tw], sq_dst[gi][0:1, 0:tw])), deps=[last_mm], inc=True))
    m1 = plan.op(V, lambda e: e.tensor_scalar(mrow[:, :], srow[:, :], 1.0 / H, None,
                                              ALU.mult), deps=ev, inc=True)
    m2t = plan.op(V, lambda e: e.tensor_mul(varrow[:, :], mrow[:, :], mrow[:, :]),
                  deps=[m1], inc=True)
    m3 = plan.op(V, lambda e: e.scalar_tensor_tensor(
        varrow[:, :], sqrow[:, :], 1.0 / H, varrow[:, :], ALU.mult, ALU.subtract),
        deps=[m2t], inc=True)
    a0 = plan.op(V, lambda e: e.tensor_scalar(arow[:, :], varrow[:, :], 1.0 + EPS,
                                              EPS * EPS, ALU.mult, ALU.add),
                 deps=[m3], inc=True)
    a1 = plan.op(SC, lambda e: e.activation(arow[:, :], arow[:, :], AF.Ln),
                 deps=[a0], inc=True)
    a1b = plan.op(V, lambda e: e.tensor_scalar(arow[:, :], arow[:, :], -0.5, None,
                                               ALU.mult), deps=[a1], inc=True)
    a2 = plan.op(SC, lambda e: e.activation(arow[:, :], arow[:, :], AF.Exp),
                 deps=[a1b], inc=True)
    a2b = plan.op(V, lambda e: e.tensor_scalar(aorow[:, :], varrow[:, :], 1.0, EPS,
                                               ALU.mult, ALU.add), deps=[m3], inc=True)
    a3 = plan.op(SC, lambda e: e.activation(aorow[:, :], aorow[:, :], AF.Ln),
                 deps=[a2b], inc=True)
    a3b = plan.op(V, lambda e: e.tensor_scalar(aorow[:, :], aorow[:, :], -0.5, None,
                                               ALU.mult), deps=[a3], inc=True)
    a4 = plan.op(SC, lambda e: e.activation(aorow[:, :], aorow[:, :], AF.Exp),
                 deps=[a3b], inc=True)

    def bcast_row(src_row, dst_tile, dep):
        tags = []
        for (t0, tw) in TG:
            bk = next_bank()
            mt = plan.op(TE, (lambda e, bk=bk, t0=t0, tw=tw, s=src_row: e.matmul(
                bk[:, 0:tw], onesf[0:1, :], s[0:1, t0:t0 + tw],
                start=True, stop=True, skip_group_check=True)),
                deps=[dep, m_ones, bank_war.get(bk.name)], inc=True)
            et = plan.op(V, (lambda e, bk=bk, t0=t0, tw=tw, d=dst_tile: e.tensor_copy(
                d[:, t0:t0 + tw], bk[:, 0:tw])), deps=[mt], inc=True)
            bank_war[bk.name] = et
            tags.append(et)
        return tags

    mb_t = bcast_row(mrow, mb, m1)
    ab_t = bcast_row(arow, ab, a2)
    aob_t = bcast_row(aorow, aob, a4)

    r_tags = []
    prev_dma = {}
    for kt in range(KT):
        b = kt % 2
        tt1 = plan.op(V, (lambda e, kt=kt: e.tensor_sub(
            t1f[:, :], xfull[:, kt * TH:(kt + 1) * TH], mb[:, :])),
            deps=[t_x] + mb_t, inc=True)
        plan.op(V, (lambda e, kt=kt: e.scalar_tensor_tensor(
            xb[:, kt * TH:(kt + 1) * TH], ab[:, :], lnw[:, kt:kt + 1], t1f[:, :],
            ALU.mult, ALU.mult)), deps=[tt1] + ab_t + [("d_const", 16 * 11)], inc=True)
        tt2 = plan.op(V, (lambda e, kt=kt, b=b: e.tensor_mul(
            t2f[b][:, :], t1f[:, :], aob[:, :])),
            deps=[tt1, prev_dma.get(b)] + aob_t, inc=True)
        tt3 = plan.op(V, (lambda e, kt=kt, b=b: e.tensor_add(
            t2f[b][:, :], t2f[b][:, :], xfull[:, kt * TH:(kt + 1) * TH])),
            deps=[tt2], inc=True)
        prev_dma[b] = plan.dma(SY, (lambda e, kt=kt, b=b: e.dma_start(
            out=r_d[kt * 128:(kt + 1) * 128, :], in_=t2f[b][:, 3:TH])),
            s_t2f[b], deps=[tt3])
        r_tags.append(prev_dma[b])

    plan.barrier()
    stage_end()

    def mm_stage(w_hbm, kb_n, slabs, rhs_fn, evac_fn, extra_dep=None,
                 slab_offs=None):
        slab_last_use = [None, None]
        slab_tag = [None, None]
        for si, (o0, wdt, meta) in enumerate(slabs):
            sbuf_i = si % 2
            soff = slab_offs[si]
            nsub = wdt // 128 if wdt >= 128 else 1
            slab_tag[sbuf_i] = plan.dma(GP, (lambda e, soff=soff, wdt=wdt, i=sbuf_i:
                e.dma_start(
                    out=slab[i][:, 0:kb_n * wdt],
                    in_=w_hbm[:, soff:soff + kb_n * wdt])),
                d_slab[sbuf_i], deps=[slab_last_use[sbuf_i], extra_dep])
            tgrps = meta["tgrps"]
            for osub in range(nsub):
                ow = min(128, wdt)
                for (t0, tw) in tgrps:
                    bk = next_bank()
                    mt = None
                    for kb in range(kb_n):
                        mt = plan.op(TE, (lambda e, bk=bk, kb=kb, osub=osub, ow=ow,
                                          wdt=wdt, t0=t0, tw=tw, i=sbuf_i: e.matmul(
                            bk[0:ow, 0:tw],
                            slab[i][:, kb * wdt + osub * 128: kb * wdt + osub * 128 + ow],
                            rhs_fn(kb, t0, tw),
                            start=(kb == 0), stop=(kb == kb_n - 1),
                            skip_group_check=True)),
                            deps=([slab_tag[sbuf_i], bank_war.get(bk.name)]
                                  if kb == 0 else []),
                            inc=(kb == kb_n - 1))
                    slab_last_use[sbuf_i] = mt
                    war = evac_fn(meta, o0 + osub * 128, t0, tw, bk, mt)
                    bank_war[bk.name] = war
        return mt

    stg_war = [None] * 4
    STG = {"i": 0}

    def next_stage():
        s = STG["i"] % 4
        STG["i"] += 1
        return s

    def rhs_xb(kb, t0, tw):
        return xb[:, kb * TH + t0: kb * TH + t0 + tw]

    def evac_inproj(meta, o_abs, t0, tw, bk, mt):
        kind = meta["kind"]
        if kind == "dt":
            return plan.op(V, (lambda e, t0=t0, tw=tw, bk=bk: e.tensor_copy(
                dt_sb[:, t0 - 3: t0 - 3 + tw], bk[0:NH, 0:tw])), deps=[mt], inc=True)
        s = next_stage()
        if kind == "gate":
            et = plan.op(SC, (lambda e, s=s, bk=bk, tw=tw: e.activation(
                stage[s][:, 0:tw], bk[:, 0:tw], AF.Silu)),
                deps=[mt, stg_war[s]], inc=True)
            dmat = plan.dma(SY, (lambda e, s=s, o=o_abs, t0=t0, tw=tw: e.dma_start(
                out=sg_d[o:o + 128, t0 - 3: t0 - 3 + tw], in_=stage[s][:, 0:tw])),
                s_stg[s], deps=[et])
        else:
            et = plan.op(V, (lambda e, s=s, bk=bk, tw=tw: e.tensor_copy(
                stage[s][:, 0:tw], bk[:, 0:tw])), deps=[mt, stg_war[s]], inc=True)
            dmat = plan.dma(SY, (lambda e, s=s, o=o_abs, t0=t0, tw=tw: e.dma_start(
                out=xbc_d[o - I:o - I + 128, t0: t0 + tw], in_=stage[s][:, 0:tw])),
                s_stg[s], deps=[et])
        stg_war[s] = dmat
        return et

    TG_MAIN = [(3, 512), (515, 512)]
    TG_FULL = [(0, 512), (512, 512), (1024, 3)]
    slabs1 = ([(o, 512, {"kind": "gate", "tgrps": TG_MAIN}) for o in range(0, I, 512)]
              + [(o, 512, {"kind": "xbc", "tgrps": TG_FULL}) for o in range(I, 8192, 512)]
              + [(8192, 128, {"kind": "xbc", "tgrps": TG_FULL})]
              + [(8320, 64, {"kind": "dt", "tgrps": TG_MAIN})])
    mm_stage(w1t, KT, slabs1, rhs_xb, evac_inproj,
             slab_offs=W_OFFS['w1t'])

    plan.barrier()
    stage_end()
    es1.close()

    es2 = ExitStack()
    dts = sb(es2, "dts", [NH, T], F32)
    lndt = sb(es2, "lndt", [NH, T], F32)
    dA = sb(es2, "dA", [NH, T], F32)
    acum = sb(es2, "acum_sb", [NH, T], F32)
    alf = sb(es2, "alf", [NH, T], F32)
    tmpa = sb(es2, "tmpa", [NH, T], F32)
    zero64 = sb(es2, "zero64", [NH, CH], F32)
    expa = sb(es2, "expa", [NH, 1], F32)
    cd = sb(es2, "cd", [NH, NCHK], F32)
    dtot = sb(es2, "dtot", [NH, 1], F32)
    decb = sb(es2, "decb", [NH, NCHK * NH], F32)
    alcol = sb(es2, "alcol", [128, 8 * NH], F32)
    lndtT = sb(es2, "lndtT", [128, 8 * NH], F32)
    wcol = sb(es2, "wcol", [128, 8 * NH], F32)
    bfm = sb(es2, "bfm", [NS, T], BF16)
    cfm = sb(es2, "cfm", [NS, T], BF16)
    smask = sb(es2, "smask", [128, NCHK * 2 * CH], BF16)

    zt = plan.op(V, lambda e: e.memset(zero64[:, :], 0.0), inc=True)
    u1 = plan.op(V, lambda e: e.tensor_scalar(tmpa[:, :], dt_sb[:, :], dtb[:, 0:1],
                                              None, ALU.add), inc=True)
    u2 = plan.op(V, lambda e: e.scalar_tensor_tensor(lndt[:, :], tmpa[:, :], -1.0,
                                                     tmpa[:, :], ALU.mult, ALU.min),
                 deps=[u1], inc=True)
    u3 = plan.op(SC, lambda e: e.activation(lndt[:, :], lndt[:, :], AF.Exp),
                 deps=[u2], inc=True)
    u4 = plan.op(SC, lambda e: e.activation(lndt[:, :], lndt[:, :], AF.Ln, bias=1.0),
                 deps=[u3], inc=True)
    u5 = plan.op(V, lambda e: e.tensor_scalar(dA[:, :], tmpa[:, :], 0.0, None, ALU.max),
                 deps=[u2], inc=True)
    u6 = plan.op(V, lambda e: e.tensor_add(dts[:, :], lndt[:, :], dA[:, :]),
                 deps=[u4, u5], inc=True)
    u7 = plan.op(SC, lambda e: e.activation(lndt[:, :], dts[:, :], AF.Ln),
                 deps=[u6], inc=True)
    u8 = plan.op(SC, lambda e: e.activation(expa[:, :], alog[:, 0:1], AF.Exp),
                 deps=[("d_const", 16 * 11)], inc=True)
    u9 = plan.op(V, lambda e: e.tensor_scalar(dA[:, :], dts[:, :], expa[:, 0:1], -1.0,
                                              ALU.mult, ALU.mult), deps=[u6, u8],
                 inc=True)
    sc_t = []
    for c in range(NCHK):
        sc_t.append(plan.op(V, (lambda e, c=c: e.tensor_tensor_scan(
            acum[:, c * CH:(c + 1) * CH], dA[:, c * CH:(c + 1) * CH], zero64[:, :],
            0.0, ALU.add, ALU.add)), deps=[u9, zt], inc=True))
    al_t = plan.op(V, lambda e: e.tensor_sub(alf[:, :], acum[:, :], lndt[:, :]),
                   deps=sc_t + [u7], inc=True)
    acum_dma = plan.dma(SY, lambda e: e.dma_start(out=acum_d[:, :], in_=acum[:, :]),
                        s_acum, deps=sc_t)
    cd_t = plan.op(SC, lambda e: e.activation(
        cd[:, :], acum.ap()[:, CH - 1::CH], AF.Exp), deps=sc_t, inc=True)
    d1 = plan.op(V, lambda e: e.tensor_mul(dtot[:, :], cd[:, 0:1], cd[:, 1:2]),
                 deps=[cd_t], inc=True)
    d2 = plan.op(V, lambda e: e.tensor_mul(dtot[:, :], dtot[:, :], cd[:, 2:3]),
                 deps=[d1], inc=True)
    dtot_t = plan.op(V, lambda e: e.tensor_mul(dtot[:, :], dtot[:, :], cd[:, 3:4]),
                     deps=[d2], inc=True)

    tp_war = [None]

    def transpose_f32(src_ap, np_, dst_ap, dep):
        mt = plan.op(TE, (lambda e: e.transpose(
            ps_tpf[:, 0:np_], src_ap, idf[0:np_, 0:np_])),
            deps=[dep, tp_war[0], ("d_const", 16 * 11)], inc=True)
        et = plan.op(V, (lambda e: e.tensor_copy(
            dst_ap, ps_tpf[:, 0:np_])), deps=[mt], inc=True)
        tp_war[0] = et
        return et

    for tt in range(8):
        transpose_f32(alf[:, tt * 128:(tt + 1) * 128], NH,
                      alcol[:, tt * NH:(tt + 1) * NH], al_t)
        transpose_f32(lndt[:, tt * 128:(tt + 1) * 128], NH,
                      lndtT[:, tt * NH:(tt + 1) * NH], u7)
    for c in range(NCHK):
        w1 = plan.op(V, (lambda e, c=c: e.tensor_scalar(
            tmpa[:, c * CH:(c + 1) * CH], acum[:, c * CH:(c + 1) * CH],
            acum[:, (c + 1) * CH - 1:(c + 1) * CH], -1.0, ALU.subtract, ALU.mult)),
            deps=sc_t, inc=True)
        w2 = plan.op(V, (lambda e, c=c: e.tensor_add(
            tmpa[:, c * CH:(c + 1) * CH], tmpa[:, c * CH:(c + 1) * CH],
            lndt[:, c * CH:(c + 1) * CH])), deps=[w1, u7], inc=True)
        w3 = plan.op(SC, (lambda e, c=c: e.activation(
            tmpa[:, c * CH:(c + 1) * CH], tmpa[:, c * CH:(c + 1) * CH], AF.Exp)),
            deps=[w2], inc=True)
        for s in range(2):
            tt = 2 * c + s
            transpose_f32(tmpa[:, tt * 128:(tt + 1) * 128], NH,
                          wcol[:, tt * NH:(tt + 1) * NH], w3)
    for c in range(NCHK):
        mt = plan.op(TE, (lambda e, c=c: e.transpose(
            ps_tpf[0:1, 0:NH], cd[:, c:c + 1], idf[0:NH, 0:NH])),
            deps=[cd_t, tp_war[0]], inc=True)
        et = plan.op(V, (lambda e: e.tensor_copy(
            rowtmp[0:1, 0:NH], ps_tpf[0:1, 0:NH])),
            deps=[mt], inc=True)
        tp_war[0] = et
        bk = next_bank()
        m2 = plan.op(TE, (lambda e, bk=bk: e.matmul(
            bk[0:NH, 0:NH], onesf[0:1, 0:NH], rowtmp[0:1, 0:NH],
            start=True, stop=True, skip_group_check=True)),
            deps=[et, m_ones, bank_war.get(bk.name)], inc=True)
        e2 = plan.op(V, (lambda e, c=c, bk=bk: e.tensor_copy(
            decb[:, c * NH:(c + 1) * NH], bk[0:NH, 0:NH])), deps=[m2], inc=True)
        bank_war[bk.name] = e2

    plan.barrier()
    stage_end()

    es3 = ExitStack()
    xbct = [sb(es3, f"xbct{i}", [128, TH], BF16) for i in range(2)]
    cacc = [sb(es3, f"cacc{i}", [128, T], F32) for i in range(2)]
    gtmp = sb(es3, "gtmp", [128, T], F32)
    conv_last = [None, None]
    acc_war = [None, None]
    for it in range(33):
        b = it % 2
        ld = plan.dma(SY, (lambda e, it=it, b=b: e.dma_start(
            out=xbct[b][:, :], in_=xbc_d[it * 128:(it + 1) * 128, :])),
            d_xbct[b], deps=[conv_last[b]] + allw(s_stg))
        a = b
        eng = V if it % 2 == 0 else GP
        t = plan.op(eng, (lambda e, b=b, a=a, it=it: e.tensor_scalar(
            cacc[a][:, :], xbct[b][:, 0:T], cw[:, it * 4:it * 4 + 1], None,
            ALU.mult)), deps=[ld, acc_war[a], ("d_const", 16 * 11)], inc=True)
        for k in range(1, 4):
            if eng == V:
                t = plan.op(V, (lambda e, b=b, a=a, it=it, k=k: e.scalar_tensor_tensor(
                    cacc[a][:, :], xbct[b][:, k:T + k], cw[:, it * 4 + k:it * 4 + k + 1],
                    cacc[a][:, :], ALU.mult, ALU.add)), deps=[t], inc=True)
            else:
                t0_ = plan.op(GP, (lambda e, b=b, it=it, k=k: e.tensor_scalar(
                    gtmp[:, :], xbct[b][:, k:T + k], cw[:, it * 4 + k:it * 4 + k + 1],
                    None, ALU.mult)), deps=[t], inc=True)
                t = plan.op(GP, (lambda e, a=a: e.tensor_add(
                    cacc[a][:, :], cacc[a][:, :], gtmp[:, :])), deps=[t0_], inc=True)
        conv_last[b] = t
        sil = []
        for s2 in range(2):
            ss = next_stage()
            et = plan.op(SC, (lambda e, a=a, s2=s2, it=it, ss=ss: e.activation(
                stage[ss][:, :], cacc[a][:, s2 * 512:(s2 + 1) * 512], AF.Silu,
                bias=cb[:, it:it + 1])), deps=[t, stg_war[ss]], inc=True)
            W33 = 33 * CH
            dmat = None
            for half in range(2):
                cc_ = s2 * 2 + half
                dmat = plan.dma(SY, (lambda e, it=it, ss=ss, cc_=cc_, half=half,
                                     W33=W33: e.dma_start(
                    out=xs_d[:, cc_ * W33 + it * CH: cc_ * W33 + (it + 1) * CH],
                    in_=stage[ss][:, half * CH:(half + 1) * CH])),
                    s_stg[ss], deps=[et])
            stg_war[ss] = dmat
            sil.append(et)
            if it == 32:
                b1 = plan.op(V, (lambda e, s2=s2, ss=ss: e.tensor_copy(
                    bfm[:, s2 * 512:(s2 + 1) * 512], stage[ss][0:NS, :])),
                    deps=[et], inc=True)
                c1 = plan.op(V, (lambda e, s2=s2, ss=ss: e.tensor_copy(
                    cfm[:, s2 * 512:(s2 + 1) * 512], stage[ss][NS:128, :])),
                    deps=[et], inc=True)
                stg_war[ss] = c1
        acc_war[a] = sil[-1]

    plan.barrier()
    stage_end()
    es3.close()

    es4 = ExitStack()
    xs_c = sb(es4, "xs_c", [128, 32 * CH], BF16)
    xsT = sb(es4, "xsT", [128, 2 * I], BF16)
    btc = sb(es4, "btc", [128, 2 * NS], BF16)
    bd = [sb(es4, f"bd{i}", [128, 2 * NS], BF16) for i in range(2)]
    es4s = ExitStack()
    ststage = sb(es4s, "ststage", [NH, I], F32)
    carry = [sb(es4s, "carry0", [NH, I], F32)]

    cz = plan.op(V, lambda e: e.memset(carry[0][:, :], 0.0), inc=True)
    bd_war = [None, None]
    tpb_war = [None]

    def transpose_b(src_ap, np_, dst_ap, dep):
        mt = plan.op(TE, (lambda e: e.transpose(
            ps_tpb[:, 0:np_], src_ap, idb[0:np_, 0:np_])),
            deps=[dep, tpb_war[0]], inc=True)
        et = plan.op(V, (lambda e: e.tensor_copy(
            dst_ap, ps_tpb[:, 0:np_])), deps=[mt], inc=True)
        tpb_war[0] = et
        return et

    def load_xs_chunk(c, war, dsem):
        W33 = 33 * CH
        return plan.dma(SY, (lambda e, c=c, W33=W33: e.dma_start(
            out=xs_c[:, :],
            in_=xs_d[:, c * W33: c * W33 + 32 * CH])),
            dsem, deps=[war] + allw(s_stg))

    def build_xsT(dep_load):
        tags = []
        for it in range(32):
            for s in range(2):
                tags.append(transpose_b(
                    xs_c[:, it * CH + s * 128: it * CH + s * 128 + 128], 128,
                    xsT[:, s * I + it * 128: s * I + it * 128 + 128], dep_load))
        return tags

    xs_war = None
    scan_done = []
    stev_prev = None
    st_dma_last = None
    for c in range(NCHK):
        xl = load_xs_chunk(c, xs_war, d_xsc)
        xst_tags = build_xsT(xl)
        xs_war = xst_tags[-1]
        for s in range(2):
            bk = next_bank()
            mt = plan.op(TE, (lambda e, c=c, s=s, bk=bk: e.matmul(
                bk[:, 0:CH], bfm[:, c * CH + s * 128: c * CH + s * 128 + 128],
                cfm[:, c * CH:(c + 1) * CH], start=True, stop=True,
                skip_group_check=True)),
                deps=[bank_war.get(bk.name)], inc=True)
            et = plan.op(V, (lambda e, c=c, s=s, bk=bk: e.tensor_tensor(
                smask[:, (2 * c + s) * CH:(2 * c + s + 1) * CH], bk[:, 0:CH],
                tri[:, s * CH:(s + 1) * CH], ALU.mult)), deps=[mt], inc=True)
            bank_war[bk.name] = et
        bt_tags = []
        for s in range(2):
            bt_tags.append(transpose_b(
                bfm[:, c * CH + s * 128: c * CH + s * 128 + 128], NS,
                btc[:, s * NS:(s + 1) * NS], dep=None))
        for hg in range(8):
            mm_last = None
            for hh in range(8):
                h = hg * 8 + hh
                bdi = hh % 2
                bdt = []
                for s in range(2):
                    tt = 2 * c + s
                    bdt.append(plan.op(V, (lambda e, h=h, s=s, tt=tt, bdi=bdi:
                        e.tensor_scalar(
                            bd[bdi][:, s * NS:(s + 1) * NS],
                            btc[:, s * NS:(s + 1) * NS],
                            wcol[:, tt * NH + h: tt * NH + h + 1], None, ALU.mult)),
                        deps=[bt_tags[s], bd_war[bdi]], inc=True))
                for s in range(2):
                    mm_last = plan.op(TE, (lambda e, h=h, hh=hh, s=s, bdi=bdi:
                        e.matmul(
                            ps_e[0:NS, hh * PD:(hh + 1) * PD],
                            bd[bdi][:, s * NS:(s + 1) * NS],
                            xsT[:, s * I + h * PD: s * I + (h + 1) * PD],
                            start=(hh == 0 and s == 0), stop=(hh == 7 and s == 1),
                            skip_group_check=True)),
                        deps=[bdt[s]] + ([xs_war, stev_prev] if hh == 0 and s == 0
                                         else []),
                        inc=True)
                bd_war[bdi] = mm_last
            stev_prev = plan.op(V, (lambda e, hg=hg: e.tensor_copy(
                ststage[:, hg * 512:(hg + 1) * 512], ps_e[0:NH, :])),
                deps=[mm_last, st_dma_last], inc=True)
        st_dma = plan.dma(GP, (lambda e, c=c: e.dma_start(
            out=st_d[c * NH:(c + 1) * NH, :], in_=ststage[:, :])),
            s_st, deps=[stev_prev])
        st_dma_last = st_dma
        src = carry[0]
        dst = carry[0]
        last = None
        for h in range(NH):
            last = plan.op(V, (lambda e, c=c, h=h, src=src, dst=dst:
                e.scalar_tensor_tensor(
                    dst[:, h * PD:(h + 1) * PD], src[:, h * PD:(h + 1) * PD],
                    decb[:, c * NH + h: c * NH + h + 1],
                    ststage[:, h * PD:(h + 1) * PD], ALU.mult, ALU.add)),
                deps=[stev_prev, st_dma] + ([cz] if c == 0 else [scan_done[-1]]),
                inc=True)
        scan_done.append(last)

    cc1 = plan.dma(SY, lambda e: e.dma_start(out=ccin_d[:, 0:I], in_=carry[0][:, :]),
                   s_ccin, deps=[scan_done[-1]])
    cc2 = plan.dma(SY, lambda e: e.dma_start(out=ccin_d[:, I:I + 1], in_=dtot[:, :],
                                             allow_slow_non_contiguous=True),
                   s_ccin, deps=[dtot_t])
    es4s.close()

    plan.barrier()
    stage_end()

    def do_cc(e):
        e.collective_compute(
            "AllGather", ALU.bypass, replica_groups=[list(range(NCORES))],
            ins=[ccin_d.opt()], outs=[ccout_d.opt()]).then_inc(_SEMS["d_cc"])
        return None
    plan.op(GP, do_cc, deps=[(s_ccin, plan.dma_fills[s_ccin])])
    CC_TAG = ("d_cc", 1)

    es4b = ExitStack()
    lq_st = sb(es4b, "lq_st", [NH, I], F32)
    dtall = sb(es4b, "dtall", [NH, NCORES], F32)
    dmall = sb(es4b, "dmall", [NH, NCORES], F32)
    wfin = sb(es4b, "wfin", [NH, NCORES], F32)
    rmb = sb(es4b, "rmb", [NH, NCORES], F32)
    wbq = sb(es4b, "wbq", [NH, NH], F32)
    prevb = sb(es4b, "prevb", [NH, I], F32)
    prevbb = sb(es4b, "prevbb", [NH, I], BF16)
    stldb = sb(es4b, "stldb", [NH, I], BF16)

    pz = plan.op(V, lambda e: e.memset(prevb[:, :], 0.0), inc=True)
    dt_tags = []
    for q in range(NCORES):
        dt_tags.append(plan.dma(SY, (lambda e, q=q: e.dma_start(
            out=dtall[:, q:q + 1], in_=ccout_d[q * NH:(q + 1) * NH, I:I + 1],
            allow_slow_non_contiguous=True)),
            d_dt, deps=[CC_TAG]))
    dt_all = dt_tags[-1]
    bk = next_bank()
    mt = plan.op(TE, (lambda e, bk=bk: e.matmul(
        bk[0:NH, 0:NCORES], onesf[0:1, 0:NH], rmask[0:1, :], start=True, stop=True,
        skip_group_check=True)), deps=[bank_war.get(bk.name), m_ones,
                                       ("d_const", 16 * 11)], inc=True)
    rmb_t = plan.op(V, (lambda e, bk=bk: e.tensor_copy(rmb[:, :], bk[0:NH, 0:NCORES])),
                    deps=[mt], inc=True)
    bank_war[bk.name] = rmb_t
    q1 = plan.op(V, lambda e: e.tensor_scalar(dmall[:, :], dtall[:, :], -1.0, None,
                                              ALU.add), deps=[dt_all], inc=True)
    q2 = plan.op(V, lambda e: e.tensor_mul(dmall[:, :], dmall[:, :], rmb[:, :]),
                 deps=[q1, rmb_t], inc=True)
    q3 = plan.op(V, lambda e: e.tensor_scalar(dmall[:, :], dmall[:, :], 1.0, None,
                                              ALU.add), deps=[q2], inc=True)
    wt = plan.op(V, lambda e: e.memset(wfin[:, NCORES - 1:NCORES], 1.0), inc=True)
    for q in range(NCORES - 2, -1, -1):
        wt = plan.op(V, (lambda e, q=q: e.tensor_mul(
            wfin[:, q:q + 1], wfin[:, q + 1:q + 2], dmall[:, q + 1:q + 2])),
            deps=[wt, q3], inc=True)
    wt = plan.op(V, lambda e: e.tensor_mul(wfin[:, :], wfin[:, :], rmb[:, :]),
                 deps=[wt], inc=True)
    lq_war = None
    for q in range(NCORES):
        mt = plan.op(TE, (lambda e, q=q: e.transpose(
            ps_tpf[0:1, 0:NH], wfin[:, q:q + 1], idf[0:NH, 0:NH])),
            deps=[wt, tp_war[0]], inc=True)
        et = plan.op(V, (lambda e: e.tensor_copy(
            rowtmp[0:1, 0:NH], ps_tpf[0:1, 0:NH])), deps=[mt], inc=True)
        tp_war[0] = et
        bk = next_bank()
        m2 = plan.op(TE, (lambda e, bk=bk: e.matmul(
            bk[0:NH, 0:NH], onesf[0:1, 0:NH], rowtmp[0:1, 0:NH], start=True,
            stop=True, skip_group_check=True)),
            deps=[et, m_ones, bank_war.get(bk.name)], inc=True)
        wb_t = plan.op(V, (lambda e, bk=bk: e.tensor_copy(wbq[:, :], bk[0:NH, 0:NH])),
                       deps=[m2], inc=True)
        bank_war[bk.name] = wb_t
        lq_t = plan.dma(SY, (lambda e, q=q: e.dma_start(
            out=lq_st[:, :], in_=ccout_d[q * NH:(q + 1) * NH, 0:I])), d_lq,
            deps=[CC_TAG, lq_war])
        last = None
        for h in range(NH):
            last = plan.op(V, (lambda e, h=h: e.scalar_tensor_tensor(
                prevb[:, h * PD:(h + 1) * PD], lq_st[:, h * PD:(h + 1) * PD],
                wbq[:, h:h + 1], prevb[:, h * PD:(h + 1) * PD], ALU.mult, ALU.add)),
                deps=[lq_t, wb_t, pz], inc=True)
        lq_war = last

    plan.barrier()
    stage_end()

    es4c = ExitStack()
    rowbuf = [sb(es4c, f"rowbuf{i}", [1, 8 * CH], F32) for i in range(2)]
    dc_sb = [sb(es4c, f"dc{i}", [128, CH], F32) for i in range(4)]
    lb_sb = [sb(es4c, f"lb{i}", [128, CH], BF16) for i in range(4)]
    pb_sb = [sb(es4c, f"pb{i}", [128, CH], BF16) for i in range(4)]
    csd = [sb(es4c, f"csd{i}", [NS, CH], BF16) for i in range(2)]
    ysta = [sb(es4c, f"ysta{i}", [128, CH], BF16) for i in range(2)]

    prev_ready = plan.op(V, lambda e: e.tensor_copy(prevbb[:, :], prevb[:, :]),
                         inc=True)
    xs_war = None
    row_war = [None, None]
    dcrot = {"i": 0}
    yrot = {"i": 0}
    ysta_war = [None, None]
    dc_war = [None] * 4
    pb_war = [None] * 4
    csd_war = [None, None]
    offmm_last = None
    for c in range(NCHK):
        if c > 0:
            stl = plan.dma(SY, (lambda e, c=c: e.dma_start(
                out=stldb[:, :], in_=st_d[(c - 1) * NH:c * NH, :])), d_stl,
                deps=[(s_st, plan.dma_fills[s_st]), prev_ready])
            last = None
            for h in range(NH):
                last = plan.op(V, (lambda e, c=c, h=h: e.scalar_tensor_tensor(
                    prevb[:, h * PD:(h + 1) * PD], prevb[:, h * PD:(h + 1) * PD],
                    decb[:, (c - 1) * NH + h:(c - 1) * NH + h + 1],
                    stldb[:, h * PD:(h + 1) * PD], ALU.mult, ALU.add)),
                    deps=[stl, prev_ready], inc=True)
            prev_ready = plan.op(V, lambda e: e.tensor_copy(prevbb[:, :],
                                                            prevb[:, :]),
                                 deps=[last, offmm_last], inc=True)
        xl = load_xs_chunk(c, xs_war, d_xsc)
        xst_tags = build_xsT(xl)
        xs_war = xst_tags[-1]
        for hg in range(8):
            rb = hg % 2
            row_t = plan.dma(SY, (lambda e, c=c, hg=hg, rb=rb: e.dma_start(
                out=rowbuf[rb][0:1, :].rearrange("o (h t) -> o h t", h=8),
                in_=acum_d[hg * 8:(hg + 1) * 8, c * CH:(c + 1) * CH])),
                d_row[rb], deps=[(s_acum, 16), row_war[rb]])
            for hh in range(8):
                h = hg * 8 + hh
                j = h // 2
                odd = h % 2
                d1b = ps_mm[0] if h % 2 == 0 else ps_mm[1]
                d1t = plan.op(TE, (lambda e, d1b=d1b, rb=rb, hh=hh: e.matmul(
                    d1b[:, 0:CH], onesf[0:1, :], rowbuf[rb][0:1, hh * CH:(hh + 1) * CH],
                    start=True, stop=True, skip_group_check=True)),
                    deps=[row_t, m_ones, bank_war.get(d1b.name)], inc=True)
                dcl = []
                for s in range(2):
                    tt = 2 * c + s
                    r = dcrot["i"] % 4
                    dcrot["i"] += 1
                    t1 = plan.op(V, (lambda e, d1b=d1b, r=r, tt=tt, h=h:
                        e.tensor_scalar(
                            dc_sb[r][:, :], d1b[:, 0:CH],
                            alcol[:, tt * NH + h: tt * NH + h + 1],
                            lndtT[:, tt * NH + h: tt * NH + h + 1],
                            ALU.subtract, ALU.min)), deps=[d1t, dc_war[r]], inc=True)
                    t2 = plan.op(SC, (lambda e, r=r: e.activation(
                        lb_sb[r][:, :], dc_sb[r][:, :], AF.Exp)), deps=[t1], inc=True)
                    dc_war[r] = t2
                    t3 = plan.op(V, (lambda e, r=r, c=c, s=s: e.tensor_tensor(
                        pb_sb[r][:, :], lb_sb[r][:, :],
                        smask[:, (2 * c + s) * CH:(2 * c + s + 1) * CH], ALU.mult)),
                        deps=[t2, pb_war[r]], inc=True)
                    dcl.append((r, t3))
                cr = odd
                lvt = plan.op(SC, (lambda e, d1b=d1b, cr=cr: e.activation(
                    csd[cr][:, :], d1b[0:NS, 0:CH], AF.Exp)),
                    deps=[d1t, csd_war[cr]], inc=True)
                csd_t = plan.op(V, (lambda e, cr=cr, c=c: e.tensor_tensor(
                    csd[cr][:, :], csd[cr][:, :], cfm[:, c * CH:(c + 1) * CH],
                    ALU.mult)), deps=[lvt], inc=True)
                bank_war[d1b.name] = csd_t
                yb = ps_mm[2] if j % 2 == 0 else ps_mm[3]
                tp = (0, 64) if odd else None
                first = (odd == 0)
                out_sl = (lambda yb=yb, odd=odd: yb[64:128, 0:CH] if odd
                          else yb[0:64, 0:CH])()
                mms = []
                for s in range(2):
                    r, t3 = dcl[s]
                    mms.append(plan.op(TE, (lambda e, out_sl=out_sl, s=s, h=h, r=r,
                                            tp=tp: e.matmul(
                        out_sl, xsT[:, s * I + h * PD: s * I + (h + 1) * PD],
                        pb_sb[r][:, :], start=(s == 0), stop=False,
                        skip_group_check=True, tile_position=tp)),
                        deps=[t3] + ([bank_war.get(yb.name), xs_war] if s == 0 and first
                                     else []), inc=True))
                    pb_war[r] = mms[-1]
                mms.append(plan.op(TE, (lambda e, out_sl=out_sl, h=h, cr=cr, tp=tp:
                    e.matmul(
                        out_sl, prevbb[:, h * PD:(h + 1) * PD], csd[cr][:, :],
                        start=False, stop=True, skip_group_check=True,
                        tile_position=tp)),
                    deps=[csd_t, prev_ready], inc=True))
                csd_war[cr] = mms[-1]
                offmm_last = mms[-1]
                if odd == 1:
                    yr = yrot["i"] % 2
                    yrot["i"] += 1
                    et = plan.op(V, (lambda e, yb=yb, j=j, c=c, yr=yr:
                        e.scalar_tensor_tensor(
                            ysta[yr][:, :], xs_c[:, j * CH:(j + 1) * CH],
                            dexp[:, j:j + 1], yb[:, 0:CH], ALU.mult, ALU.add)),
                        deps=[mms[-1], ysta_war[yr], xl], inc=True)
                    ydma = plan.dma(SY, (lambda e, j=j, c=c, yr=yr: e.dma_start(
                        out=y_d[j * 128:(j + 1) * 128, c * CH:(c + 1) * CH],
                        in_=ysta[yr][:, :])), s_y[yr], deps=[et])
                    ysta_war[yr] = ydma
                    bank_war[yb.name] = et
            row_war[rb] = d1t

    plan.barrier()
    stage_end()
    es4c.close()
    es4b.close()
    es4.close()
    es2.close()

    es5 = ExitStack()
    ytl = [sb(es5, f"ytl{i}", [128, 512], BF16) for i in range(2)]
    sgt = [sb(es5, f"sgt{i}", [128, 512], BF16) for i in range(2)]
    yp = sb(es5, "yp", [128, 32 * 512], BF16)
    ysq = [sb(es5, f"ysq{i}", [128, 512], BF16) for i in range(2)]
    rstdb = sb(es5, "rstdb", [128, 512], F32)
    rtile = [sb(es5, f"rtile{i}", [128, 512], F32) for i in range(2)]
    x1bf = [sb(es5, f"x1bf{i}", [128, 512], BF16) for i in range(2)]
    sum2row = sb(es5, "sum2row", [1, T], F32)
    sqr2row = sb(es5, "sqr2row", [1, T], F32)

    opj_last = None
    prev_r2 = None
    carry_sq = [None, None]
    opj_rt_war = {}
    opj_sf_war = {}
    opj_bf_war = {}
    opj_sq_done = {}
    for tb in range(2):
        ysq_war = list(carry_sq)
        sq_mm_last = None
        yp_tags = []
        for j in range(32):
            b = j % 2
            ylt = plan.dma(SY, (lambda e, j=j, tb=tb, b=b: e.dma_start(
                out=ytl[b][:, :], in_=y_d[j * 128:(j + 1) * 128,
                                          tb * 512:(tb + 1) * 512])),
                d_yt[b], deps=allw(s_y) + [yp_tags[-2] if j >= 2 else None])
            sgl = plan.dma(SY, (lambda e, j=j, tb=tb, b=b: e.dma_start(
                out=sgt[b][:, :], in_=sg_d[j * 128:(j + 1) * 128,
                                           tb * 512:(tb + 1) * 512])),
                d_sgt[b], deps=allw(s_stg) + [yp_tags[-2] if j >= 2 else None])
            ypt = plan.op(V, (lambda e, j=j, b=b: e.tensor_tensor(
                yp[:, j * 512:(j + 1) * 512], ytl[b][:, :], sgt[b][:, :], ALU.mult)),
                deps=[ylt, sgl, opj_last, prev_r2], inc=True)
            yp_tags.append(ypt)
            sqt = plan.op(SC, (lambda e, j=j, b=b: e.activation(
                ysq[b][:, :], yp[:, j * 512:(j + 1) * 512], AF.Square)),
                deps=[ypt, ysq_war[b]], inc=True)
            sq_mm_last = plan.op(TE, (lambda e, j=j, b=b: e.matmul(
                ps_row[0:1, :], onesbc[:, 0:1], ysq[b][:, :],
                start=(j == 0), stop=(j == 31), skip_group_check=True)),
                deps=[sqt, m_ones] + ([bank_war.get(ps_row.name)] if j == 0 else []),
                inc=True)
            ysq_war[b] = sq_mm_last
        r0 = plan.op(V, (lambda e, tb=tb: e.tensor_scalar(
            sum2row[0:1, tb * 512:(tb + 1) * 512], ps_row[0:1, :], 1.0 / I, EPS,
            ALU.mult, ALU.add)), deps=[sq_mm_last], inc=True)
        r1 = plan.op(SC, (lambda e, tb=tb: e.activation(
            sum2row[0:1, tb * 512:(tb + 1) * 512],
            sum2row[0:1, tb * 512:(tb + 1) * 512], AF.Ln)), deps=[r0], inc=True)
        r1b = plan.op(V, (lambda e, tb=tb: e.tensor_scalar(
            sum2row[0:1, tb * 512:(tb + 1) * 512],
            sum2row[0:1, tb * 512:(tb + 1) * 512], -0.5, None, ALU.mult)),
            deps=[r1], inc=True)
        r2 = plan.op(SC, (lambda e, tb=tb: e.activation(
            sum2row[0:1, tb * 512:(tb + 1) * 512],
            sum2row[0:1, tb * 512:(tb + 1) * 512], AF.Exp)), deps=[r1b], inc=True)
        bank_war[ps_row.name] = r0
        bk = next_bank()
        m3 = plan.op(TE, (lambda e, bk=bk, tb=tb: e.matmul(
            bk[:, :], onesf[0:1, :], sum2row[0:1, tb * 512:(tb + 1) * 512],
            start=True, stop=True, skip_group_check=True)),
            deps=[r2, m_ones, bank_war.get(bk.name)], inc=True)
        rb_t = plan.op(V, (lambda e, bk=bk: e.tensor_copy(rstdb[:, :], bk[:, :])),
                       deps=[m3], inc=True)
        bank_war[bk.name] = rb_t
        ypp_tags = []
        for j in range(32):
            ypp_tags.append(plan.op(V, (lambda e, j=j: e.scalar_tensor_tensor(
                yp[:, j * 512:(j + 1) * 512], yp[:, j * 512:(j + 1) * 512],
                nw[:, j:j + 1], rstdb[:, :], ALU.mult, ALU.mult)),
                deps=[yp_tags[j], rb_t], inc=True))
        ypp_all = ypp_tags[-1]

        def rhs_yp(kb, t0, tw):
            return yp[:, kb * 512: kb * 512 + 512]

        x1_evt = {"n": 0}

        def evac_outproj(meta, o_abs, t0, tw, bk2, mt2):
            tb_ = meta["tb"]
            b = x1_evt["n"] % 2
            x1_evt["n"] += 1
            rt = plan.dma(SY, (lambda e, o=o_abs, tb_=tb_, b=b: e.dma_start(
                out=rtile[b][:, :], in_=r_d[o:o + 128, tb_ * 512:(tb_ + 1) * 512])),
                d_rt[b], deps=allw(s_t2f) + [meta["rt_war"].get(b)])
            at = plan.op(V, (lambda e, bk2=bk2, b=b: e.tensor_add(
                stagef[b][:, :], bk2[:, :], rtile[b][:, :])),
                deps=[mt2, rt, meta["sf_war"].get(b), meta["sq_done"].get(b)],
                inc=True)
            meta["rt_war"][b] = at
            xd = plan.dma(SY, (lambda e, o=o_abs, tb_=tb_, b=b: e.dma_start(
                out=x1_d[o:o + 128, tb_ * 512:(tb_ + 1) * 512],
                in_=stagef[b][:, :])), s_sf[b], deps=[at])
            cst = plan.op(V, (lambda e, b=b: e.tensor_copy(
                x1bf[b][:, :], stagef[b][:, :])), deps=[at, meta["bf_war"].get(b)],
                inc=True)
            sq2 = plan.op(SC, (lambda e, b=b: e.activation(
                ysq[b][:, :], stagef[b][:, :], AF.Square)),
                deps=[at, meta["sq_war"].get(b)], inc=True)
            ki = meta["ki"]["n"]
            meta["ki"]["n"] += 1
            mm1 = plan.op(TE, (lambda e, b=b, ki=ki: e.matmul(
                ps_row[0:1, :], onesbc[:, 0:1], x1bf[b][:, :], start=(ki == 0),
                stop=(ki == 15), skip_group_check=True)),
                deps=[cst, m_ones] + ([bank_war.get(ps_row.name)] if ki == 0 else []),
                inc=True)
            mm2 = plan.op(TE, (lambda e, b=b, ki=ki: e.matmul(
                ps_e[0:1, :], onesbc[:, 0:1], ysq[b][:, :], start=(ki == 0),
                stop=(ki == 15), skip_group_check=True)),
                deps=[sq2, m_ones] + ([bank_war.get(ps_e.name)] if ki == 0 else []),
                inc=True)
            meta["bf_war"][b] = mm1
            meta["sq_war"][b] = mm2
            meta["sq_done"][b] = sq2
            meta["sf_war"][b] = xd
            meta["last_mm"] = (mm1, mm2, xd)
            return at

        meta2 = {"tb": tb, "tgrps": [(tb * 512, 512)], "rt_war": opj_rt_war,
                 "sf_war": opj_sf_war, "bf_war": opj_bf_war,
                 "sq_war": {0: sq_mm_last, 1: sq_mm_last},
                 "sq_done": opj_sq_done, "ki": {"n": 0}, "last_mm": None}
        slabs2 = [(o, 256, meta2) for o in range(0, H, 256)]
        opj_last = mm_stage(w2t, 32, slabs2, rhs_yp, evac_outproj,
                            extra_dep=ypp_all, slab_offs=W_OFFS['w2t'])
        prev_r2 = r2
        carry_sq = [meta2["sq_war"].get(0), meta2["sq_war"].get(1)]
        mm1, mm2, _ = meta2["last_mm"]
        ev1 = plan.op(V, (lambda e, tb=tb: e.tensor_copy(
            sum2row[0:1, tb * 512:(tb + 1) * 512], ps_row[0:1, :])),
            deps=[mm1], inc=True)
        ev2 = plan.op(V, (lambda e, tb=tb: e.tensor_copy(
            sqr2row[0:1, tb * 512:(tb + 1) * 512], ps_e[0:1, :])),
            deps=[mm2], inc=True)
        bank_war[ps_row.name] = ev1
        bank_war[ps_e.name] = ev2

    plan.barrier()
    stage_end()
    es5.close()

    es6 = ExitStack()
    m2row = sb(es6, "m2row", [1, T], F32)
    var2 = sb(es6, "var2", [1, T], F32)
    a2row = sb(es6, "a2row", [1, T], F32)
    mb2 = sb(es6, "mb2", [128, T], F32)
    ab2 = sb(es6, "ab2", [128, T], F32)
    x1t = [sb(es6, f"x1t{i}", [128, T], F32) for i in range(2)]
    jt = sb(es6, "jt", [128, T], F32)
    xb2 = sb(es6, "xb2", [128, KT * T], BF16)

    n1 = plan.op(V, lambda e: e.tensor_scalar(m2row[:, :], sum2row[:, :], 1.0 / H,
                                              None, ALU.mult), inc=True)
    n2 = plan.op(V, lambda e: e.tensor_mul(var2[:, :], m2row[:, :], m2row[:, :]),
                 deps=[n1], inc=True)
    n3 = plan.op(V, lambda e: e.scalar_tensor_tensor(
        var2[:, :], sqr2row[:, :], 1.0 / H, var2[:, :], ALU.mult, ALU.subtract),
        deps=[n2], inc=True)
    n3b = plan.op(V, lambda e: e.tensor_scalar(a2row[:, :], var2[:, :], 1.0, EPS,
                                               ALU.mult, ALU.add), deps=[n3],
                  inc=True)
    n4 = plan.op(SC, lambda e: e.activation(a2row[:, :], a2row[:, :], AF.Ln),
                 deps=[n3b], inc=True)
    n4b = plan.op(V, lambda e: e.tensor_scalar(a2row[:, :], a2row[:, :], -0.5, None,
                                               ALU.mult), deps=[n4], inc=True)
    n5 = plan.op(SC, lambda e: e.activation(a2row[:, :], a2row[:, :], AF.Exp),
                 deps=[n4b], inc=True)

    def bcast_row_t(src_row, dst_tile, dep):
        tags = []
        for t0 in (0, 512):
            bk = next_bank()
            mt = plan.op(TE, (lambda e, bk=bk, t0=t0, s=src_row: e.matmul(
                bk[:, :], onesf[0:1, :], s[0:1, t0:t0 + 512], start=True, stop=True,
                skip_group_check=True)), deps=[dep, m_ones, bank_war.get(bk.name)],
                inc=True)
            et = plan.op(V, (lambda e, bk=bk, t0=t0, d=dst_tile: e.tensor_copy(
                d[:, t0:t0 + 512], bk[:, :])), deps=[mt], inc=True)
            bank_war[bk.name] = et
            tags.append(et)
        return tags

    mb2_t = bcast_row_t(m2row, mb2, n1)
    ab2_t = bcast_row_t(a2row, ab2, n5)
    x1_war = [None, None]
    j2 = None
    for kt in range(KT):
        b = kt % 2
        xlt = plan.dma(SY, (lambda e, kt=kt, b=b: e.dma_start(
            out=x1t[b][:, :], in_=x1_d[kt * 128:(kt + 1) * 128, :])), d_x1t[b],
            deps=allw(s_sf) + [x1_war[b]])
        j1 = plan.op(V, (lambda e, b=b: e.tensor_sub(jt[:, :], x1t[b][:, :],
                                                     mb2[:, :])),
                     deps=[xlt, j2] + mb2_t, inc=True)
        j2 = plan.op(V, (lambda e, kt=kt: e.tensor_mul(
            xb2[:, kt * T:(kt + 1) * T], jt[:, :], ab2[:, :])),
            deps=[j1] + ab2_t, inc=True)
        x1_war[b] = j2

    plan.barrier()
    stage_end()

    es7 = ExitStack()
    sgf = sb(es7, "sgf", [128, 8 * 512], BF16)

    def rhs_xb2(kb, t0, tw):
        return xb2[:, kb * T + t0: kb * T + t0 + tw]

    sgf_war = {}

    def evac_ffn1(meta, o_abs, t0, tw, bk, mt):
        if meta["kind"] == "fg":
            gi = (o_abs % 512) // 128 * 2 + (t0 // 512)
            et = plan.op(SC, (lambda e, gi=gi, bk=bk: e.activation(
                sgf[:, gi * 512:(gi + 1) * 512], bk[:, :], AF.Silu)),
                deps=[mt, sgf_war.get(gi)], inc=True)
            sgf_war[gi] = et
            meta.setdefault("sg_tags", {})[gi] = et
            return et
        else:
            gi = (o_abs % 512) // 128 * 2 + (t0 // 512)
            gtag = meta["g_meta"]["sg_tags"][gi]
            s = next_stage()
            et = plan.op(V, (lambda e, gi=gi, bk=bk, s=s: e.tensor_tensor(
                stage[s][:, :], bk[:, :], sgf[:, gi * 512:(gi + 1) * 512], ALU.mult)),
                deps=[mt, gtag, stg_war[s]], inc=True)
            kb_ = (o_abs - INTER) // 128
            tb_ = t0 // 512
            dmat = plan.dma(SY, (lambda e, kb_=kb_, tb_=tb_, s=s: e.dma_start(
                out=h_d[:, tb_ * 32768 + kb_ * 512: tb_ * 32768 + (kb_ + 1) * 512],
                in_=stage[s][:, :])),
                s_stg[s], deps=[et])
            stg_war[s] = dmat
            sgf_war[gi] = et
            return et

    TG_T = [(0, 512), (512, 512)]
    slabs3 = []
    for gi in range(INTER // 512):
        mg = {"kind": "fg", "tgrps": TG_T}
        mu = {"kind": "fu", "tgrps": TG_T, "g_meta": mg}
        slabs3.append((gi * 512, 512, mg))
        slabs3.append((INTER + gi * 512, 512, mu))
    mm_stage(w3t, KT, slabs3, rhs_xb2, evac_ffn1,
             slab_offs=W_OFFS['w3t'])

    plan.barrier()
    stage_end()
    es7.close()

    es8 = ExitStack()
    hblk = sb(es8, "hblk", [128, 64 * 512], BF16)
    rtile2 = [sb(es8, f"rtile2_{i}", [128, 512], F32) for i in range(2)]
    KB2 = INTER // 128
    ffn2_last = None
    x1w = {}
    for tb in range(2):
        hl = plan.dma(SY, (lambda e, tb=tb: e.dma_start(
            out=hblk[:, :], in_=h_d[:, tb * 32768:(tb + 1) * 32768])),
            d_hblk, deps=allw(s_stg) + [ffn2_last])

        def rhs_h(kb, t0, tw):
            return hblk[:, kb * 512: kb * 512 + 512]

        def evac_ffn2(meta, o_abs, t0, tw, bk, mt, tb=tb):
            b = (o_abs // 128) % 2
            xlt = plan.dma(SY, (lambda e, o=o_abs, tb=tb, b=b: e.dma_start(
                out=rtile2[b][:, :], in_=x1_d[o:o + 128, tb * 512:(tb + 1) * 512])),
                d_x1t[b], deps=allw(s_sf) + [x1w.get(b)])
            at = plan.op(V, (lambda e, bk=bk, b=b: e.tensor_add(
                stagef[b][:, :], bk[:, :], rtile2[b][:, :])),
                deps=[mt, xlt, x1w.get(("sf", b))], inc=True)
            od = plan.dma(SY, (lambda e, o=o_abs, tb=tb, b=b: e.dma_start(
                out=out_hbm[o:o + 128, tb * 512:(tb + 1) * 512],
                in_=stagef[b][:, :])), s_sf[b], deps=[at])
            x1w[b] = at
            x1w[("sf", b)] = od
            return at

        meta4 = {"tgrps": [(tb * 512, 512)]}
        slabs4 = [(o, 128, meta4) for o in range(0, H, 128)]
        ffn2_last = mm_stage(w4t, KB2, slabs4, rhs_h, evac_ffn2,
                             extra_dep=hl, slab_offs=W_OFFS['w4t'])

    plan.barrier()
    stage_end()

    sems = {}
    sem_stack = ExitStack()
    for name in plan.sem_names:
        sems[name] = sem_stack.enter_context(nc.semaphore(name))
    global _SEMS
    _SEMS = sems

    with nc.Block() as block:
        def runner(engname):
            def run(e):
                for item in plan.ops[engname]:
                    if item[0] == "wait":
                        e.wait_ge(sems[item[1]], item[2])
                    else:
                        inst = item[1](e)
                        tag = item[2]
                        if tag is not None and inst is not None:
                            inst.then_inc(sems[tag[0]],
                                          16 if item[0] == "dma" else 1)
            return run
        block.tensor(runner(TE))
        block.vector(runner(V))
        block.scalar(runner(SC))
        block.gpsimd(runner(GP))
        block.sync(runner(SY))

    sem_stack.close()
    es8.close()
    es6.close()
    es0.close()
    return nc


_SEMS = None


def _tile_w(WT, slabs, kb_n):
    parts = []
    for (o0, w) in slabs:
        blk = WT[:, o0:o0 + w].reshape(kb_n, 128, w).transpose(1, 0, 2)
        parts.append(blk.reshape(128, kb_n * w))
    return np.ascontiguousarray(np.concatenate(parts, axis=1))


def _host_inputs(full):
    x = np.asarray(full["L0_attn_hidden_states"], np.float32)[0]
    xT = np.ascontiguousarray(x.T)
    w1t = _tile_w(np.asarray(full["L0_attn_in_proj_w"], np.float32).T,
                  SLABS_W1, KT)
    w2t = _tile_w(np.asarray(full["L0_attn_out_proj_w"], np.float32).T,
                  SLABS_W2, 32)
    w3t = _tile_w(np.asarray(full["L0_ffn_w_gate_up"], np.float32).T,
                  SLABS_W3, KT)
    w4t = _tile_w(np.asarray(full["L0_ffn_w_down"], np.float32).T,
                  SLABS_W4, INTER // 128)
    lnw = np.ascontiguousarray(
        np.asarray(full["L0_attn_ln_weight"], np.float32).reshape(KT, 128).T)
    nw = np.ascontiguousarray(
        np.asarray(full["L0_attn_norm_weight"], np.float32).reshape(I // 128, 128).T)
    dexp = np.ascontiguousarray(
        np.repeat(np.asarray(full["L0_attn_D"], np.float32), PD)
        .reshape(I // 128, 128).T)
    cwf = np.asarray(full["L0_attn_conv_w"], np.float32)
    cw = np.ascontiguousarray(
        cwf.reshape(33, 128, 4).transpose(1, 0, 2).reshape(128, 33 * 4))
    cb = np.ascontiguousarray(
        np.asarray(full["L0_attn_conv_b"], np.float32).reshape(33, 128).T)
    dtb = np.asarray(full["L0_attn_dt_bias"], np.float32).reshape(NH, 1)
    alog = np.asarray(full["L0_attn_A_log"], np.float32).reshape(NH, 1)
    import ml_dtypes
    sidx = np.arange(128)[:, None]
    lidx = np.arange(CH)[None, :]
    tri0 = (sidx <= lidx).astype(ml_dtypes.bfloat16)
    tri1 = ((sidx + 128) <= lidx).astype(ml_dtypes.bfloat16)
    tri = np.ascontiguousarray(np.concatenate([tri0, tri1], axis=1))
    identf = np.eye(128, dtype=np.float32)
    identb = np.eye(128, dtype=ml_dtypes.bfloat16)

    shared = dict(w1t=w1t, w2t=w2t, w3t=w3t, w4t=w4t, lnw=lnw, nw=nw, dexp=dexp,
                  cw=cw, cb=cb, dtb=dtb, alog=alog, tri=tri, identf=identf,
                  identb=identb)
    in_maps = []
    for r in range(NCORES):
        lo = r * T - 3
        if lo < 0:
            xs = np.concatenate(
                [np.zeros((H, 3), np.float32), xT[:, 0:T]], axis=1)
        else:
            xs = xT[:, lo:lo + TH]
        m = dict(shared)
        m["xt"] = np.ascontiguousarray(
            xs.reshape(KT, 128, TH).transpose(1, 0, 2).reshape(128, KT * TH))
        m["rmask"] = (np.arange(NCORES) < r).astype(np.float32).reshape(1, NCORES)
        in_maps.append(m)
    return in_maps


def kernel(**inputs):
    in_maps = _host_inputs(inputs)
    nc = build_nc()
    res = run_bass_kernel_spmd(nc, in_maps, core_ids=list(range(NCORES)))
    full = np.concatenate([np.asarray(res.results[r]["out"]) for r in range(NCORES)],
                          axis=1)
    return np.ascontiguousarray(full.T).reshape(1, 8192, H).astype(np.float32)


# version 04
# speedup vs baseline: 1.1616x; 1.1616x over previous
import numpy as np
import concourse.bass as bass
import concourse.mybir as mybir
from concourse.bass_utils import run_bass_kernel_spmd

F32 = mybir.dt.float32
BF16 = mybir.dt.bfloat16
AF = mybir.ActivationFunctionType
ALU = mybir.AluOpType

NCORES = 8
H = 2048
I = 4096
NH = 64
PD = 64
NS = 64
CH = 256
CONV = 4224
PROJ = 8384
T = 1024
TH = T + 3
NCHK = T // CH
KT = H // 128
EPS = 1e-5
INTER = 8192

ENGINES = ("tensor", "vector", "scalar", "gpsimd", "sync")
V, SC, TE, GP, SY = "vector", "scalar", "tensor", "gpsimd", "sync"

SLABS_W1 = ([(o, 512) for o in range(0, I, 512)]
            + [(o, 512) for o in range(I, 8192, 512)]
            + [(8192, 128), (8320, 64)])
SLABS_W2 = [(o, 256) for o in range(0, H, 256)]
SLABS_W3 = [x for gi in range(INTER // 512)
            for x in ((gi * 512, 512), (INTER + gi * 512, 512))]
SLABS_W4 = [(o, 128) for o in range(0, H, 128)]


def _slab_offsets(slabs, kb_n):
    offs = []
    t = 0
    for (o0, w) in slabs:
        offs.append(t)
        t += kb_n * w
    return offs, t


class Plan:
    def __init__(self):
        self.ops = {e: [] for e in ENGINES}
        self.cnt = {e: 0 for e in ENGINES}
        self.waited = {e: {} for e in ENGINES}
        self.dma_fills = {}
        self.sem_names = (["m_" + e for e in ENGINES]
                          + ["b_" + e for e in ENGINES])
        self.bcnt = {e: 0 for e in ENGINES}
        self.stopped = False

    def new_dma_sem(self, name):
        assert name not in self.dma_fills
        self.dma_fills[name] = 0
        self.sem_names.append(name)
        return name

    def _wait(self, eng, sem, val):
        if self.waited[eng].get(sem, 0) >= val:
            return
        self.waited[eng][sem] = val
        self.ops[eng].append(("wait", sem, val))

    def op(self, eng, fn, deps=(), inc=False):
        if self.stopped:
            return None
        for d in deps:
            if d is not None:
                self._wait(eng, d[0], d[1])
        tag = None
        if inc:
            self.cnt[eng] += 1
            tag = ("m_" + eng, self.cnt[eng])
        self.ops[eng].append(("op", fn, tag))
        return tag

    def dma(self, queue, fn, sem, deps=()):
        if self.stopped:
            return None
        for d in deps:
            if d is not None:
                self._wait(queue, d[0], d[1])
        self.dma_fills[sem] += 16
        tag = (sem, self.dma_fills[sem])
        self.ops[queue].append(("dma", fn, tag))
        return tag

    def barrier(self):
        if self.stopped:
            return
        marks = []
        for e in ENGINES:
            if e == SY:
                continue
            self.bcnt[e] += 1
            tag = ("b_" + e, self.bcnt[e])
            self.ops[e].append(("op", (lambda eng: eng.nop()), tag))
            marks.append(tag)
            if self.cnt[e] > 0:
                marks.append(("m_" + e, self.cnt[e]))
        for s, vv in self.dma_fills.items():
            if vv > 0:
                marks.append((s, vv))
        for e in ENGINES:
            for m in marks:
                self._wait(e, m[0], m[1])


def build_nc(stop_after=None):
    from contextlib import ExitStack
    nc = bass.Bass("TRN2", target_bir_lowering=False, debug=False)

    def din(name, shape, dt=F32):
        return nc.dram_tensor(name, shape, dt, kind="ExternalInput").ap()

    xt = din("xt", [128, KT * TH])
    _o1, _t1 = _slab_offsets(SLABS_W1, KT)
    _o2, _t2 = _slab_offsets(SLABS_W2, 32)
    _o3, _t3 = _slab_offsets(SLABS_W3, KT)
    _o4, _t4 = _slab_offsets(SLABS_W4, INTER // 128)
    w1t = din("w1t", [128, _t1])
    w2t = din("w2t", [128, _t2])
    w3t = din("w3t", [128, _t3])
    w4t = din("w4t", [128, _t4])
    W_OFFS = {"w1t": _o1, "w2t": _o2, "w3t": _o3, "w4t": _o4}
    lnw_c = din("lnw", [128, KT])
    nw_c = din("nw", [128, I // 128])
    dexp_c = din("dexp", [128, I // 128])
    cw_c = din("cw", [128, 33 * 4])
    cb_c = din("cb", [128, 33])
    dtb_c = din("dtb", [NH, 1])
    alog_c = din("alog", [NH, 1])
    tri_c = din("tri", [128, 2 * CH], BF16)
    identf_c = din("identf", [128, 128])
    identb_c = din("identb", [128, 128], BF16)
    rmask_c = din("rmask", [1, NCORES])

    out_hbm = nc.dram_tensor("out", [H, T], F32, kind="ExternalOutput").ap()

    def dint(name, shape, dt):
        return nc.dram_tensor(name, shape, dt, kind="Internal").ap()

    xbc_d = dint("xbc_d", [CONV, TH], BF16)
    sg_d = dint("sg_d", [I, T], BF16)
    xs_d = dint("xs_d", [128, NCHK * 33 * CH], BF16)
    r_d = dint("r_d", [H, T], F32)
    acum_d = dint("acum_dd", [NH, T], F32)
    st_d = dint("st_d", [NCHK * NH, I], BF16)
    ccin_d = dint("ccin_d", [NH, I + 1], F32)
    ccout_d = dint("ccout_d", [NCORES * NH, I + 1], F32)
    y_d = dint("y_d", [I, T], BF16)
    x1_d = dint("x1_d", [H, T], F32)
    h_d = dint("h_d", [128, 2 * 64 * 512], BF16)

    es0 = ExitStack()

    def sb(stack, name, shape, dt):
        return stack.enter_context(nc.sbuf_tensor(name, shape, dt))

    onesf = sb(es0, "onesf", [1, 128], F32)
    onescf = sb(es0, "onescf", [128, 1], F32)
    onesbc = sb(es0, "onesbc", [128, 1], BF16)
    lnw = sb(es0, "lnw_sb", [128, KT], F32)
    nw = sb(es0, "nw_sb", [128, I // 128], F32)
    dexp = sb(es0, "dexp_sb", [128, I // 128], F32)
    cw = sb(es0, "cw_sb", [128, 33 * 4], F32)
    cb = sb(es0, "cb_sb", [128, 33], F32)
    dtb = sb(es0, "dtb_sb", [NH, 1], F32)
    alog = sb(es0, "alog_sb", [NH, 1], F32)
    tri = sb(es0, "tri_sb", [128, 2 * CH], BF16)
    idf = sb(es0, "idf_sb", [128, 128], F32)
    idb = sb(es0, "idb_sb", [128, 128], BF16)
    rmask = sb(es0, "rmask_sb", [1, NCORES], F32)
    SLABW = 8192
    slab = [sb(es0, f"slab{i}", [128, SLABW], BF16) for i in range(2)]
    stage = [sb(es0, f"stage{i}", [128, 512], BF16) for i in range(4)]
    stagef = [sb(es0, f"stagef{i}", [128, 512], F32) for i in range(2)]
    dt_sb = sb(es0, "dt_sb", [NH, T], F32)
    rowtmp = sb(es0, "rowtmp", [1, 128], F32)

    ps_mm = [es0.enter_context(nc.psum_tensor(f"ps_mm{i}", [128, 512], F32))
             for i in range(4)]
    ps_e = es0.enter_context(nc.psum_tensor("ps_e", [128, 512], F32))
    ps_tpb = es0.enter_context(nc.psum_tensor("ps_tpb", [128, 1024], BF16))
    ps_tpf = es0.enter_context(nc.psum_tensor("ps_tpf", [128, 512], F32))
    ps_row = es0.enter_context(nc.psum_tensor("ps_row", [1, 512], F32))

    plan = Plan()
    _STOP = {'n': 0}

    def stage_end():
        _STOP['n'] += 1
        if stop_after is not None and _STOP['n'] >= stop_after:
            plan.stopped = True
    nds = plan.new_dma_sem
    d_const = nds("d_const"); d_xf = nds("d_xf")
    d_slab = [nds("d_slab0"), nds("d_slab1")]
    s_stg = [nds(f"s_stg{i}") for i in range(4)]
    s_sf = [nds(f"s_sf{i}") for i in range(2)]
    s_t2f = [nds(f"s_t2f{i}") for i in range(2)]
    s_acum = nds("s_acum"); s_st = nds("s_st")
    s_ccin = nds("s_ccin"); d_cc = nds("d_cc")
    d_xbct = [nds("d_xbct0"), nds("d_xbct1")]
    d_xsc = nds("d_xsc")
    d_row = [nds("d_row0"), nds("d_row1")]
    d_lq = nds("d_lq"); d_dt = nds("d_dt"); d_stl = nds("d_stl")
    s_y = [nds("s_y0"), nds("s_y1")]
    d_yt = [nds("d_yt0"), nds("d_yt1")]
    d_sgt = [nds("d_sgt0"), nds("d_sgt1")]
    d_x1t = [nds("d_x1t0"), nds("d_x1t1")]
    d_rt = [nds("d_rt0"), nds("d_rt1")]
    d_hblk = nds("d_hblk")

    def allw(sems_):
        return [(s, plan.dma_fills[s]) for s in sems_ if plan.dma_fills[s] > 0]

    MMB = {"i": 0}

    def next_bank():
        b = ps_mm[MMB["i"] % 4]
        MMB["i"] += 1
        return b

    bank_war = {}

    for dst, src in [(lnw, lnw_c), (nw, nw_c), (dexp, dexp_c), (cw, cw_c),
                     (cb, cb_c), (dtb, dtb_c), (alog, alog_c), (tri, tri_c),
                     (idf, identf_c), (idb, identb_c), (rmask, rmask_c)]:
        plan.dma(SY, (lambda e, d=dst, s=src: e.dma_start(out=d[:, :], in_=s[:, :])),
                 d_const)
    plan.op(V, lambda e: e.memset(onesf[:, :], 1.0), inc=True)
    plan.op(V, lambda e: e.memset(onescf[:, :], 1.0), inc=True)
    m_ones = plan.op(V, lambda e: e.memset(onesbc[:, :], 1.0), inc=True)

    es1 = ExitStack()
    xfull = sb(es1, "xfull", [128, KT * TH], F32)
    xsq = [sb(es1, f"xsq{i}", [128, TH], F32) for i in range(2)]
    srow = sb(es1, "srow", [1, TH], F32)
    sqrow = sb(es1, "sqrow", [1, TH], F32)
    mrow = sb(es1, "mrow", [1, TH], F32)
    varrow = sb(es1, "varrow", [1, TH], F32)
    arow = sb(es1, "arow", [1, TH], F32)
    aorow = sb(es1, "aorow", [1, TH], F32)
    mb = sb(es1, "mb", [128, TH], F32)
    ab = sb(es1, "ab", [128, TH], F32)
    aob = sb(es1, "aob", [128, TH], F32)
    t1f = sb(es1, "t1f", [128, TH], F32)
    t2f = [sb(es1, f"t2f{i}", [128, TH], F32) for i in range(2)]
    xb = sb(es1, "xb", [128, KT * TH], BF16)

    t_x = plan.dma(SY, lambda e: e.dma_start(out=xfull[:, :], in_=xt[:, :]), d_xf)

    TG = [(0, 512), (512, 512), (1024, 3)]
    sum_dst = [ps_row, ps_mm[0], ps_mm[1]]
    sq_dst = [ps_e, ps_mm[2], ps_mm[3]]
    sqmm_tag = {}
    last_mm = None
    for kt in range(KT):
        b = kt % 2
        sq_t = plan.op(
            SC, (lambda e, b=b, kt=kt: e.activation(
                xsq[b][:, :], xfull[:, kt * TH:(kt + 1) * TH], AF.Square)),
            deps=[t_x, sqmm_tag.get(kt - 2)], inc=True)
        for gi, (t0, tw) in enumerate(TG):
            plan.op(TE, (lambda e, kt=kt, gi=gi, t0=t0, tw=tw: e.matmul(
                sum_dst[gi][0:1, 0:tw], onescf[:, 0:1],
                xfull[:, kt * TH + t0: kt * TH + t0 + tw],
                start=(kt == 0), stop=(kt == KT - 1), skip_group_check=True)),
                deps=[t_x, m_ones])
            last_mm = plan.op(TE, (lambda e, b=b, gi=gi, t0=t0, tw=tw, kt=kt:
                e.matmul(
                    sq_dst[gi][0:1, 0:tw], onescf[:, 0:1], xsq[b][:, t0:t0 + tw],
                    start=(kt == 0), stop=(kt == KT - 1), skip_group_check=True)),
                deps=[sq_t, m_ones], inc=True)
        sqmm_tag[kt] = last_mm
    ev = []
    for gi, (t0, tw) in enumerate(TG):
        ev.append(plan.op(V, (lambda e, gi=gi, t0=t0, tw=tw: e.tensor_copy(
            srow[0:1, t0:t0 + tw], sum_dst[gi][0:1, 0:tw])), deps=[last_mm], inc=True))
        ev.append(plan.op(V, (lambda e, gi=gi, t0=t0, tw=tw: e.tensor_copy(
            sqrow[0:1, t0:t0 + tw], sq_dst[gi][0:1, 0:tw])), deps=[last_mm], inc=True))
    m1 = plan.op(V, lambda e: e.tensor_scalar(mrow[:, :], srow[:, :], 1.0 / H, None,
                                              ALU.mult), deps=ev, inc=True)
    m2t = plan.op(V, lambda e: e.tensor_mul(varrow[:, :], mrow[:, :], mrow[:, :]),
                  deps=[m1], inc=True)
    m3 = plan.op(V, lambda e: e.scalar_tensor_tensor(
        varrow[:, :], sqrow[:, :], 1.0 / H, varrow[:, :], ALU.mult, ALU.subtract),
        deps=[m2t], inc=True)
    a0 = plan.op(V, lambda e: e.tensor_scalar(arow[:, :], varrow[:, :], 1.0 + EPS,
                                              EPS * EPS, ALU.mult, ALU.add),
                 deps=[m3], inc=True)
    a1 = plan.op(SC, lambda e: e.activation(arow[:, :], arow[:, :], AF.Ln),
                 deps=[a0], inc=True)
    a1b = plan.op(V, lambda e: e.tensor_scalar(arow[:, :], arow[:, :], -0.5, None,
                                               ALU.mult), deps=[a1], inc=True)
    a2 = plan.op(SC, lambda e: e.activation(arow[:, :], arow[:, :], AF.Exp),
                 deps=[a1b], inc=True)
    a2b = plan.op(V, lambda e: e.tensor_scalar(aorow[:, :], varrow[:, :], 1.0, EPS,
                                               ALU.mult, ALU.add), deps=[m3], inc=True)
    a3 = plan.op(SC, lambda e: e.activation(aorow[:, :], aorow[:, :], AF.Ln),
                 deps=[a2b], inc=True)
    a3b = plan.op(V, lambda e: e.tensor_scalar(aorow[:, :], aorow[:, :], -0.5, None,
                                               ALU.mult), deps=[a3], inc=True)
    a4 = plan.op(SC, lambda e: e.activation(aorow[:, :], aorow[:, :], AF.Exp),
                 deps=[a3b], inc=True)

    def bcast_row(src_row, dst_tile, dep):
        tags = []
        for (t0, tw) in TG:
            bk = next_bank()
            mt = plan.op(TE, (lambda e, bk=bk, t0=t0, tw=tw, s=src_row: e.matmul(
                bk[:, 0:tw], onesf[0:1, :], s[0:1, t0:t0 + tw],
                start=True, stop=True, skip_group_check=True)),
                deps=[dep, m_ones, bank_war.get(bk.name)], inc=True)
            et = plan.op(V, (lambda e, bk=bk, t0=t0, tw=tw, d=dst_tile: e.tensor_copy(
                d[:, t0:t0 + tw], bk[:, 0:tw])), deps=[mt], inc=True)
            bank_war[bk.name] = et
            tags.append(et)
        return tags

    mb_t = bcast_row(mrow, mb, m1)
    ab_t = bcast_row(arow, ab, a2)
    aob_t = bcast_row(aorow, aob, a4)

    r_tags = []
    prev_dma = {}
    for kt in range(KT):
        b = kt % 2
        tt1 = plan.op(V, (lambda e, kt=kt: e.tensor_sub(
            t1f[:, :], xfull[:, kt * TH:(kt + 1) * TH], mb[:, :])),
            deps=[t_x] + mb_t, inc=True)
        plan.op(V, (lambda e, kt=kt: e.scalar_tensor_tensor(
            xb[:, kt * TH:(kt + 1) * TH], ab[:, :], lnw[:, kt:kt + 1], t1f[:, :],
            ALU.mult, ALU.mult)), deps=[tt1] + ab_t + [("d_const", 16 * 11)], inc=True)
        tt2 = plan.op(V, (lambda e, kt=kt, b=b: e.tensor_mul(
            t2f[b][:, :], t1f[:, :], aob[:, :])),
            deps=[tt1, prev_dma.get(b)] + aob_t, inc=True)
        tt3 = plan.op(V, (lambda e, kt=kt, b=b: e.tensor_add(
            t2f[b][:, :], t2f[b][:, :], xfull[:, kt * TH:(kt + 1) * TH])),
            deps=[tt2], inc=True)
        prev_dma[b] = plan.dma(SY, (lambda e, kt=kt, b=b: e.dma_start(
            out=r_d[kt * 128:(kt + 1) * 128, :], in_=t2f[b][:, 3:TH])),
            s_t2f[b], deps=[tt3])
        r_tags.append(prev_dma[b])

    plan.barrier()
    stage_end()

    def mm_stage(w_hbm, kb_n, slabs, rhs_fn, evac_fn, extra_dep=None,
                 slab_offs=None):
        slab_last_use = [None, None]
        slab_tag = [None, None]
        for si, (o0, wdt, meta) in enumerate(slabs):
            sbuf_i = si % 2
            soff = slab_offs[si]
            nsub = wdt // 128 if wdt >= 128 else 1
            slab_tag[sbuf_i] = plan.dma(GP, (lambda e, soff=soff, wdt=wdt, i=sbuf_i:
                e.dma_start(
                    out=slab[i][:, 0:kb_n * wdt],
                    in_=w_hbm[:, soff:soff + kb_n * wdt])),
                d_slab[sbuf_i], deps=[slab_last_use[sbuf_i], extra_dep])
            tgrps = meta["tgrps"]
            for osub in range(nsub):
                ow = min(128, wdt)
                for (t0, tw) in tgrps:
                    bk = next_bank()
                    mt = None
                    for kb in range(kb_n):
                        mt = plan.op(TE, (lambda e, bk=bk, kb=kb, osub=osub, ow=ow,
                                          wdt=wdt, t0=t0, tw=tw, i=sbuf_i: e.matmul(
                            bk[0:ow, 0:tw],
                            slab[i][:, kb * wdt + osub * 128: kb * wdt + osub * 128 + ow],
                            rhs_fn(kb, t0, tw),
                            start=(kb == 0), stop=(kb == kb_n - 1),
                            skip_group_check=True)),
                            deps=([slab_tag[sbuf_i], bank_war.get(bk.name)]
                                  if kb == 0 else []),
                            inc=(kb == kb_n - 1))
                    slab_last_use[sbuf_i] = mt
                    war = evac_fn(meta, o0 + osub * 128, t0, tw, bk, mt)
                    bank_war[bk.name] = war
        return mt

    stg_war = [None] * 4
    STG = {"i": 0}

    def next_stage():
        s = STG["i"] % 4
        STG["i"] += 1
        return s

    def rhs_xb(kb, t0, tw):
        return xb[:, kb * TH + t0: kb * TH + t0 + tw]

    def evac_inproj(meta, o_abs, t0, tw, bk, mt):
        kind = meta["kind"]
        if kind == "dt":
            return plan.op(V, (lambda e, t0=t0, tw=tw, bk=bk: e.tensor_copy(
                dt_sb[:, t0 - 3: t0 - 3 + tw], bk[0:NH, 0:tw])), deps=[mt], inc=True)
        s = next_stage()
        if kind == "gate":
            et = plan.op(SC, (lambda e, s=s, bk=bk, tw=tw: e.activation(
                stage[s][:, 0:tw], bk[:, 0:tw], AF.Silu)),
                deps=[mt, stg_war[s]], inc=True)
            dmat = plan.dma(SY, (lambda e, s=s, o=o_abs, t0=t0, tw=tw: e.dma_start(
                out=sg_d[o:o + 128, t0 - 3: t0 - 3 + tw], in_=stage[s][:, 0:tw])),
                s_stg[s], deps=[et])
        else:
            et = plan.op(V, (lambda e, s=s, bk=bk, tw=tw: e.tensor_copy(
                stage[s][:, 0:tw], bk[:, 0:tw])), deps=[mt, stg_war[s]], inc=True)
            dmat = plan.dma(SY, (lambda e, s=s, o=o_abs, t0=t0, tw=tw: e.dma_start(
                out=xbc_d[o - I:o - I + 128, t0: t0 + tw], in_=stage[s][:, 0:tw])),
                s_stg[s], deps=[et])
        stg_war[s] = dmat
        return et

    TG_MAIN = [(3, 512), (515, 512)]
    TG_FULL = [(0, 512), (512, 512), (1024, 3)]
    slabs1 = ([(o, 512, {"kind": "gate", "tgrps": TG_MAIN}) for o in range(0, I, 512)]
              + [(o, 512, {"kind": "xbc", "tgrps": TG_FULL}) for o in range(I, 8192, 512)]
              + [(8192, 128, {"kind": "xbc", "tgrps": TG_FULL})]
              + [(8320, 64, {"kind": "dt", "tgrps": TG_MAIN})])
    mm_stage(w1t, KT, slabs1, rhs_xb, evac_inproj,
             slab_offs=W_OFFS['w1t'])

    plan.barrier()
    stage_end()
    es1.close()

    es2 = ExitStack()
    dts = sb(es2, "dts", [NH, T], F32)
    lndt = sb(es2, "lndt", [NH, T], F32)
    dA = sb(es2, "dA", [NH, T], F32)
    acum = sb(es2, "acum_sb", [NH, T], F32)
    alf = sb(es2, "alf", [NH, T], F32)
    tmpa = sb(es2, "tmpa", [NH, T], F32)
    zero64 = sb(es2, "zero64", [NH, CH], F32)
    expa = sb(es2, "expa", [NH, 1], F32)
    cd = sb(es2, "cd", [NH, NCHK], F32)
    dtot = sb(es2, "dtot", [NH, 1], F32)
    decb = sb(es2, "decb", [NH, NCHK * NH], F32)
    alcol = sb(es2, "alcol", [128, 8 * NH], F32)
    lndtT = sb(es2, "lndtT", [128, 8 * NH], F32)
    wcol = sb(es2, "wcol", [128, 8 * NH], F32)
    bfm = sb(es2, "bfm", [NS, T], BF16)
    cfm = sb(es2, "cfm", [NS, T], BF16)
    smask = sb(es2, "smask", [128, NCHK * 2 * CH], BF16)

    zt = plan.op(V, lambda e: e.memset(zero64[:, :], 0.0), inc=True)
    u1 = plan.op(V, lambda e: e.tensor_scalar(tmpa[:, :], dt_sb[:, :], dtb[:, 0:1],
                                              None, ALU.add), inc=True)
    u2 = plan.op(V, lambda e: e.scalar_tensor_tensor(lndt[:, :], tmpa[:, :], -1.0,
                                                     tmpa[:, :], ALU.mult, ALU.min),
                 deps=[u1], inc=True)
    u3 = plan.op(SC, lambda e: e.activation(lndt[:, :], lndt[:, :], AF.Exp),
                 deps=[u2], inc=True)
    u4 = plan.op(SC, lambda e: e.activation(lndt[:, :], lndt[:, :], AF.Ln, bias=1.0),
                 deps=[u3], inc=True)
    u5 = plan.op(V, lambda e: e.tensor_scalar(dA[:, :], tmpa[:, :], 0.0, None, ALU.max),
                 deps=[u2], inc=True)
    u6 = plan.op(V, lambda e: e.tensor_add(dts[:, :], lndt[:, :], dA[:, :]),
                 deps=[u4, u5], inc=True)
    u7 = plan.op(SC, lambda e: e.activation(lndt[:, :], dts[:, :], AF.Ln),
                 deps=[u6], inc=True)
    u8 = plan.op(SC, lambda e: e.activation(expa[:, :], alog[:, 0:1], AF.Exp),
                 deps=[("d_const", 16 * 11)], inc=True)
    u9 = plan.op(V, lambda e: e.tensor_scalar(dA[:, :], dts[:, :], expa[:, 0:1], -1.0,
                                              ALU.mult, ALU.mult), deps=[u6, u8],
                 inc=True)
    sc_t = []
    for c in range(NCHK):
        sc_t.append(plan.op(V, (lambda e, c=c: e.tensor_tensor_scan(
            acum[:, c * CH:(c + 1) * CH], dA[:, c * CH:(c + 1) * CH], zero64[:, :],
            0.0, ALU.add, ALU.add)), deps=[u9, zt], inc=True))
    al_t = plan.op(V, lambda e: e.tensor_sub(alf[:, :], acum[:, :], lndt[:, :]),
                   deps=sc_t + [u7], inc=True)
    acum_dma = plan.dma(SY, lambda e: e.dma_start(out=acum_d[:, :], in_=acum[:, :]),
                        s_acum, deps=sc_t)
    cd_t = plan.op(SC, lambda e: e.activation(
        cd[:, :], acum.ap()[:, CH - 1::CH], AF.Exp), deps=sc_t, inc=True)
    d1 = plan.op(V, lambda e: e.tensor_mul(dtot[:, :], cd[:, 0:1], cd[:, 1:2]),
                 deps=[cd_t], inc=True)
    d2 = plan.op(V, lambda e: e.tensor_mul(dtot[:, :], dtot[:, :], cd[:, 2:3]),
                 deps=[d1], inc=True)
    dtot_t = plan.op(V, lambda e: e.tensor_mul(dtot[:, :], dtot[:, :], cd[:, 3:4]),
                     deps=[d2], inc=True)

    tp_war = [None]

    def transpose_f32(src_ap, np_, dst_ap, dep):
        mt = plan.op(TE, (lambda e: e.transpose(
            ps_tpf[:, 0:np_], src_ap, idf[0:np_, 0:np_])),
            deps=[dep, tp_war[0], ("d_const", 16 * 11)], inc=True)
        et = plan.op(V, (lambda e: e.tensor_copy(
            dst_ap, ps_tpf[:, 0:np_])), deps=[mt], inc=True)
        tp_war[0] = et
        return et

    for tt in range(8):
        transpose_f32(alf[:, tt * 128:(tt + 1) * 128], NH,
                      alcol[:, tt * NH:(tt + 1) * NH], al_t)
        transpose_f32(lndt[:, tt * 128:(tt + 1) * 128], NH,
                      lndtT[:, tt * NH:(tt + 1) * NH], u7)
    for c in range(NCHK):
        w1 = plan.op(V, (lambda e, c=c: e.tensor_scalar(
            tmpa[:, c * CH:(c + 1) * CH], acum[:, c * CH:(c + 1) * CH],
            acum[:, (c + 1) * CH - 1:(c + 1) * CH], -1.0, ALU.subtract, ALU.mult)),
            deps=sc_t, inc=True)
        w2 = plan.op(V, (lambda e, c=c: e.tensor_add(
            tmpa[:, c * CH:(c + 1) * CH], tmpa[:, c * CH:(c + 1) * CH],
            lndt[:, c * CH:(c + 1) * CH])), deps=[w1, u7], inc=True)
        w3 = plan.op(SC, (lambda e, c=c: e.activation(
            tmpa[:, c * CH:(c + 1) * CH], tmpa[:, c * CH:(c + 1) * CH], AF.Exp)),
            deps=[w2], inc=True)
        for s in range(2):
            tt = 2 * c + s
            transpose_f32(tmpa[:, tt * 128:(tt + 1) * 128], NH,
                          wcol[:, tt * NH:(tt + 1) * NH], w3)
    for c in range(NCHK):
        mt = plan.op(TE, (lambda e, c=c: e.transpose(
            ps_tpf[0:1, 0:NH], cd[:, c:c + 1], idf[0:NH, 0:NH])),
            deps=[cd_t, tp_war[0]], inc=True)
        et = plan.op(V, (lambda e: e.tensor_copy(
            rowtmp[0:1, 0:NH], ps_tpf[0:1, 0:NH])),
            deps=[mt], inc=True)
        tp_war[0] = et
        bk = next_bank()
        m2 = plan.op(TE, (lambda e, bk=bk: e.matmul(
            bk[0:NH, 0:NH], onesf[0:1, 0:NH], rowtmp[0:1, 0:NH],
            start=True, stop=True, skip_group_check=True)),
            deps=[et, m_ones, bank_war.get(bk.name)], inc=True)
        e2 = plan.op(V, (lambda e, c=c, bk=bk: e.tensor_copy(
            decb[:, c * NH:(c + 1) * NH], bk[0:NH, 0:NH])), deps=[m2], inc=True)
        bank_war[bk.name] = e2

    plan.barrier()
    stage_end()

    es3 = ExitStack()
    xbct = [sb(es3, f"xbct{i}", [128, TH], BF16) for i in range(2)]
    cacc = [sb(es3, f"cacc{i}", [128, T], F32) for i in range(2)]
    gtmp = sb(es3, "gtmp", [128, T], F32)
    conv_last = [None, None]
    acc_war = [None, None]
    for it in range(33):
        b = it % 2
        ld = plan.dma(SY, (lambda e, it=it, b=b: e.dma_start(
            out=xbct[b][:, :], in_=xbc_d[it * 128:(it + 1) * 128, :])),
            d_xbct[b], deps=[conv_last[b]] + allw(s_stg))
        a = b
        t = plan.op(V, (lambda e, b=b, a=a, it=it: e.tensor_scalar(
            cacc[a][:, :], xbct[b][:, 0:T], cw[:, it * 4:it * 4 + 1], None,
            ALU.mult)), deps=[ld, acc_war[a], ("d_const", 16 * 11)], inc=True)
        for k in range(1, 4):
            t = plan.op(V, (lambda e, b=b, a=a, it=it, k=k: e.scalar_tensor_tensor(
                cacc[a][:, :], xbct[b][:, k:T + k], cw[:, it * 4 + k:it * 4 + k + 1],
                cacc[a][:, :], ALU.mult, ALU.add)), deps=[t], inc=True)
        conv_last[b] = t
        sil = []
        for s2 in range(2):
            ss = next_stage()
            et = plan.op(SC, (lambda e, a=a, s2=s2, it=it, ss=ss: e.activation(
                stage[ss][:, :], cacc[a][:, s2 * 512:(s2 + 1) * 512], AF.Silu,
                bias=cb[:, it:it + 1])), deps=[t, stg_war[ss]], inc=True)
            W33 = 33 * CH
            dmat = None
            for half in range(2):
                cc_ = s2 * 2 + half
                dmat = plan.dma(SY, (lambda e, it=it, ss=ss, cc_=cc_, half=half,
                                     W33=W33: e.dma_start(
                    out=xs_d[:, cc_ * W33 + it * CH: cc_ * W33 + (it + 1) * CH],
                    in_=stage[ss][:, half * CH:(half + 1) * CH])),
                    s_stg[ss], deps=[et])
            stg_war[ss] = dmat
            sil.append(et)
            if it == 32:
                b1 = plan.op(V, (lambda e, s2=s2, ss=ss: e.tensor_copy(
                    bfm[:, s2 * 512:(s2 + 1) * 512], stage[ss][0:NS, :])),
                    deps=[et], inc=True)
                c1 = plan.op(V, (lambda e, s2=s2, ss=ss: e.tensor_copy(
                    cfm[:, s2 * 512:(s2 + 1) * 512], stage[ss][NS:128, :])),
                    deps=[et], inc=True)
                stg_war[ss] = c1
        acc_war[a] = sil[-1]

    plan.barrier()
    stage_end()
    es3.close()

    es4 = ExitStack()
    xs_c = sb(es4, "xs_c", [128, 32 * CH], BF16)
    xsT = sb(es4, "xsT", [128, 2 * I], BF16)
    btc = sb(es4, "btc", [128, 2 * NS], BF16)
    bd = [sb(es4, f"bd{i}", [128, 2 * NS], BF16) for i in range(2)]
    es4s = ExitStack()
    ststage = sb(es4s, "ststage", [NH, I], F32)
    carry = [sb(es4s, "carry0", [NH, I], F32)]

    cz = plan.op(V, lambda e: e.memset(carry[0][:, :], 0.0), inc=True)
    bd_war = [None, None]
    tpb_war = [None]

    def transpose_b(src_ap, np_, dst_ap, dep):
        mt = plan.op(TE, (lambda e: e.transpose(
            ps_tpb[:, 0:np_], src_ap, idb[0:np_, 0:np_])),
            deps=[dep, tpb_war[0]], inc=True)
        et = plan.op(V, (lambda e: e.tensor_copy(
            dst_ap, ps_tpb[:, 0:np_])), deps=[mt], inc=True)
        tpb_war[0] = et
        return et

    def load_xs_chunk(c, war, dsem):
        W33 = 33 * CH
        return plan.dma(SY, (lambda e, c=c, W33=W33: e.dma_start(
            out=xs_c[:, :],
            in_=xs_d[:, c * W33: c * W33 + 32 * CH])),
            dsem, deps=[war] + allw(s_stg))

    def build_xsT(dep_load):
        tags = []
        for it in range(32):
            for s in range(2):
                tags.append(transpose_b(
                    xs_c[:, it * CH + s * 128: it * CH + s * 128 + 128], 128,
                    xsT[:, s * I + it * 128: s * I + it * 128 + 128], dep_load))
        return tags

    xs_war = None
    scan_done = []
    stev_prev = None
    st_dma_last = None
    for c in range(NCHK):
        xl = load_xs_chunk(c, xs_war, d_xsc)
        xst_tags = build_xsT(xl)
        xs_war = xst_tags[-1]
        for s in range(2):
            bk = next_bank()
            mt = plan.op(TE, (lambda e, c=c, s=s, bk=bk: e.matmul(
                bk[:, 0:CH], bfm[:, c * CH + s * 128: c * CH + s * 128 + 128],
                cfm[:, c * CH:(c + 1) * CH], start=True, stop=True,
                skip_group_check=True)),
                deps=[bank_war.get(bk.name)], inc=True)
            et = plan.op(V, (lambda e, c=c, s=s, bk=bk: e.tensor_tensor(
                smask[:, (2 * c + s) * CH:(2 * c + s + 1) * CH], bk[:, 0:CH],
                tri[:, s * CH:(s + 1) * CH], ALU.mult)), deps=[mt], inc=True)
            bank_war[bk.name] = et
        bt_tags = []
        for s in range(2):
            bt_tags.append(transpose_b(
                bfm[:, c * CH + s * 128: c * CH + s * 128 + 128], NS,
                btc[:, s * NS:(s + 1) * NS], dep=None))
        for hg in range(8):
            mm_last = None
            for hh in range(8):
                h = hg * 8 + hh
                bdi = hh % 2
                bdt = []
                for s in range(2):
                    tt = 2 * c + s
                    bdt.append(plan.op(V, (lambda e, h=h, s=s, tt=tt, bdi=bdi:
                        e.tensor_scalar(
                            bd[bdi][:, s * NS:(s + 1) * NS],
                            btc[:, s * NS:(s + 1) * NS],
                            wcol[:, tt * NH + h: tt * NH + h + 1], None, ALU.mult)),
                        deps=[bt_tags[s], bd_war[bdi]], inc=True))
                for s in range(2):
                    mm_last = plan.op(TE, (lambda e, h=h, hh=hh, s=s, bdi=bdi:
                        e.matmul(
                            ps_e[0:NS, hh * PD:(hh + 1) * PD],
                            bd[bdi][:, s * NS:(s + 1) * NS],
                            xsT[:, s * I + h * PD: s * I + (h + 1) * PD],
                            start=(hh == 0 and s == 0), stop=(hh == 7 and s == 1),
                            skip_group_check=True)),
                        deps=[bdt[s]] + ([xs_war, stev_prev] if hh == 0 and s == 0
                                         else []),
                        inc=True)
                bd_war[bdi] = mm_last
            stev_prev = plan.op(V, (lambda e, hg=hg: e.tensor_copy(
                ststage[:, hg * 512:(hg + 1) * 512], ps_e[0:NH, :])),
                deps=[mm_last, st_dma_last], inc=True)
        st_dma = plan.dma(GP, (lambda e, c=c: e.dma_start(
            out=st_d[c * NH:(c + 1) * NH, :], in_=ststage[:, :])),
            s_st, deps=[stev_prev])
        st_dma_last = st_dma
        src = carry[0]
        dst = carry[0]
        last = None
        for h in range(NH):
            last = plan.op(V, (lambda e, c=c, h=h, src=src, dst=dst:
                e.scalar_tensor_tensor(
                    dst[:, h * PD:(h + 1) * PD], src[:, h * PD:(h + 1) * PD],
                    decb[:, c * NH + h: c * NH + h + 1],
                    ststage[:, h * PD:(h + 1) * PD], ALU.mult, ALU.add)),
                deps=[stev_prev, st_dma] + ([cz] if c == 0 else [scan_done[-1]]),
                inc=True)
        scan_done.append(last)

    cc1 = plan.dma(SY, lambda e: e.dma_start(out=ccin_d[:, 0:I], in_=carry[0][:, :]),
                   s_ccin, deps=[scan_done[-1]])
    cc2 = plan.dma(SY, lambda e: e.dma_start(out=ccin_d[:, I:I + 1], in_=dtot[:, :],
                                             allow_slow_non_contiguous=True),
                   s_ccin, deps=[dtot_t])
    es4s.close()

    plan.barrier()
    stage_end()

    def do_cc(e):
        e.collective_compute(
            "AllGather", ALU.bypass, replica_groups=[list(range(NCORES))],
            ins=[ccin_d.opt()], outs=[ccout_d.opt()]).then_inc(_SEMS["d_cc"])
        return None
    plan.op(GP, do_cc, deps=[(s_ccin, plan.dma_fills[s_ccin])])
    CC_TAG = ("d_cc", 1)

    es4b = ExitStack()
    lq_st = sb(es4b, "lq_st", [NH, I], F32)
    dtall = sb(es4b, "dtall", [NH, NCORES], F32)
    dmall = sb(es4b, "dmall", [NH, NCORES], F32)
    wfin = sb(es4b, "wfin", [NH, NCORES], F32)
    rmb = sb(es4b, "rmb", [NH, NCORES], F32)
    wbq = sb(es4b, "wbq", [NH, NH], F32)
    prevb = sb(es4b, "prevb", [NH, I], F32)
    prevbb = sb(es4b, "prevbb", [NH, I], BF16)
    stldb = sb(es4b, "stldb", [NH, I], BF16)

    pz = plan.op(V, lambda e: e.memset(prevb[:, :], 0.0), inc=True)
    dt_tags = []
    for q in range(NCORES):
        dt_tags.append(plan.dma(SY, (lambda e, q=q: e.dma_start(
            out=dtall[:, q:q + 1], in_=ccout_d[q * NH:(q + 1) * NH, I:I + 1],
            allow_slow_non_contiguous=True)),
            d_dt, deps=[CC_TAG]))
    dt_all = dt_tags[-1]
    bk = next_bank()
    mt = plan.op(TE, (lambda e, bk=bk: e.matmul(
        bk[0:NH, 0:NCORES], onesf[0:1, 0:NH], rmask[0:1, :], start=True, stop=True,
        skip_group_check=True)), deps=[bank_war.get(bk.name), m_ones,
                                       ("d_const", 16 * 11)], inc=True)
    rmb_t = plan.op(V, (lambda e, bk=bk: e.tensor_copy(rmb[:, :], bk[0:NH, 0:NCORES])),
                    deps=[mt], inc=True)
    bank_war[bk.name] = rmb_t
    q1 = plan.op(V, lambda e: e.tensor_scalar(dmall[:, :], dtall[:, :], -1.0, None,
                                              ALU.add), deps=[dt_all], inc=True)
    q2 = plan.op(V, lambda e: e.tensor_mul(dmall[:, :], dmall[:, :], rmb[:, :]),
                 deps=[q1, rmb_t], inc=True)
    q3 = plan.op(V, lambda e: e.tensor_scalar(dmall[:, :], dmall[:, :], 1.0, None,
                                              ALU.add), deps=[q2], inc=True)
    wt = plan.op(V, lambda e: e.memset(wfin[:, NCORES - 1:NCORES], 1.0), inc=True)
    for q in range(NCORES - 2, -1, -1):
        wt = plan.op(V, (lambda e, q=q: e.tensor_mul(
            wfin[:, q:q + 1], wfin[:, q + 1:q + 2], dmall[:, q + 1:q + 2])),
            deps=[wt, q3], inc=True)
    wt = plan.op(V, lambda e: e.tensor_mul(wfin[:, :], wfin[:, :], rmb[:, :]),
                 deps=[wt], inc=True)
    lq_war = None
    for q in range(NCORES):
        mt = plan.op(TE, (lambda e, q=q: e.transpose(
            ps_tpf[0:1, 0:NH], wfin[:, q:q + 1], idf[0:NH, 0:NH])),
            deps=[wt, tp_war[0]], inc=True)
        et = plan.op(V, (lambda e: e.tensor_copy(
            rowtmp[0:1, 0:NH], ps_tpf[0:1, 0:NH])), deps=[mt], inc=True)
        tp_war[0] = et
        bk = next_bank()
        m2 = plan.op(TE, (lambda e, bk=bk: e.matmul(
            bk[0:NH, 0:NH], onesf[0:1, 0:NH], rowtmp[0:1, 0:NH], start=True,
            stop=True, skip_group_check=True)),
            deps=[et, m_ones, bank_war.get(bk.name)], inc=True)
        wb_t = plan.op(V, (lambda e, bk=bk: e.tensor_copy(wbq[:, :], bk[0:NH, 0:NH])),
                       deps=[m2], inc=True)
        bank_war[bk.name] = wb_t
        lq_t = plan.dma(SY, (lambda e, q=q: e.dma_start(
            out=lq_st[:, :], in_=ccout_d[q * NH:(q + 1) * NH, 0:I])), d_lq,
            deps=[CC_TAG, lq_war])
        last = None
        for h in range(NH):
            last = plan.op(V, (lambda e, h=h: e.scalar_tensor_tensor(
                prevb[:, h * PD:(h + 1) * PD], lq_st[:, h * PD:(h + 1) * PD],
                wbq[:, h:h + 1], prevb[:, h * PD:(h + 1) * PD], ALU.mult, ALU.add)),
                deps=[lq_t, wb_t, pz], inc=True)
        lq_war = last

    plan.barrier()
    stage_end()

    es4c = ExitStack()
    rowbuf = [sb(es4c, f"rowbuf{i}", [1, 8 * CH], F32) for i in range(2)]
    dc_sb = [sb(es4c, f"dc{i}", [128, CH], F32) for i in range(4)]
    lb_sb = [sb(es4c, f"lb{i}", [128, CH], BF16) for i in range(4)]
    pb_sb = [sb(es4c, f"pb{i}", [128, CH], BF16) for i in range(4)]
    csd = [sb(es4c, f"csd{i}", [NS, CH], BF16) for i in range(2)]
    ysta = [sb(es4c, f"ysta{i}", [128, CH], BF16) for i in range(2)]

    prev_ready = plan.op(V, lambda e: e.tensor_copy(prevbb[:, :], prevb[:, :]),
                         inc=True)
    xs_war = None
    row_war = [None, None]
    dcrot = {"i": 0}
    yrot = {"i": 0}
    ysta_war = [None, None]
    dc_war = [None] * 4
    pb_war = [None] * 4
    csd_war = [None, None]
    offmm_last = None
    for c in range(NCHK):
        if c > 0:
            stl = plan.dma(SY, (lambda e, c=c: e.dma_start(
                out=stldb[:, :], in_=st_d[(c - 1) * NH:c * NH, :])), d_stl,
                deps=[(s_st, plan.dma_fills[s_st]), prev_ready])
            last = None
            for h in range(NH):
                last = plan.op(V, (lambda e, c=c, h=h: e.scalar_tensor_tensor(
                    prevb[:, h * PD:(h + 1) * PD], prevb[:, h * PD:(h + 1) * PD],
                    decb[:, (c - 1) * NH + h:(c - 1) * NH + h + 1],
                    stldb[:, h * PD:(h + 1) * PD], ALU.mult, ALU.add)),
                    deps=[stl, prev_ready], inc=True)
            prev_ready = plan.op(V, lambda e: e.tensor_copy(prevbb[:, :],
                                                            prevb[:, :]),
                                 deps=[last, offmm_last], inc=True)
        xl = load_xs_chunk(c, xs_war, d_xsc)
        xst_tags = build_xsT(xl)
        xs_war = xst_tags[-1]
        for hg in range(8):
            rb = hg % 2
            row_t = plan.dma(SY, (lambda e, c=c, hg=hg, rb=rb: e.dma_start(
                out=rowbuf[rb][0:1, :].rearrange("o (h t) -> o h t", h=8),
                in_=acum_d[hg * 8:(hg + 1) * 8, c * CH:(c + 1) * CH])),
                d_row[rb], deps=[(s_acum, 16), row_war[rb]])
            for hh in range(8):
                h = hg * 8 + hh
                j = h // 2
                odd = h % 2
                d1b = ps_mm[0] if h % 2 == 0 else ps_mm[1]
                d1t = plan.op(TE, (lambda e, d1b=d1b, rb=rb, hh=hh: e.matmul(
                    d1b[:, 0:CH], onesf[0:1, :], rowbuf[rb][0:1, hh * CH:(hh + 1) * CH],
                    start=True, stop=True, skip_group_check=True)),
                    deps=[row_t, m_ones, bank_war.get(d1b.name)], inc=True)
                dcl = []
                for s in range(2):
                    tt = 2 * c + s
                    r = dcrot["i"] % 4
                    dcrot["i"] += 1
                    t1 = plan.op(V, (lambda e, d1b=d1b, r=r, tt=tt, h=h:
                        e.tensor_scalar(
                            dc_sb[r][:, :], d1b[:, 0:CH],
                            alcol[:, tt * NH + h: tt * NH + h + 1],
                            lndtT[:, tt * NH + h: tt * NH + h + 1],
                            ALU.subtract, ALU.min)), deps=[d1t, dc_war[r]], inc=True)
                    t2 = plan.op(SC, (lambda e, r=r: e.activation(
                        lb_sb[r][:, :], dc_sb[r][:, :], AF.Exp)), deps=[t1], inc=True)
                    dc_war[r] = t2
                    t3 = plan.op(V, (lambda e, r=r, c=c, s=s: e.tensor_tensor(
                        pb_sb[r][:, :], lb_sb[r][:, :],
                        smask[:, (2 * c + s) * CH:(2 * c + s + 1) * CH], ALU.mult)),
                        deps=[t2, pb_war[r]], inc=True)
                    dcl.append((r, t3))
                cr = odd
                lvt = plan.op(SC, (lambda e, d1b=d1b, cr=cr: e.activation(
                    csd[cr][:, :], d1b[0:NS, 0:CH], AF.Exp)),
                    deps=[d1t, csd_war[cr]], inc=True)
                csd_t = plan.op(V, (lambda e, cr=cr, c=c: e.tensor_tensor(
                    csd[cr][:, :], csd[cr][:, :], cfm[:, c * CH:(c + 1) * CH],
                    ALU.mult)), deps=[lvt], inc=True)
                bank_war[d1b.name] = csd_t
                yb = ps_mm[2] if j % 2 == 0 else ps_mm[3]
                tp = (0, 64) if odd else None
                first = (odd == 0)
                out_sl = (lambda yb=yb, odd=odd: yb[64:128, 0:CH] if odd
                          else yb[0:64, 0:CH])()
                mms = []
                for s in range(2):
                    r, t3 = dcl[s]
                    mms.append(plan.op(TE, (lambda e, out_sl=out_sl, s=s, h=h, r=r,
                                            tp=tp: e.matmul(
                        out_sl, xsT[:, s * I + h * PD: s * I + (h + 1) * PD],
                        pb_sb[r][:, :], start=(s == 0), stop=False,
                        skip_group_check=True, tile_position=tp)),
                        deps=[t3] + ([bank_war.get(yb.name), xs_war] if s == 0 and first
                                     else []), inc=True))
                    pb_war[r] = mms[-1]
                mms.append(plan.op(TE, (lambda e, out_sl=out_sl, h=h, cr=cr, tp=tp:
                    e.matmul(
                        out_sl, prevbb[:, h * PD:(h + 1) * PD], csd[cr][:, :],
                        start=False, stop=True, skip_group_check=True,
                        tile_position=tp)),
                    deps=[csd_t, prev_ready], inc=True))
                csd_war[cr] = mms[-1]
                offmm_last = mms[-1]
                if odd == 1:
                    yr = yrot["i"] % 2
                    yrot["i"] += 1
                    et = plan.op(V, (lambda e, yb=yb, j=j, c=c, yr=yr:
                        e.scalar_tensor_tensor(
                            ysta[yr][:, :], xs_c[:, j * CH:(j + 1) * CH],
                            dexp[:, j:j + 1], yb[:, 0:CH], ALU.mult, ALU.add)),
                        deps=[mms[-1], ysta_war[yr], xl], inc=True)
                    ydma = plan.dma(SY, (lambda e, j=j, c=c, yr=yr: e.dma_start(
                        out=y_d[j * 128:(j + 1) * 128, c * CH:(c + 1) * CH],
                        in_=ysta[yr][:, :])), s_y[yr], deps=[et])
                    ysta_war[yr] = ydma
                    bank_war[yb.name] = et
            row_war[rb] = d1t

    plan.barrier()
    stage_end()
    es4c.close()
    es4b.close()
    es4.close()
    es2.close()

    es5 = ExitStack()
    ytl = [sb(es5, f"ytl{i}", [128, 512], BF16) for i in range(2)]
    sgt = [sb(es5, f"sgt{i}", [128, 512], BF16) for i in range(2)]
    yp = sb(es5, "yp", [128, 32 * 512], BF16)
    ysq = [sb(es5, f"ysq{i}", [128, 512], BF16) for i in range(2)]
    rstdb = sb(es5, "rstdb", [128, 512], F32)
    rtile = [sb(es5, f"rtile{i}", [128, 512], F32) for i in range(2)]
    x1bf = [sb(es5, f"x1bf{i}", [128, 512], BF16) for i in range(2)]
    sum2row = sb(es5, "sum2row", [1, T], F32)
    sqr2row = sb(es5, "sqr2row", [1, T], F32)

    opj_last = None
    prev_r2 = None
    carry_sq = [None, None]
    opj_rt_war = {}
    opj_sf_war = {}
    opj_bf_war = {}
    opj_sq_done = {}
    for tb in range(2):
        ysq_war = list(carry_sq)
        sq_mm_last = None
        yp_tags = []
        for j in range(32):
            b = j % 2
            ylt = plan.dma(SY, (lambda e, j=j, tb=tb, b=b: e.dma_start(
                out=ytl[b][:, :], in_=y_d[j * 128:(j + 1) * 128,
                                          tb * 512:(tb + 1) * 512])),
                d_yt[b], deps=allw(s_y) + [yp_tags[-2] if j >= 2 else None])
            sgl = plan.dma(SY, (lambda e, j=j, tb=tb, b=b: e.dma_start(
                out=sgt[b][:, :], in_=sg_d[j * 128:(j + 1) * 128,
                                           tb * 512:(tb + 1) * 512])),
                d_sgt[b], deps=allw(s_stg) + [yp_tags[-2] if j >= 2 else None])
            ypt = plan.op(V, (lambda e, j=j, b=b: e.tensor_tensor(
                yp[:, j * 512:(j + 1) * 512], ytl[b][:, :], sgt[b][:, :], ALU.mult)),
                deps=[ylt, sgl, opj_last, prev_r2], inc=True)
            yp_tags.append(ypt)
            sqt = plan.op(SC, (lambda e, j=j, b=b: e.activation(
                ysq[b][:, :], yp[:, j * 512:(j + 1) * 512], AF.Square)),
                deps=[ypt, ysq_war[b]], inc=True)
            sq_mm_last = plan.op(TE, (lambda e, j=j, b=b: e.matmul(
                ps_row[0:1, :], onesbc[:, 0:1], ysq[b][:, :],
                start=(j == 0), stop=(j == 31), skip_group_check=True)),
                deps=[sqt, m_ones] + ([bank_war.get(ps_row.name)] if j == 0 else []),
                inc=True)
            ysq_war[b] = sq_mm_last
        r0 = plan.op(V, (lambda e, tb=tb: e.tensor_scalar(
            sum2row[0:1, tb * 512:(tb + 1) * 512], ps_row[0:1, :], 1.0 / I, EPS,
            ALU.mult, ALU.add)), deps=[sq_mm_last], inc=True)
        r1 = plan.op(SC, (lambda e, tb=tb: e.activation(
            sum2row[0:1, tb * 512:(tb + 1) * 512],
            sum2row[0:1, tb * 512:(tb + 1) * 512], AF.Ln)), deps=[r0], inc=True)
        r1b = plan.op(V, (lambda e, tb=tb: e.tensor_scalar(
            sum2row[0:1, tb * 512:(tb + 1) * 512],
            sum2row[0:1, tb * 512:(tb + 1) * 512], -0.5, None, ALU.mult)),
            deps=[r1], inc=True)
        r2 = plan.op(SC, (lambda e, tb=tb: e.activation(
            sum2row[0:1, tb * 512:(tb + 1) * 512],
            sum2row[0:1, tb * 512:(tb + 1) * 512], AF.Exp)), deps=[r1b], inc=True)
        bank_war[ps_row.name] = r0
        bk = next_bank()
        m3 = plan.op(TE, (lambda e, bk=bk, tb=tb: e.matmul(
            bk[:, :], onesf[0:1, :], sum2row[0:1, tb * 512:(tb + 1) * 512],
            start=True, stop=True, skip_group_check=True)),
            deps=[r2, m_ones, bank_war.get(bk.name)], inc=True)
        rb_t = plan.op(V, (lambda e, bk=bk: e.tensor_copy(rstdb[:, :], bk[:, :])),
                       deps=[m3], inc=True)
        bank_war[bk.name] = rb_t
        ypp_tags = []
        for j in range(32):
            ypp_tags.append(plan.op(V, (lambda e, j=j: e.scalar_tensor_tensor(
                yp[:, j * 512:(j + 1) * 512], yp[:, j * 512:(j + 1) * 512],
                nw[:, j:j + 1], rstdb[:, :], ALU.mult, ALU.mult)),
                deps=[yp_tags[j], rb_t], inc=True))
        ypp_all = ypp_tags[-1]

        def rhs_yp(kb, t0, tw):
            return yp[:, kb * 512: kb * 512 + 512]

        x1_evt = {"n": 0}

        def evac_outproj(meta, o_abs, t0, tw, bk2, mt2):
            tb_ = meta["tb"]
            b = x1_evt["n"] % 2
            x1_evt["n"] += 1
            rt = plan.dma(SY, (lambda e, o=o_abs, tb_=tb_, b=b: e.dma_start(
                out=rtile[b][:, :], in_=r_d[o:o + 128, tb_ * 512:(tb_ + 1) * 512])),
                d_rt[b], deps=allw(s_t2f) + [meta["rt_war"].get(b)])
            at = plan.op(V, (lambda e, bk2=bk2, b=b: e.tensor_add(
                stagef[b][:, :], bk2[:, :], rtile[b][:, :])),
                deps=[mt2, rt, meta["sf_war"].get(b), meta["sq_done"].get(b)],
                inc=True)
            meta["rt_war"][b] = at
            xd = plan.dma(SY, (lambda e, o=o_abs, tb_=tb_, b=b: e.dma_start(
                out=x1_d[o:o + 128, tb_ * 512:(tb_ + 1) * 512],
                in_=stagef[b][:, :])), s_sf[b], deps=[at])
            cst = plan.op(V, (lambda e, b=b: e.tensor_copy(
                x1bf[b][:, :], stagef[b][:, :])), deps=[at, meta["bf_war"].get(b)],
                inc=True)
            sq2 = plan.op(SC, (lambda e, b=b: e.activation(
                ysq[b][:, :], stagef[b][:, :], AF.Square)),
                deps=[at, meta["sq_war"].get(b)], inc=True)
            ki = meta["ki"]["n"]
            meta["ki"]["n"] += 1
            mm1 = plan.op(TE, (lambda e, b=b, ki=ki: e.matmul(
                ps_row[0:1, :], onesbc[:, 0:1], x1bf[b][:, :], start=(ki == 0),
                stop=(ki == 15), skip_group_check=True)),
                deps=[cst, m_ones] + ([bank_war.get(ps_row.name)] if ki == 0 else []),
                inc=True)
            mm2 = plan.op(TE, (lambda e, b=b, ki=ki: e.matmul(
                ps_e[0:1, :], onesbc[:, 0:1], ysq[b][:, :], start=(ki == 0),
                stop=(ki == 15), skip_group_check=True)),
                deps=[sq2, m_ones] + ([bank_war.get(ps_e.name)] if ki == 0 else []),
                inc=True)
            meta["bf_war"][b] = mm1
            meta["sq_war"][b] = mm2
            meta["sq_done"][b] = sq2
            meta["sf_war"][b] = xd
            meta["last_mm"] = (mm1, mm2, xd)
            return at

        meta2 = {"tb": tb, "tgrps": [(tb * 512, 512)], "rt_war": opj_rt_war,
                 "sf_war": opj_sf_war, "bf_war": opj_bf_war,
                 "sq_war": {0: sq_mm_last, 1: sq_mm_last},
                 "sq_done": opj_sq_done, "ki": {"n": 0}, "last_mm": None}
        slabs2 = [(o, 256, meta2) for o in range(0, H, 256)]
        opj_last = mm_stage(w2t, 32, slabs2, rhs_yp, evac_outproj,
                            extra_dep=ypp_all, slab_offs=W_OFFS['w2t'])
        prev_r2 = r2
        carry_sq = [meta2["sq_war"].get(0), meta2["sq_war"].get(1)]
        mm1, mm2, _ = meta2["last_mm"]
        ev1 = plan.op(V, (lambda e, tb=tb: e.tensor_copy(
            sum2row[0:1, tb * 512:(tb + 1) * 512], ps_row[0:1, :])),
            deps=[mm1], inc=True)
        ev2 = plan.op(V, (lambda e, tb=tb: e.tensor_copy(
            sqr2row[0:1, tb * 512:(tb + 1) * 512], ps_e[0:1, :])),
            deps=[mm2], inc=True)
        bank_war[ps_row.name] = ev1
        bank_war[ps_e.name] = ev2

    plan.barrier()
    stage_end()
    es5.close()

    es6 = ExitStack()
    m2row = sb(es6, "m2row", [1, T], F32)
    var2 = sb(es6, "var2", [1, T], F32)
    a2row = sb(es6, "a2row", [1, T], F32)
    mb2 = sb(es6, "mb2", [128, T], F32)
    ab2 = sb(es6, "ab2", [128, T], F32)
    x1t = [sb(es6, f"x1t{i}", [128, T], F32) for i in range(2)]
    jt = sb(es6, "jt", [128, T], F32)
    xb2 = sb(es6, "xb2", [128, KT * T], BF16)

    n1 = plan.op(V, lambda e: e.tensor_scalar(m2row[:, :], sum2row[:, :], 1.0 / H,
                                              None, ALU.mult), inc=True)
    n2 = plan.op(V, lambda e: e.tensor_mul(var2[:, :], m2row[:, :], m2row[:, :]),
                 deps=[n1], inc=True)
    n3 = plan.op(V, lambda e: e.scalar_tensor_tensor(
        var2[:, :], sqr2row[:, :], 1.0 / H, var2[:, :], ALU.mult, ALU.subtract),
        deps=[n2], inc=True)
    n3b = plan.op(V, lambda e: e.tensor_scalar(a2row[:, :], var2[:, :], 1.0, EPS,
                                               ALU.mult, ALU.add), deps=[n3],
                  inc=True)
    n4 = plan.op(SC, lambda e: e.activation(a2row[:, :], a2row[:, :], AF.Ln),
                 deps=[n3b], inc=True)
    n4b = plan.op(V, lambda e: e.tensor_scalar(a2row[:, :], a2row[:, :], -0.5, None,
                                               ALU.mult), deps=[n4], inc=True)
    n5 = plan.op(SC, lambda e: e.activation(a2row[:, :], a2row[:, :], AF.Exp),
                 deps=[n4b], inc=True)

    def bcast_row_t(src_row, dst_tile, dep):
        tags = []
        for t0 in (0, 512):
            bk = next_bank()
            mt = plan.op(TE, (lambda e, bk=bk, t0=t0, s=src_row: e.matmul(
                bk[:, :], onesf[0:1, :], s[0:1, t0:t0 + 512], start=True, stop=True,
                skip_group_check=True)), deps=[dep, m_ones, bank_war.get(bk.name)],
                inc=True)
            et = plan.op(V, (lambda e, bk=bk, t0=t0, d=dst_tile: e.tensor_copy(
                d[:, t0:t0 + 512], bk[:, :])), deps=[mt], inc=True)
            bank_war[bk.name] = et
            tags.append(et)
        return tags

    mb2_t = bcast_row_t(m2row, mb2, n1)
    ab2_t = bcast_row_t(a2row, ab2, n5)
    x1_war = [None, None]
    j2 = None
    for kt in range(KT):
        b = kt % 2
        xlt = plan.dma(SY, (lambda e, kt=kt, b=b: e.dma_start(
            out=x1t[b][:, :], in_=x1_d[kt * 128:(kt + 1) * 128, :])), d_x1t[b],
            deps=allw(s_sf) + [x1_war[b]])
        j1 = plan.op(V, (lambda e, b=b: e.tensor_sub(jt[:, :], x1t[b][:, :],
                                                     mb2[:, :])),
                     deps=[xlt, j2] + mb2_t, inc=True)
        j2 = plan.op(V, (lambda e, kt=kt: e.tensor_mul(
            xb2[:, kt * T:(kt + 1) * T], jt[:, :], ab2[:, :])),
            deps=[j1] + ab2_t, inc=True)
        x1_war[b] = j2

    plan.barrier()
    stage_end()

    es7 = ExitStack()
    sgf = sb(es7, "sgf", [128, 8 * 512], BF16)

    def rhs_xb2(kb, t0, tw):
        return xb2[:, kb * T + t0: kb * T + t0 + tw]

    sgf_war = {}

    def evac_ffn1(meta, o_abs, t0, tw, bk, mt):
        if meta["kind"] == "fg":
            gi = (o_abs % 512) // 128 * 2 + (t0 // 512)
            et = plan.op(SC, (lambda e, gi=gi, bk=bk: e.activation(
                sgf[:, gi * 512:(gi + 1) * 512], bk[:, :], AF.Silu)),
                deps=[mt, sgf_war.get(gi)], inc=True)
            sgf_war[gi] = et
            meta.setdefault("sg_tags", {})[gi] = et
            return et
        else:
            gi = (o_abs % 512) // 128 * 2 + (t0 // 512)
            gtag = meta["g_meta"]["sg_tags"][gi]
            s = next_stage()
            et = plan.op(V, (lambda e, gi=gi, bk=bk, s=s: e.tensor_tensor(
                stage[s][:, :], bk[:, :], sgf[:, gi * 512:(gi + 1) * 512], ALU.mult)),
                deps=[mt, gtag, stg_war[s]], inc=True)
            kb_ = (o_abs - INTER) // 128
            tb_ = t0 // 512
            dmat = plan.dma(SY, (lambda e, kb_=kb_, tb_=tb_, s=s: e.dma_start(
                out=h_d[:, tb_ * 32768 + kb_ * 512: tb_ * 32768 + (kb_ + 1) * 512],
                in_=stage[s][:, :])),
                s_stg[s], deps=[et])
            stg_war[s] = dmat
            sgf_war[gi] = et
            return et

    TG_T = [(0, 512), (512, 512)]
    slabs3 = []
    for gi in range(INTER // 512):
        mg = {"kind": "fg", "tgrps": TG_T}
        mu = {"kind": "fu", "tgrps": TG_T, "g_meta": mg}
        slabs3.append((gi * 512, 512, mg))
        slabs3.append((INTER + gi * 512, 512, mu))
    mm_stage(w3t, KT, slabs3, rhs_xb2, evac_ffn1,
             slab_offs=W_OFFS['w3t'])

    plan.barrier()
    stage_end()
    es7.close()

    es8 = ExitStack()
    hblk = sb(es8, "hblk", [128, 64 * 512], BF16)
    rtile2 = [sb(es8, f"rtile2_{i}", [128, 512], F32) for i in range(2)]
    KB2 = INTER // 128
    ffn2_last = None
    x1w = {}
    for tb in range(2):
        hl = plan.dma(SY, (lambda e, tb=tb: e.dma_start(
            out=hblk[:, :], in_=h_d[:, tb * 32768:(tb + 1) * 32768])),
            d_hblk, deps=allw(s_stg) + [ffn2_last])

        def rhs_h(kb, t0, tw):
            return hblk[:, kb * 512: kb * 512 + 512]

        def evac_ffn2(meta, o_abs, t0, tw, bk, mt, tb=tb):
            b = (o_abs // 128) % 2
            xlt = plan.dma(SY, (lambda e, o=o_abs, tb=tb, b=b: e.dma_start(
                out=rtile2[b][:, :], in_=x1_d[o:o + 128, tb * 512:(tb + 1) * 512])),
                d_x1t[b], deps=allw(s_sf) + [x1w.get(b)])
            at = plan.op(V, (lambda e, bk=bk, b=b: e.tensor_add(
                stagef[b][:, :], bk[:, :], rtile2[b][:, :])),
                deps=[mt, xlt, x1w.get(("sf", b))], inc=True)
            od = plan.dma(SY, (lambda e, o=o_abs, tb=tb, b=b: e.dma_start(
                out=out_hbm[o:o + 128, tb * 512:(tb + 1) * 512],
                in_=stagef[b][:, :])), s_sf[b], deps=[at])
            x1w[b] = at
            x1w[("sf", b)] = od
            return at

        meta4 = {"tgrps": [(tb * 512, 512)]}
        slabs4 = [(o, 128, meta4) for o in range(0, H, 128)]
        ffn2_last = mm_stage(w4t, KB2, slabs4, rhs_h, evac_ffn2,
                             extra_dep=hl, slab_offs=W_OFFS['w4t'])

    plan.barrier()
    stage_end()

    sems = {}
    sem_stack = ExitStack()
    for name in plan.sem_names:
        sems[name] = sem_stack.enter_context(nc.semaphore(name))
    global _SEMS
    _SEMS = sems

    with nc.Block() as block:
        def runner(engname):
            def run(e):
                for item in plan.ops[engname]:
                    if item[0] == "wait":
                        e.wait_ge(sems[item[1]], item[2])
                    else:
                        inst = item[1](e)
                        tag = item[2]
                        if tag is not None and inst is not None:
                            inst.then_inc(sems[tag[0]],
                                          16 if item[0] == "dma" else 1)
            return run
        block.tensor(runner(TE))
        block.vector(runner(V))
        block.scalar(runner(SC))
        block.gpsimd(runner(GP))
        block.sync(runner(SY))

    sem_stack.close()
    es8.close()
    es6.close()
    es0.close()
    return nc


_SEMS = None


def _tile_w(WT, slabs, kb_n):
    parts = []
    for (o0, w) in slabs:
        blk = WT[:, o0:o0 + w].reshape(kb_n, 128, w).transpose(1, 0, 2)
        parts.append(blk.reshape(128, kb_n * w))
    return np.ascontiguousarray(np.concatenate(parts, axis=1))


def _host_inputs(full):
    x = np.asarray(full["L0_attn_hidden_states"], np.float32)[0]
    xT = np.ascontiguousarray(x.T)
    w1t = _tile_w(np.asarray(full["L0_attn_in_proj_w"], np.float32).T,
                  SLABS_W1, KT)
    w2t = _tile_w(np.asarray(full["L0_attn_out_proj_w"], np.float32).T,
                  SLABS_W2, 32)
    w3t = _tile_w(np.asarray(full["L0_ffn_w_gate_up"], np.float32).T,
                  SLABS_W3, KT)
    w4t = _tile_w(np.asarray(full["L0_ffn_w_down"], np.float32).T,
                  SLABS_W4, INTER // 128)
    lnw = np.ascontiguousarray(
        np.asarray(full["L0_attn_ln_weight"], np.float32).reshape(KT, 128).T)
    nw = np.ascontiguousarray(
        np.asarray(full["L0_attn_norm_weight"], np.float32).reshape(I // 128, 128).T)
    dexp = np.ascontiguousarray(
        np.repeat(np.asarray(full["L0_attn_D"], np.float32), PD)
        .reshape(I // 128, 128).T)
    cwf = np.asarray(full["L0_attn_conv_w"], np.float32)
    cw = np.ascontiguousarray(
        cwf.reshape(33, 128, 4).transpose(1, 0, 2).reshape(128, 33 * 4))
    cb = np.ascontiguousarray(
        np.asarray(full["L0_attn_conv_b"], np.float32).reshape(33, 128).T)
    dtb = np.asarray(full["L0_attn_dt_bias"], np.float32).reshape(NH, 1)
    alog = np.asarray(full["L0_attn_A_log"], np.float32).reshape(NH, 1)
    import ml_dtypes
    sidx = np.arange(128)[:, None]
    lidx = np.arange(CH)[None, :]
    tri0 = (sidx <= lidx).astype(ml_dtypes.bfloat16)
    tri1 = ((sidx + 128) <= lidx).astype(ml_dtypes.bfloat16)
    tri = np.ascontiguousarray(np.concatenate([tri0, tri1], axis=1))
    identf = np.eye(128, dtype=np.float32)
    identb = np.eye(128, dtype=ml_dtypes.bfloat16)

    shared = dict(w1t=w1t, w2t=w2t, w3t=w3t, w4t=w4t, lnw=lnw, nw=nw, dexp=dexp,
                  cw=cw, cb=cb, dtb=dtb, alog=alog, tri=tri, identf=identf,
                  identb=identb)
    in_maps = []
    for r in range(NCORES):
        lo = r * T - 3
        if lo < 0:
            xs = np.concatenate(
                [np.zeros((H, 3), np.float32), xT[:, 0:T]], axis=1)
        else:
            xs = xT[:, lo:lo + TH]
        m = dict(shared)
        m["xt"] = np.ascontiguousarray(
            xs.reshape(KT, 128, TH).transpose(1, 0, 2).reshape(128, KT * TH))
        m["rmask"] = (np.arange(NCORES) < r).astype(np.float32).reshape(1, NCORES)
        in_maps.append(m)
    return in_maps


def kernel(**inputs):
    in_maps = _host_inputs(inputs)
    nc = build_nc()
    res = run_bass_kernel_spmd(nc, in_maps, core_ids=list(range(NCORES)))
    full = np.concatenate([np.asarray(res.results[r]["out"]) for r in range(NCORES)],
                          axis=1)
    return np.ascontiguousarray(full.T).reshape(1, 8192, H).astype(np.float32)


# version 05
# speedup vs baseline: 1.2033x; 1.0360x over previous
import numpy as np
import concourse.bass as bass
import concourse.mybir as mybir
from concourse.bass_utils import run_bass_kernel_spmd

F32 = mybir.dt.float32
BF16 = mybir.dt.bfloat16
AF = mybir.ActivationFunctionType
ALU = mybir.AluOpType

NCORES = 8
H = 2048
I = 4096
NH = 64
PD = 64
NS = 64
CH = 256
CONV = 4224
PROJ = 8384
T = 1024
TH = T + 3
NCHK = T // CH
KT = H // 128
EPS = 1e-5
INTER = 8192

ENGINES = ("tensor", "vector", "scalar", "gpsimd", "sync")
V, SC, TE, GP, SY = "vector", "scalar", "tensor", "gpsimd", "sync"

SLABS_W1 = ([(o, 512) for o in range(0, I, 512)]
            + [(o, 512) for o in range(I, 8192, 512)]
            + [(8192, 128), (8320, 64)])
SLABS_W2 = [(o, 256) for o in range(0, H, 256)]
SLABS_W3 = [x for gi in range(INTER // 512)
            for x in ((gi * 512, 512), (INTER + gi * 512, 512))]
SLABS_W4 = [(o, 128) for o in range(0, H, 128)]


def _slab_offsets(slabs, kb_n):
    offs = []
    t = 0
    for (o0, w) in slabs:
        offs.append(t)
        t += kb_n * w
    return offs, t


class Plan:
    def __init__(self):
        self.ops = {e: [] for e in ENGINES}
        self.cnt = {e: 0 for e in ENGINES}
        self.waited = {e: {} for e in ENGINES}
        self.dma_fills = {}
        self.sem_names = (["m_" + e for e in ENGINES]
                          + ["b_" + e for e in ENGINES])
        self.bcnt = {e: 0 for e in ENGINES}
        self.stopped = False

    def new_dma_sem(self, name):
        assert name not in self.dma_fills
        self.dma_fills[name] = 0
        self.sem_names.append(name)
        return name

    def _wait(self, eng, sem, val):
        if self.waited[eng].get(sem, 0) >= val:
            return
        self.waited[eng][sem] = val
        self.ops[eng].append(("wait", sem, val))

    def op(self, eng, fn, deps=(), inc=False):
        if self.stopped:
            return None
        for d in deps:
            if d is not None:
                self._wait(eng, d[0], d[1])
        tag = None
        if inc:
            self.cnt[eng] += 1
            tag = ("m_" + eng, self.cnt[eng])
        self.ops[eng].append(("op", fn, tag))
        return tag

    def dma(self, queue, fn, sem, deps=()):
        if self.stopped:
            return None
        for d in deps:
            if d is not None:
                self._wait(queue, d[0], d[1])
        self.dma_fills[sem] += 16
        tag = (sem, self.dma_fills[sem])
        self.ops[queue].append(("dma", fn, tag))
        return tag

    def barrier(self):
        if self.stopped:
            return
        marks = []
        for e in ENGINES:
            if e == SY:
                continue
            self.bcnt[e] += 1
            tag = ("b_" + e, self.bcnt[e])
            self.ops[e].append(("op", (lambda eng: eng.nop()), tag))
            marks.append(tag)
            if self.cnt[e] > 0:
                marks.append(("m_" + e, self.cnt[e]))
        for s, vv in self.dma_fills.items():
            if vv > 0:
                marks.append((s, vv))
        for e in ENGINES:
            for m in marks:
                self._wait(e, m[0], m[1])


def build_nc(stop_after=None):
    from contextlib import ExitStack
    nc = bass.Bass("TRN2", target_bir_lowering=False, debug=False)

    def din(name, shape, dt=F32):
        return nc.dram_tensor(name, shape, dt, kind="ExternalInput").ap()

    xt = din("xt", [128, KT * TH])
    _o1, _t1 = _slab_offsets(SLABS_W1, KT)
    _o2, _t2 = _slab_offsets(SLABS_W2, 32)
    _o3, _t3 = _slab_offsets(SLABS_W3, KT)
    _o4, _t4 = _slab_offsets(SLABS_W4, INTER // 128)
    w1t = din("w1t", [128, _t1])
    w2t = din("w2t", [128, _t2])
    w3t = din("w3t", [128, _t3])
    w4t = din("w4t", [128, _t4])
    W_OFFS = {"w1t": _o1, "w2t": _o2, "w3t": _o3, "w4t": _o4}
    lnw_c = din("lnw", [128, KT])
    nw_c = din("nw", [128, I // 128])
    dexp_c = din("dexp", [128, I // 128])
    cw_c = din("cw", [128, 33 * 4])
    cb_c = din("cb", [128, 33])
    dtb_c = din("dtb", [NH, 1])
    alog_c = din("alog", [NH, 1])
    tri_c = din("tri", [128, 2 * CH], BF16)
    identf_c = din("identf", [128, 128])
    identb_c = din("identb", [128, 128], BF16)
    rmask_c = din("rmask", [1, NCORES])

    out_hbm = nc.dram_tensor("out", [H, T], F32, kind="ExternalOutput").ap()

    def dint(name, shape, dt):
        return nc.dram_tensor(name, shape, dt, kind="Internal").ap()

    xbc_d = dint("xbc_d", [CONV, TH], BF16)
    sg_d = dint("sg_d", [I, T], BF16)
    xs_d = dint("xs_d", [128, NCHK * 33 * CH], BF16)
    r_d = dint("r_d", [H, T], F32)
    acum_d = dint("acum_dd", [NH, T], F32)
    st_d = dint("st_d", [NCHK * NH, I], BF16)
    ccin_d = dint("ccin_d", [NH, I + 1], F32)
    ccout_d = dint("ccout_d", [NCORES * NH, I + 1], F32)
    y_d = dint("y_d", [I, T], BF16)
    x1_d = dint("x1_d", [H, T], F32)
    h_d = dint("h_d", [128, 2 * 64 * 512], BF16)

    es0 = ExitStack()

    def sb(stack, name, shape, dt):
        return stack.enter_context(nc.sbuf_tensor(name, shape, dt))

    onesf = sb(es0, "onesf", [1, 128], F32)
    onescf = sb(es0, "onescf", [128, 1], F32)
    onesbc = sb(es0, "onesbc", [128, 1], BF16)
    lnw = sb(es0, "lnw_sb", [128, KT], F32)
    nw = sb(es0, "nw_sb", [128, I // 128], F32)
    dexp = sb(es0, "dexp_sb", [128, I // 128], F32)
    cw = sb(es0, "cw_sb", [128, 33 * 4], F32)
    cb = sb(es0, "cb_sb", [128, 33], F32)
    dtb = sb(es0, "dtb_sb", [NH, 1], F32)
    alog = sb(es0, "alog_sb", [NH, 1], F32)
    tri = sb(es0, "tri_sb", [128, 2 * CH], BF16)
    idf = sb(es0, "idf_sb", [128, 128], F32)
    idb = sb(es0, "idb_sb", [128, 128], BF16)
    rmask = sb(es0, "rmask_sb", [1, NCORES], F32)
    SLABW = 8192
    slab = [sb(es0, f"slab{i}", [128, SLABW], BF16) for i in range(2)]
    stage = [sb(es0, f"stage{i}", [128, 512], BF16) for i in range(4)]
    stagef = [sb(es0, f"stagef{i}", [128, 512], F32) for i in range(2)]
    dt_sb = sb(es0, "dt_sb", [NH, T], F32)
    rowtmp = sb(es0, "rowtmp", [1, 128], F32)

    ps_mm = [es0.enter_context(nc.psum_tensor(f"ps_mm{i}", [128, 512], F32))
             for i in range(4)]
    ps_e = es0.enter_context(nc.psum_tensor("ps_e", [128, 512], F32))
    ps_tpb = es0.enter_context(nc.psum_tensor("ps_tpb", [128, 1024], BF16))
    ps_tpf = es0.enter_context(nc.psum_tensor("ps_tpf", [128, 512], F32))
    ps_row = es0.enter_context(nc.psum_tensor("ps_row", [1, 512], F32))

    plan = Plan()
    _STOP = {'n': 0}

    def stage_end():
        _STOP['n'] += 1
        if stop_after is not None and _STOP['n'] >= stop_after:
            plan.stopped = True
    nds = plan.new_dma_sem
    d_const = nds("d_const"); d_xf = nds("d_xf")
    d_slab = [nds("d_slab0"), nds("d_slab1")]
    s_stg = [nds(f"s_stg{i}") for i in range(4)]
    s_sf = [nds(f"s_sf{i}") for i in range(2)]
    s_t2f = [nds(f"s_t2f{i}") for i in range(2)]
    s_acum = nds("s_acum"); s_st = nds("s_st")
    s_ccin = nds("s_ccin"); d_cc = nds("d_cc")
    d_xbct = [nds("d_xbct0"), nds("d_xbct1")]
    d_xsc = nds("d_xsc")
    d_abc = [nds(f"d_abc{i}") for i in range(4)]
    d_lq = nds("d_lq"); d_dt = nds("d_dt"); d_stl = nds("d_stl")
    s_y = [nds("s_y0"), nds("s_y1")]
    d_yt = [nds("d_yt0"), nds("d_yt1")]
    d_sgt = [nds("d_sgt0"), nds("d_sgt1")]
    d_x1t = [nds("d_x1t0"), nds("d_x1t1")]
    d_rt = [nds("d_rt0"), nds("d_rt1")]
    d_hblk = nds("d_hblk")

    def allw(sems_):
        return [(s, plan.dma_fills[s]) for s in sems_ if plan.dma_fills[s] > 0]

    MMB = {"i": 0}

    def next_bank():
        b = ps_mm[MMB["i"] % 4]
        MMB["i"] += 1
        return b

    bank_war = {}

    for dst, src in [(lnw, lnw_c), (nw, nw_c), (dexp, dexp_c), (cw, cw_c),
                     (cb, cb_c), (dtb, dtb_c), (alog, alog_c), (tri, tri_c),
                     (idf, identf_c), (idb, identb_c), (rmask, rmask_c)]:
        plan.dma(SY, (lambda e, d=dst, s=src: e.dma_start(out=d[:, :], in_=s[:, :])),
                 d_const)
    plan.op(V, lambda e: e.memset(onesf[:, :], 1.0), inc=True)
    plan.op(V, lambda e: e.memset(onescf[:, :], 1.0), inc=True)
    m_ones = plan.op(V, lambda e: e.memset(onesbc[:, :], 1.0), inc=True)

    es1 = ExitStack()
    xfull = sb(es1, "xfull", [128, KT * TH], F32)
    xsq = [sb(es1, f"xsq{i}", [128, TH], F32) for i in range(2)]
    srow = sb(es1, "srow", [1, TH], F32)
    sqrow = sb(es1, "sqrow", [1, TH], F32)
    mrow = sb(es1, "mrow", [1, TH], F32)
    varrow = sb(es1, "varrow", [1, TH], F32)
    arow = sb(es1, "arow", [1, TH], F32)
    aorow = sb(es1, "aorow", [1, TH], F32)
    mb = sb(es1, "mb", [128, TH], F32)
    ab = sb(es1, "ab", [128, TH], F32)
    aob = sb(es1, "aob", [128, TH], F32)
    t1f = sb(es1, "t1f", [128, TH], F32)
    t2f = [sb(es1, f"t2f{i}", [128, TH], F32) for i in range(2)]
    xb = sb(es1, "xb", [128, KT * TH], BF16)

    t_x = plan.dma(SY, lambda e: e.dma_start(out=xfull[:, :], in_=xt[:, :]), d_xf)

    TG = [(0, 512), (512, 512), (1024, 3)]
    sum_dst = [ps_row, ps_mm[0], ps_mm[1]]
    sq_dst = [ps_e, ps_mm[2], ps_mm[3]]
    sqmm_tag = {}
    last_mm = None
    for kt in range(KT):
        b = kt % 2
        sq_t = plan.op(
            SC, (lambda e, b=b, kt=kt: e.activation(
                xsq[b][:, :], xfull[:, kt * TH:(kt + 1) * TH], AF.Square)),
            deps=[t_x, sqmm_tag.get(kt - 2)], inc=True)
        for gi, (t0, tw) in enumerate(TG):
            plan.op(TE, (lambda e, kt=kt, gi=gi, t0=t0, tw=tw: e.matmul(
                sum_dst[gi][0:1, 0:tw], onescf[:, 0:1],
                xfull[:, kt * TH + t0: kt * TH + t0 + tw],
                start=(kt == 0), stop=(kt == KT - 1), skip_group_check=True)),
                deps=[t_x, m_ones])
            last_mm = plan.op(TE, (lambda e, b=b, gi=gi, t0=t0, tw=tw, kt=kt:
                e.matmul(
                    sq_dst[gi][0:1, 0:tw], onescf[:, 0:1], xsq[b][:, t0:t0 + tw],
                    start=(kt == 0), stop=(kt == KT - 1), skip_group_check=True)),
                deps=[sq_t, m_ones], inc=True)
        sqmm_tag[kt] = last_mm
    ev = []
    for gi, (t0, tw) in enumerate(TG):
        ev.append(plan.op(V, (lambda e, gi=gi, t0=t0, tw=tw: e.tensor_copy(
            srow[0:1, t0:t0 + tw], sum_dst[gi][0:1, 0:tw])), deps=[last_mm], inc=True))
        ev.append(plan.op(V, (lambda e, gi=gi, t0=t0, tw=tw: e.tensor_copy(
            sqrow[0:1, t0:t0 + tw], sq_dst[gi][0:1, 0:tw])), deps=[last_mm], inc=True))
    m1 = plan.op(V, lambda e: e.tensor_scalar(mrow[:, :], srow[:, :], 1.0 / H, None,
                                              ALU.mult), deps=ev, inc=True)
    m2t = plan.op(V, lambda e: e.tensor_mul(varrow[:, :], mrow[:, :], mrow[:, :]),
                  deps=[m1], inc=True)
    m3 = plan.op(V, lambda e: e.scalar_tensor_tensor(
        varrow[:, :], sqrow[:, :], 1.0 / H, varrow[:, :], ALU.mult, ALU.subtract),
        deps=[m2t], inc=True)
    a0 = plan.op(V, lambda e: e.tensor_scalar(arow[:, :], varrow[:, :], 1.0 + EPS,
                                              EPS * EPS, ALU.mult, ALU.add),
                 deps=[m3], inc=True)
    a1 = plan.op(SC, lambda e: e.activation(arow[:, :], arow[:, :], AF.Ln),
                 deps=[a0], inc=True)
    a1b = plan.op(V, lambda e: e.tensor_scalar(arow[:, :], arow[:, :], -0.5, None,
                                               ALU.mult), deps=[a1], inc=True)
    a2 = plan.op(SC, lambda e: e.activation(arow[:, :], arow[:, :], AF.Exp),
                 deps=[a1b], inc=True)
    a2b = plan.op(V, lambda e: e.tensor_scalar(aorow[:, :], varrow[:, :], 1.0, EPS,
                                               ALU.mult, ALU.add), deps=[m3], inc=True)
    a3 = plan.op(SC, lambda e: e.activation(aorow[:, :], aorow[:, :], AF.Ln),
                 deps=[a2b], inc=True)
    a3b = plan.op(V, lambda e: e.tensor_scalar(aorow[:, :], aorow[:, :], -0.5, None,
                                               ALU.mult), deps=[a3], inc=True)
    a4 = plan.op(SC, lambda e: e.activation(aorow[:, :], aorow[:, :], AF.Exp),
                 deps=[a3b], inc=True)

    def bcast_row(src_row, dst_tile, dep):
        tags = []
        for (t0, tw) in TG:
            bk = next_bank()
            mt = plan.op(TE, (lambda e, bk=bk, t0=t0, tw=tw, s=src_row: e.matmul(
                bk[:, 0:tw], onesf[0:1, :], s[0:1, t0:t0 + tw],
                start=True, stop=True, skip_group_check=True)),
                deps=[dep, m_ones, bank_war.get(bk.name)], inc=True)
            et = plan.op(V, (lambda e, bk=bk, t0=t0, tw=tw, d=dst_tile: e.tensor_copy(
                d[:, t0:t0 + tw], bk[:, 0:tw])), deps=[mt], inc=True)
            bank_war[bk.name] = et
            tags.append(et)
        return tags

    mb_t = bcast_row(mrow, mb, m1)
    ab_t = bcast_row(arow, ab, a2)
    aob_t = bcast_row(aorow, aob, a4)

    r_tags = []
    prev_dma = {}
    for kt in range(KT):
        b = kt % 2
        tt1 = plan.op(V, (lambda e, kt=kt: e.tensor_sub(
            t1f[:, :], xfull[:, kt * TH:(kt + 1) * TH], mb[:, :])),
            deps=[t_x] + mb_t, inc=True)
        plan.op(V, (lambda e, kt=kt: e.scalar_tensor_tensor(
            xb[:, kt * TH:(kt + 1) * TH], ab[:, :], lnw[:, kt:kt + 1], t1f[:, :],
            ALU.mult, ALU.mult)), deps=[tt1] + ab_t + [("d_const", 16 * 11)], inc=True)
        tt2 = plan.op(V, (lambda e, kt=kt, b=b: e.tensor_mul(
            t2f[b][:, :], t1f[:, :], aob[:, :])),
            deps=[tt1, prev_dma.get(b)] + aob_t, inc=True)
        tt3 = plan.op(V, (lambda e, kt=kt, b=b: e.tensor_add(
            t2f[b][:, :], t2f[b][:, :], xfull[:, kt * TH:(kt + 1) * TH])),
            deps=[tt2], inc=True)
        prev_dma[b] = plan.dma(SY, (lambda e, kt=kt, b=b: e.dma_start(
            out=r_d[kt * 128:(kt + 1) * 128, :], in_=t2f[b][:, 3:TH])),
            s_t2f[b], deps=[tt3])
        r_tags.append(prev_dma[b])

    plan.barrier()
    stage_end()

    def mm_stage(w_hbm, kb_n, slabs, rhs_fn, evac_fn, extra_dep=None,
                 slab_offs=None):
        slab_last_use = [None, None]
        slab_tag = [None, None]
        for si, (o0, wdt, meta) in enumerate(slabs):
            sbuf_i = si % 2
            soff = slab_offs[si]
            nsub = wdt // 128 if wdt >= 128 else 1
            slab_tag[sbuf_i] = plan.dma(GP, (lambda e, soff=soff, wdt=wdt, i=sbuf_i:
                e.dma_start(
                    out=slab[i][:, 0:kb_n * wdt],
                    in_=w_hbm[:, soff:soff + kb_n * wdt])),
                d_slab[sbuf_i], deps=[slab_last_use[sbuf_i], extra_dep])
            tgrps = meta["tgrps"]
            for osub in range(nsub):
                ow = min(128, wdt)
                for (t0, tw) in tgrps:
                    bk = next_bank()
                    mt = None
                    for kb in range(kb_n):
                        mt = plan.op(TE, (lambda e, bk=bk, kb=kb, osub=osub, ow=ow,
                                          wdt=wdt, t0=t0, tw=tw, i=sbuf_i: e.matmul(
                            bk[0:ow, 0:tw],
                            slab[i][:, kb * wdt + osub * 128: kb * wdt + osub * 128 + ow],
                            rhs_fn(kb, t0, tw),
                            start=(kb == 0), stop=(kb == kb_n - 1),
                            skip_group_check=True)),
                            deps=([slab_tag[sbuf_i], bank_war.get(bk.name)]
                                  if kb == 0 else []),
                            inc=(kb == kb_n - 1))
                    slab_last_use[sbuf_i] = mt
                    war = evac_fn(meta, o0 + osub * 128, t0, tw, bk, mt)
                    bank_war[bk.name] = war
        return mt

    stg_war = [None] * 4
    STG = {"i": 0}

    def next_stage():
        s = STG["i"] % 4
        STG["i"] += 1
        return s

    def rhs_xb(kb, t0, tw):
        return xb[:, kb * TH + t0: kb * TH + t0 + tw]

    def evac_inproj(meta, o_abs, t0, tw, bk, mt):
        kind = meta["kind"]
        if kind == "dt":
            return plan.op(V, (lambda e, t0=t0, tw=tw, bk=bk: e.tensor_copy(
                dt_sb[:, t0 - 3: t0 - 3 + tw], bk[0:NH, 0:tw])), deps=[mt], inc=True)
        s = next_stage()
        if kind == "gate":
            et = plan.op(SC, (lambda e, s=s, bk=bk, tw=tw: e.activation(
                stage[s][:, 0:tw], bk[:, 0:tw], AF.Silu)),
                deps=[mt, stg_war[s]], inc=True)
            dmat = plan.dma(SY, (lambda e, s=s, o=o_abs, t0=t0, tw=tw: e.dma_start(
                out=sg_d[o:o + 128, t0 - 3: t0 - 3 + tw], in_=stage[s][:, 0:tw])),
                s_stg[s], deps=[et])
        else:
            et = plan.op(V, (lambda e, s=s, bk=bk, tw=tw: e.tensor_copy(
                stage[s][:, 0:tw], bk[:, 0:tw])), deps=[mt, stg_war[s]], inc=True)
            dmat = plan.dma(SY, (lambda e, s=s, o=o_abs, t0=t0, tw=tw: e.dma_start(
                out=xbc_d[o - I:o - I + 128, t0: t0 + tw], in_=stage[s][:, 0:tw])),
                s_stg[s], deps=[et])
        stg_war[s] = dmat
        return et

    TG_MAIN = [(3, 512), (515, 512)]
    TG_FULL = [(0, 512), (512, 512), (1024, 3)]
    slabs1 = ([(o, 512, {"kind": "gate", "tgrps": TG_MAIN}) for o in range(0, I, 512)]
              + [(o, 512, {"kind": "xbc", "tgrps": TG_FULL}) for o in range(I, 8192, 512)]
              + [(8192, 128, {"kind": "xbc", "tgrps": TG_FULL})]
              + [(8320, 64, {"kind": "dt", "tgrps": TG_MAIN})])
    mm_stage(w1t, KT, slabs1, rhs_xb, evac_inproj,
             slab_offs=W_OFFS['w1t'])

    plan.barrier()
    stage_end()
    es1.close()

    es2 = ExitStack()
    cd = sb(es2, "cd", [NH, NCHK], F32)
    dtot = sb(es2, "dtot", [NH, 1], F32)
    decb = sb(es2, "decb", [NH, NCHK * NH], F32)
    alcol = sb(es2, "alcol", [128, 8 * NH], F32)
    lndtT = sb(es2, "lndtT", [128, 8 * NH], F32)
    wcol = sb(es2, "wcol", [128, 8 * NH], F32)
    bfm = sb(es2, "bfm", [NS, T], BF16)
    cfm = sb(es2, "cfm", [NS, T], BF16)
    smask = sb(es2, "smask", [128, NCHK * 2 * CH], BF16)
    es2a = ExitStack()
    dts = sb(es2a, "dts", [NH, T], F32)
    lndt = sb(es2a, "lndt", [NH, T], F32)
    dA = sb(es2a, "dA", [NH, T], F32)
    acum = sb(es2a, "acum_sb", [NH, T], F32)
    alf = sb(es2a, "alf", [NH, T], F32)
    tmpa = sb(es2a, "tmpa", [NH, T], F32)
    zero64 = sb(es2a, "zero64", [NH, CH], F32)
    expa = sb(es2a, "expa", [NH, 1], F32)

    zt = plan.op(V, lambda e: e.memset(zero64[:, :], 0.0), inc=True)
    u1 = plan.op(V, lambda e: e.tensor_scalar(tmpa[:, :], dt_sb[:, :], dtb[:, 0:1],
                                              None, ALU.add), inc=True)
    u2 = plan.op(V, lambda e: e.scalar_tensor_tensor(lndt[:, :], tmpa[:, :], -1.0,
                                                     tmpa[:, :], ALU.mult, ALU.min),
                 deps=[u1], inc=True)
    u3 = plan.op(SC, lambda e: e.activation(lndt[:, :], lndt[:, :], AF.Exp),
                 deps=[u2], inc=True)
    u4 = plan.op(SC, lambda e: e.activation(lndt[:, :], lndt[:, :], AF.Ln, bias=1.0),
                 deps=[u3], inc=True)
    u5 = plan.op(V, lambda e: e.tensor_scalar(dA[:, :], tmpa[:, :], 0.0, None, ALU.max),
                 deps=[u2], inc=True)
    u6 = plan.op(V, lambda e: e.tensor_add(dts[:, :], lndt[:, :], dA[:, :]),
                 deps=[u4, u5], inc=True)
    u7 = plan.op(SC, lambda e: e.activation(lndt[:, :], dts[:, :], AF.Ln),
                 deps=[u6], inc=True)
    u8 = plan.op(SC, lambda e: e.activation(expa[:, :], alog[:, 0:1], AF.Exp),
                 deps=[("d_const", 16 * 11)], inc=True)
    u9 = plan.op(V, lambda e: e.tensor_scalar(dA[:, :], dts[:, :], expa[:, 0:1], -1.0,
                                              ALU.mult, ALU.mult), deps=[u6, u8],
                 inc=True)
    sc_t = []
    for c in range(NCHK):
        sc_t.append(plan.op(V, (lambda e, c=c: e.tensor_tensor_scan(
            acum[:, c * CH:(c + 1) * CH], dA[:, c * CH:(c + 1) * CH], zero64[:, :],
            0.0, ALU.add, ALU.add)), deps=[u9, zt], inc=True))
    al_t = plan.op(V, lambda e: e.tensor_sub(alf[:, :], acum[:, :], lndt[:, :]),
                   deps=sc_t + [u7], inc=True)
    acum_dma = plan.dma(SY, lambda e: e.dma_start(out=acum_d[:, :], in_=acum[:, :]),
                        s_acum, deps=sc_t)
    cd_t = plan.op(SC, lambda e: e.activation(
        cd[:, :], acum.ap()[:, CH - 1::CH], AF.Exp), deps=sc_t, inc=True)
    d1 = plan.op(V, lambda e: e.tensor_mul(dtot[:, :], cd[:, 0:1], cd[:, 1:2]),
                 deps=[cd_t], inc=True)
    d2 = plan.op(V, lambda e: e.tensor_mul(dtot[:, :], dtot[:, :], cd[:, 2:3]),
                 deps=[d1], inc=True)
    dtot_t = plan.op(V, lambda e: e.tensor_mul(dtot[:, :], dtot[:, :], cd[:, 3:4]),
                     deps=[d2], inc=True)

    tp_war = [None]

    def transpose_f32(src_ap, np_, dst_ap, dep):
        mt = plan.op(TE, (lambda e: e.transpose(
            ps_tpf[:, 0:np_], src_ap, idf[0:np_, 0:np_])),
            deps=[dep, tp_war[0], ("d_const", 16 * 11)], inc=True)
        et = plan.op(V, (lambda e: e.tensor_copy(
            dst_ap, ps_tpf[:, 0:np_])), deps=[mt], inc=True)
        tp_war[0] = et
        return et

    for tt in range(8):
        transpose_f32(alf[:, tt * 128:(tt + 1) * 128], NH,
                      alcol[:, tt * NH:(tt + 1) * NH], al_t)
        transpose_f32(lndt[:, tt * 128:(tt + 1) * 128], NH,
                      lndtT[:, tt * NH:(tt + 1) * NH], u7)
    for c in range(NCHK):
        w1 = plan.op(V, (lambda e, c=c: e.tensor_scalar(
            tmpa[:, c * CH:(c + 1) * CH], acum[:, c * CH:(c + 1) * CH],
            acum[:, (c + 1) * CH - 1:(c + 1) * CH], -1.0, ALU.subtract, ALU.mult)),
            deps=sc_t, inc=True)
        w2 = plan.op(V, (lambda e, c=c: e.tensor_add(
            tmpa[:, c * CH:(c + 1) * CH], tmpa[:, c * CH:(c + 1) * CH],
            lndt[:, c * CH:(c + 1) * CH])), deps=[w1, u7], inc=True)
        w3 = plan.op(SC, (lambda e, c=c: e.activation(
            tmpa[:, c * CH:(c + 1) * CH], tmpa[:, c * CH:(c + 1) * CH], AF.Exp)),
            deps=[w2], inc=True)
        for s in range(2):
            tt = 2 * c + s
            transpose_f32(tmpa[:, tt * 128:(tt + 1) * 128], NH,
                          wcol[:, tt * NH:(tt + 1) * NH], w3)
    for c in range(NCHK):
        mt = plan.op(TE, (lambda e, c=c: e.transpose(
            ps_tpf[0:1, 0:NH], cd[:, c:c + 1], idf[0:NH, 0:NH])),
            deps=[cd_t, tp_war[0]], inc=True)
        et = plan.op(V, (lambda e: e.tensor_copy(
            rowtmp[0:1, 0:NH], ps_tpf[0:1, 0:NH])),
            deps=[mt], inc=True)
        tp_war[0] = et
        bk = next_bank()
        m2 = plan.op(TE, (lambda e, bk=bk: e.matmul(
            bk[0:NH, 0:NH], onesf[0:1, 0:NH], rowtmp[0:1, 0:NH],
            start=True, stop=True, skip_group_check=True)),
            deps=[et, m_ones, bank_war.get(bk.name)], inc=True)
        e2 = plan.op(V, (lambda e, c=c, bk=bk: e.tensor_copy(
            decb[:, c * NH:(c + 1) * NH], bk[0:NH, 0:NH])), deps=[m2], inc=True)
        bank_war[bk.name] = e2

    plan.barrier()
    stage_end()
    es2a.close()

    es3 = ExitStack()
    xbct = [sb(es3, f"xbct{i}", [128, TH], BF16) for i in range(2)]
    cacc = [sb(es3, f"cacc{i}", [128, T], F32) for i in range(2)]
    conv_last = [None, None]
    acc_war = [None, None]
    for it in range(33):
        b = it % 2
        ld = plan.dma(SY, (lambda e, it=it, b=b: e.dma_start(
            out=xbct[b][:, :], in_=xbc_d[it * 128:(it + 1) * 128, :])),
            d_xbct[b], deps=[conv_last[b]] + allw(s_stg))
        a = b
        t = plan.op(V, (lambda e, b=b, a=a, it=it: e.tensor_scalar(
            cacc[a][:, :], xbct[b][:, 0:T], cw[:, it * 4:it * 4 + 1], None,
            ALU.mult)), deps=[ld, acc_war[a], ("d_const", 16 * 11)], inc=True)
        for k in range(1, 4):
            t = plan.op(V, (lambda e, b=b, a=a, it=it, k=k: e.scalar_tensor_tensor(
                cacc[a][:, :], xbct[b][:, k:T + k], cw[:, it * 4 + k:it * 4 + k + 1],
                cacc[a][:, :], ALU.mult, ALU.add)), deps=[t], inc=True)
        conv_last[b] = t
        sil = []
        for s2 in range(2):
            ss = next_stage()
            et = plan.op(SC, (lambda e, a=a, s2=s2, it=it, ss=ss: e.activation(
                stage[ss][:, :], cacc[a][:, s2 * 512:(s2 + 1) * 512], AF.Silu,
                bias=cb[:, it:it + 1])), deps=[t, stg_war[ss]], inc=True)
            W33 = 33 * CH
            dmat = None
            for half in range(2):
                cc_ = s2 * 2 + half
                dmat = plan.dma(SY, (lambda e, it=it, ss=ss, cc_=cc_, half=half,
                                     W33=W33: e.dma_start(
                    out=xs_d[:, cc_ * W33 + it * CH: cc_ * W33 + (it + 1) * CH],
                    in_=stage[ss][:, half * CH:(half + 1) * CH])),
                    s_stg[ss], deps=[et])
            stg_war[ss] = dmat
            sil.append(et)
            if it == 32:
                b1 = plan.op(V, (lambda e, s2=s2, ss=ss: e.tensor_copy(
                    bfm[:, s2 * 512:(s2 + 1) * 512], stage[ss][0:NS, :])),
                    deps=[et], inc=True)
                c1 = plan.op(V, (lambda e, s2=s2, ss=ss: e.tensor_copy(
                    cfm[:, s2 * 512:(s2 + 1) * 512], stage[ss][NS:128, :])),
                    deps=[et], inc=True)
                stg_war[ss] = c1
        acc_war[a] = sil[-1]

    plan.barrier()
    stage_end()
    es3.close()

    es4 = ExitStack()
    xs_c = sb(es4, "xs_c", [128, 32 * CH], BF16)
    xsT = sb(es4, "xsT", [128, NCHK * 2 * I], BF16)
    btc = sb(es4, "btc", [128, 2 * NS], BF16)
    bd = [sb(es4, f"bd{i}", [128, 2 * NS], BF16) for i in range(2)]
    es4s = ExitStack()
    ststage = sb(es4s, "ststage", [NH, I], F32)
    carry = [sb(es4s, "carry0", [NH, I], F32)]

    cz = plan.op(V, lambda e: e.memset(carry[0][:, :], 0.0), inc=True)
    bd_war = [None, None]
    tpb_war = [None]

    def transpose_b(src_ap, np_, dst_ap, dep):
        mt = plan.op(TE, (lambda e: e.transpose(
            ps_tpb[:, 0:np_], src_ap, idb[0:np_, 0:np_])),
            deps=[dep, tpb_war[0]], inc=True)
        et = plan.op(V, (lambda e: e.tensor_copy(
            dst_ap, ps_tpb[:, 0:np_])), deps=[mt], inc=True)
        tpb_war[0] = et
        return et

    def load_xs_chunk(c, war, dsem):
        W33 = 33 * CH
        return plan.dma(SY, (lambda e, c=c, W33=W33: e.dma_start(
            out=xs_c[:, :],
            in_=xs_d[:, c * W33: c * W33 + 32 * CH])),
            dsem, deps=[war] + allw(s_stg))

    def build_xsT(c, dep_load):
        tags = []
        base = c * 2 * I
        for s in range(2):
            for it0 in range(0, 32, 4):
                mt = None
                for q in range(4):
                    it = it0 + q
                    mt = plan.op(TE, (lambda e, s=s, it=it, q=q: e.transpose(
                        ps_tpb[:, q * 128:(q + 1) * 128],
                        xs_c[:, it * CH + s * 128: it * CH + s * 128 + 128],
                        idb[0:128, 0:128])),
                        deps=[dep_load] + ([tpb_war[0]] if q == 0 else []),
                        inc=(q == 3))
                et = plan.op(V, (lambda e, s=s, it0=it0, base=base: e.tensor_copy(
                    xsT[:, base + s * I + it0 * 128: base + s * I + it0 * 128 + 512],
                    ps_tpb[:, 0:512])), deps=[mt], inc=True)
                tpb_war[0] = et
                tags.append(et)
        return tags

    xs_war = None
    scan_done = []
    stev_prev = None
    st_dma_last = None
    for c in range(NCHK):
        xl = load_xs_chunk(c, xs_war, d_xsc)
        xst_tags = build_xsT(c, xl)
        xs_war = xst_tags[-1]
        for s in range(2):
            bk = next_bank()
            mt = plan.op(TE, (lambda e, c=c, s=s, bk=bk: e.matmul(
                bk[:, 0:CH], bfm[:, c * CH + s * 128: c * CH + s * 128 + 128],
                cfm[:, c * CH:(c + 1) * CH], start=True, stop=True,
                skip_group_check=True)),
                deps=[bank_war.get(bk.name)], inc=True)
            et = plan.op(V, (lambda e, c=c, s=s, bk=bk: e.tensor_tensor(
                smask[:, (2 * c + s) * CH:(2 * c + s + 1) * CH], bk[:, 0:CH],
                tri[:, s * CH:(s + 1) * CH], ALU.mult)), deps=[mt], inc=True)
            bank_war[bk.name] = et
        bt_tags = []
        for s in range(2):
            bt_tags.append(transpose_b(
                bfm[:, c * CH + s * 128: c * CH + s * 128 + 128], NS,
                btc[:, s * NS:(s + 1) * NS], dep=None))
        for hg in range(8):
            mm_last = None
            for hh in range(8):
                h = hg * 8 + hh
                bdi = hh % 2
                bdt = []
                for s in range(2):
                    tt = 2 * c + s
                    bdt.append(plan.op(V, (lambda e, h=h, s=s, tt=tt, bdi=bdi:
                        e.tensor_scalar(
                            bd[bdi][:, s * NS:(s + 1) * NS],
                            btc[:, s * NS:(s + 1) * NS],
                            wcol[:, tt * NH + h: tt * NH + h + 1], None, ALU.mult)),
                        deps=[bt_tags[s], bd_war[bdi]], inc=True))
                for s in range(2):
                    mm_last = plan.op(TE, (lambda e, c=c, h=h, hh=hh, s=s, bdi=bdi:
                        e.matmul(
                            ps_e[0:NS, hh * PD:(hh + 1) * PD],
                            bd[bdi][:, s * NS:(s + 1) * NS],
                            xsT[:, c * 2 * I + s * I + h * PD:
                                c * 2 * I + s * I + (h + 1) * PD],
                            start=(hh == 0 and s == 0), stop=(hh == 7 and s == 1),
                            skip_group_check=True)),
                        deps=[bdt[s]] + ([xs_war, stev_prev] if hh == 0 and s == 0
                                         else []),
                        inc=True)
                bd_war[bdi] = mm_last
            stev_prev = plan.op(V, (lambda e, hg=hg: e.tensor_copy(
                ststage[:, hg * 512:(hg + 1) * 512], ps_e[0:NH, :])),
                deps=[mm_last, st_dma_last], inc=True)
        st_dma = plan.dma(GP, (lambda e, c=c: e.dma_start(
            out=st_d[c * NH:(c + 1) * NH, :], in_=ststage[:, :])),
            s_st, deps=[stev_prev])
        st_dma_last = st_dma
        src = carry[0]
        dst = carry[0]
        last = None
        for h in range(NH):
            last = plan.op(V, (lambda e, c=c, h=h, src=src, dst=dst:
                e.scalar_tensor_tensor(
                    dst[:, h * PD:(h + 1) * PD], src[:, h * PD:(h + 1) * PD],
                    decb[:, c * NH + h: c * NH + h + 1],
                    ststage[:, h * PD:(h + 1) * PD], ALU.mult, ALU.add)),
                deps=[stev_prev, st_dma] + ([cz] if c == 0 else [scan_done[-1]]),
                inc=True)
        scan_done.append(last)

    cc1 = plan.dma(SY, lambda e: e.dma_start(out=ccin_d[:, 0:I], in_=carry[0][:, :]),
                   s_ccin, deps=[scan_done[-1]])
    cc2 = plan.dma(SY, lambda e: e.dma_start(out=ccin_d[:, I:I + 1], in_=dtot[:, :],
                                             allow_slow_non_contiguous=True),
                   s_ccin, deps=[dtot_t])
    es4s.close()

    plan.barrier()
    stage_end()

    def do_cc(e):
        e.collective_compute(
            "AllGather", ALU.bypass, replica_groups=[list(range(NCORES))],
            ins=[ccin_d.opt()], outs=[ccout_d.opt()]).then_inc(_SEMS["d_cc"])
        return None
    plan.op(GP, do_cc, deps=[(s_ccin, plan.dma_fills[s_ccin])])
    CC_TAG = ("d_cc", 1)

    es4b = ExitStack()
    lq_st = sb(es4b, "lq_st", [NH, I], BF16)
    dtall = sb(es4b, "dtall", [NH, NCORES], F32)
    dmall = sb(es4b, "dmall", [NH, NCORES], F32)
    wfin = sb(es4b, "wfin", [NH, NCORES], F32)
    rmb = sb(es4b, "rmb", [NH, NCORES], F32)
    wbq = sb(es4b, "wbq", [NH, NH], F32)
    prevb = sb(es4b, "prevb", [NH, I], F32)
    prevbb = sb(es4b, "prevbb", [NH, I], BF16)
    stldb = sb(es4b, "stldb", [NH, I], BF16)

    pz = plan.op(V, lambda e: e.memset(prevb[:, :], 0.0), inc=True)
    dt_tags = []
    for q in range(NCORES):
        dt_tags.append(plan.dma(SY, (lambda e, q=q: e.dma_start(
            out=dtall[:, q:q + 1], in_=ccout_d[q * NH:(q + 1) * NH, I:I + 1],
            allow_slow_non_contiguous=True)),
            d_dt, deps=[CC_TAG]))
    dt_all = dt_tags[-1]
    bk = next_bank()
    mt = plan.op(TE, (lambda e, bk=bk: e.matmul(
        bk[0:NH, 0:NCORES], onesf[0:1, 0:NH], rmask[0:1, :], start=True, stop=True,
        skip_group_check=True)), deps=[bank_war.get(bk.name), m_ones,
                                       ("d_const", 16 * 11)], inc=True)
    rmb_t = plan.op(V, (lambda e, bk=bk: e.tensor_copy(rmb[:, :], bk[0:NH, 0:NCORES])),
                    deps=[mt], inc=True)
    bank_war[bk.name] = rmb_t
    q1 = plan.op(V, lambda e: e.tensor_scalar(dmall[:, :], dtall[:, :], -1.0, None,
                                              ALU.add), deps=[dt_all], inc=True)
    q2 = plan.op(V, lambda e: e.tensor_mul(dmall[:, :], dmall[:, :], rmb[:, :]),
                 deps=[q1, rmb_t], inc=True)
    q3 = plan.op(V, lambda e: e.tensor_scalar(dmall[:, :], dmall[:, :], 1.0, None,
                                              ALU.add), deps=[q2], inc=True)
    wt = plan.op(V, lambda e: e.memset(wfin[:, NCORES - 1:NCORES], 1.0), inc=True)
    for q in range(NCORES - 2, -1, -1):
        wt = plan.op(V, (lambda e, q=q: e.tensor_mul(
            wfin[:, q:q + 1], wfin[:, q + 1:q + 2], dmall[:, q + 1:q + 2])),
            deps=[wt, q3], inc=True)
    wt = plan.op(V, lambda e: e.tensor_mul(wfin[:, :], wfin[:, :], rmb[:, :]),
                 deps=[wt], inc=True)
    lq_war = None
    for q in range(NCORES):
        mt = plan.op(TE, (lambda e, q=q: e.transpose(
            ps_tpf[0:1, 0:NH], wfin[:, q:q + 1], idf[0:NH, 0:NH])),
            deps=[wt, tp_war[0]], inc=True)
        et = plan.op(V, (lambda e: e.tensor_copy(
            rowtmp[0:1, 0:NH], ps_tpf[0:1, 0:NH])), deps=[mt], inc=True)
        tp_war[0] = et
        bk = next_bank()
        m2 = plan.op(TE, (lambda e, bk=bk: e.matmul(
            bk[0:NH, 0:NH], onesf[0:1, 0:NH], rowtmp[0:1, 0:NH], start=True,
            stop=True, skip_group_check=True)),
            deps=[et, m_ones, bank_war.get(bk.name)], inc=True)
        wb_t = plan.op(V, (lambda e, bk=bk: e.tensor_copy(wbq[:, :], bk[0:NH, 0:NH])),
                       deps=[m2], inc=True)
        bank_war[bk.name] = wb_t
        lq_t = plan.dma(GP, (lambda e, q=q: e.dma_start(
            out=lq_st[:, :], in_=ccout_d[q * NH:(q + 1) * NH, 0:I])), d_lq,
            deps=[CC_TAG, lq_war])
        last = None
        for h in range(NH):
            last = plan.op(V, (lambda e, h=h: e.scalar_tensor_tensor(
                prevb[:, h * PD:(h + 1) * PD], lq_st[:, h * PD:(h + 1) * PD],
                wbq[:, h:h + 1], prevb[:, h * PD:(h + 1) * PD], ALU.mult, ALU.add)),
                deps=[lq_t, wb_t, pz], inc=True)
        lq_war = last

    plan.barrier()
    stage_end()

    es4c = ExitStack()
    abc = [sb(es4c, f"abc{i}", [128, CH], F32) for i in range(4)]
    dc_sb = [sb(es4c, f"dc{i}", [128, CH], F32) for i in range(6)]
    lb_sb = [sb(es4c, f"lb{i}", [128, CH], BF16) for i in range(6)]
    pb_sb = [sb(es4c, f"pb{i}", [128, CH], BF16) for i in range(6)]
    csd = [sb(es4c, f"csd{i}", [NS, CH], BF16) for i in range(4)]
    ysta = [sb(es4c, f"ysta{i}", [128, CH], BF16) for i in range(2)]

    prev_ready = plan.op(V, lambda e: e.tensor_copy(prevbb[:, :], prevb[:, :]),
                         inc=True)
    xs_war = None
    ysta_war = [None, None]
    abc_war = [None] * 4
    dc_war = [None] * 6
    pb_war = [None] * 6
    csd_war = [None] * 4
    offmm_last = None
    for c in range(NCHK):
        if c > 0:
            stl = plan.dma(SY, (lambda e, c=c: e.dma_start(
                out=stldb[:, :], in_=st_d[(c - 1) * NH:c * NH, :])), d_stl,
                deps=[(s_st, plan.dma_fills[s_st]), prev_ready])
            last = None
            for h in range(NH):
                last = plan.op(V, (lambda e, c=c, h=h: e.scalar_tensor_tensor(
                    prevb[:, h * PD:(h + 1) * PD], prevb[:, h * PD:(h + 1) * PD],
                    decb[:, (c - 1) * NH + h:(c - 1) * NH + h + 1],
                    stldb[:, h * PD:(h + 1) * PD], ALU.mult, ALU.add)),
                    deps=[stl, prev_ready], inc=True)
            prev_ready = plan.op(V, lambda e: e.tensor_copy(prevbb[:, :],
                                                            prevb[:, :]),
                                 deps=[last, offmm_last], inc=True)
        xl = load_xs_chunk(c, xs_war, d_xsc)
        last_et = None
        for h in range(NH):
            n = h % 4
            dq = SY if h % 2 == 0 else GP
            abct = plan.dma(dq, (lambda e, c=c, h=h, n=n: e.dma_start(
                out=abc[n][:, :],
                in_=acum_d[h:h + 1, c * CH:(c + 1) * CH].partition_broadcast(128))),
                d_abc[n], deps=list(abc_war[n] or ()) + [(s_acum, 16)])
            dcl = []
            for s in range(2):
                tt = 2 * c + s
                r = (2 * h + s) % 6
                t1 = plan.op(V, (lambda e, n=n, r=r, tt=tt, h=h: e.tensor_scalar(
                    dc_sb[r][:, :], abc[n][:, :],
                    alcol[:, tt * NH + h: tt * NH + h + 1],
                    lndtT[:, tt * NH + h: tt * NH + h + 1],
                    ALU.subtract, ALU.min)), deps=[abct, dc_war[r]], inc=True)
                t2 = plan.op(SC, (lambda e, r=r: e.activation(
                    lb_sb[r][:, :], dc_sb[r][:, :], AF.Exp)), deps=[t1], inc=True)
                dc_war[r] = t2
                t3 = plan.op(V, (lambda e, r=r, c=c, s=s: e.tensor_tensor(
                    pb_sb[r][:, :], lb_sb[r][:, :],
                    smask[:, (2 * c + s) * CH:(2 * c + s + 1) * CH], ALU.mult)),
                    deps=[t2, pb_war[r]], inc=True)
                dcl.append((r, t3))
            t1_last = t1
            cr = h % 4
            lvt = plan.op(SC, (lambda e, n=n, cr=cr: e.activation(
                csd[cr][:, :], abc[n][0:NS, :], AF.Exp)),
                deps=[abct, csd_war[cr]], inc=True)
            csd_t = plan.op(V, (lambda e, cr=cr, c=c: e.tensor_tensor(
                csd[cr][:, :], csd[cr][:, :], cfm[:, c * CH:(c + 1) * CH],
                ALU.mult)), deps=[lvt], inc=True)
            abc_war[n] = (t1_last, lvt)
            j = h // 2
            odd = h % 2
            yb = ps_mm[j % 4]
            tp = (0, 64) if odd else None
            out_sl = yb[64:128, 0:CH] if odd else yb[0:64, 0:CH]
            mms = []
            for s in range(2):
                r, t3 = dcl[s]
                mms.append(plan.op(TE, (lambda e, out_sl=out_sl, c=c, s=s, h=h,
                                        r=r, tp=tp: e.matmul(
                    out_sl,
                    xsT[:, c * 2 * I + s * I + h * PD:
                        c * 2 * I + s * I + (h + 1) * PD],
                    pb_sb[r][:, :], start=(s == 0), stop=False,
                    skip_group_check=True, tile_position=tp)),
                    deps=[t3] + ([bank_war.get(yb.name)]
                                 if s == 0 and not odd else []),
                    inc=True))
                pb_war[r] = mms[-1]
            mms.append(plan.op(TE, (lambda e, out_sl=out_sl, h=h, cr=cr, tp=tp:
                e.matmul(
                    out_sl, prevbb[:, h * PD:(h + 1) * PD], csd[cr][:, :],
                    start=False, stop=True, skip_group_check=True,
                    tile_position=tp)),
                deps=[csd_t, prev_ready], inc=True))
            csd_war[cr] = mms[-1]
            offmm_last = mms[-1]
            if odd:
                yr = j % 2
                et = plan.op(V, (lambda e, yb=yb, j=j, yr=yr:
                    e.scalar_tensor_tensor(
                        ysta[yr][:, :], xs_c[:, j * CH:(j + 1) * CH],
                        dexp[:, j:j + 1], yb[:, 0:CH], ALU.mult, ALU.add)),
                    deps=[mms[-1], ysta_war[yr], xl], inc=True)
                ydma = plan.dma(SY, (lambda e, j=j, c=c, yr=yr: e.dma_start(
                    out=y_d[j * 128:(j + 1) * 128, c * CH:(c + 1) * CH],
                    in_=ysta[yr][:, :])), s_y[yr], deps=[et])
                ysta_war[yr] = ydma
                bank_war[yb.name] = et
                last_et = et
        xs_war = last_et


    plan.barrier()
    stage_end()
    es4c.close()
    es4b.close()
    es4.close()
    es2.close()

    es5 = ExitStack()
    ytl = [sb(es5, f"ytl{i}", [128, 512], BF16) for i in range(2)]
    sgt = [sb(es5, f"sgt{i}", [128, 512], BF16) for i in range(2)]
    yp = sb(es5, "yp", [128, 32 * 512], BF16)
    ysq = [sb(es5, f"ysq{i}", [128, 512], BF16) for i in range(2)]
    rstdb = sb(es5, "rstdb", [128, 512], F32)
    rtile = [sb(es5, f"rtile{i}", [128, 512], F32) for i in range(2)]
    x1bf = [sb(es5, f"x1bf{i}", [128, 512], BF16) for i in range(2)]
    sum2row = sb(es5, "sum2row", [1, T], F32)
    sqr2row = sb(es5, "sqr2row", [1, T], F32)

    opj_last = None
    prev_r2 = None
    carry_sq = [None, None]
    opj_rt_war = {}
    opj_sf_war = {}
    opj_bf_war = {}
    opj_sq_done = {}
    for tb in range(2):
        ysq_war = list(carry_sq)
        sq_mm_last = None
        yp_tags = []
        for j in range(32):
            b = j % 2
            ylt = plan.dma(SY, (lambda e, j=j, tb=tb, b=b: e.dma_start(
                out=ytl[b][:, :], in_=y_d[j * 128:(j + 1) * 128,
                                          tb * 512:(tb + 1) * 512])),
                d_yt[b], deps=allw(s_y) + [yp_tags[-2] if j >= 2 else None])
            sgl = plan.dma(SY, (lambda e, j=j, tb=tb, b=b: e.dma_start(
                out=sgt[b][:, :], in_=sg_d[j * 128:(j + 1) * 128,
                                           tb * 512:(tb + 1) * 512])),
                d_sgt[b], deps=allw(s_stg) + [yp_tags[-2] if j >= 2 else None])
            ypt = plan.op(V, (lambda e, j=j, b=b: e.tensor_tensor(
                yp[:, j * 512:(j + 1) * 512], ytl[b][:, :], sgt[b][:, :], ALU.mult)),
                deps=[ylt, sgl, opj_last, prev_r2], inc=True)
            yp_tags.append(ypt)
            sqt = plan.op(SC, (lambda e, j=j, b=b: e.activation(
                ysq[b][:, :], yp[:, j * 512:(j + 1) * 512], AF.Square)),
                deps=[ypt, ysq_war[b]], inc=True)
            sq_mm_last = plan.op(TE, (lambda e, j=j, b=b: e.matmul(
                ps_row[0:1, :], onesbc[:, 0:1], ysq[b][:, :],
                start=(j == 0), stop=(j == 31), skip_group_check=True)),
                deps=[sqt, m_ones] + ([bank_war.get(ps_row.name)] if j == 0 else []),
                inc=True)
            ysq_war[b] = sq_mm_last
        r0 = plan.op(V, (lambda e, tb=tb: e.tensor_scalar(
            sum2row[0:1, tb * 512:(tb + 1) * 512], ps_row[0:1, :], 1.0 / I, EPS,
            ALU.mult, ALU.add)), deps=[sq_mm_last], inc=True)
        r1 = plan.op(SC, (lambda e, tb=tb: e.activation(
            sum2row[0:1, tb * 512:(tb + 1) * 512],
            sum2row[0:1, tb * 512:(tb + 1) * 512], AF.Ln)), deps=[r0], inc=True)
        r1b = plan.op(V, (lambda e, tb=tb: e.tensor_scalar(
            sum2row[0:1, tb * 512:(tb + 1) * 512],
            sum2row[0:1, tb * 512:(tb + 1) * 512], -0.5, None, ALU.mult)),
            deps=[r1], inc=True)
        r2 = plan.op(SC, (lambda e, tb=tb: e.activation(
            sum2row[0:1, tb * 512:(tb + 1) * 512],
            sum2row[0:1, tb * 512:(tb + 1) * 512], AF.Exp)), deps=[r1b], inc=True)
        bank_war[ps_row.name] = r0
        bk = next_bank()
        m3 = plan.op(TE, (lambda e, bk=bk, tb=tb: e.matmul(
            bk[:, :], onesf[0:1, :], sum2row[0:1, tb * 512:(tb + 1) * 512],
            start=True, stop=True, skip_group_check=True)),
            deps=[r2, m_ones, bank_war.get(bk.name)], inc=True)
        rb_t = plan.op(V, (lambda e, bk=bk: e.tensor_copy(rstdb[:, :], bk[:, :])),
                       deps=[m3], inc=True)
        bank_war[bk.name] = rb_t
        ypp_tags = []
        for j in range(32):
            ypp_tags.append(plan.op(V, (lambda e, j=j: e.scalar_tensor_tensor(
                yp[:, j * 512:(j + 1) * 512], yp[:, j * 512:(j + 1) * 512],
                nw[:, j:j + 1], rstdb[:, :], ALU.mult, ALU.mult)),
                deps=[yp_tags[j], rb_t], inc=True))
        ypp_all = ypp_tags[-1]

        def rhs_yp(kb, t0, tw):
            return yp[:, kb * 512: kb * 512 + 512]

        x1_evt = {"n": 0}

        def evac_outproj(meta, o_abs, t0, tw, bk2, mt2):
            tb_ = meta["tb"]
            b = x1_evt["n"] % 2
            x1_evt["n"] += 1
            rt = plan.dma(SY, (lambda e, o=o_abs, tb_=tb_, b=b: e.dma_start(
                out=rtile[b][:, :], in_=r_d[o:o + 128, tb_ * 512:(tb_ + 1) * 512])),
                d_rt[b], deps=allw(s_t2f) + [meta["rt_war"].get(b)])
            at = plan.op(V, (lambda e, bk2=bk2, b=b: e.tensor_add(
                stagef[b][:, :], bk2[:, :], rtile[b][:, :])),
                deps=[mt2, rt, meta["sf_war"].get(b), meta["sq_done"].get(b)],
                inc=True)
            meta["rt_war"][b] = at
            xd = plan.dma(SY, (lambda e, o=o_abs, tb_=tb_, b=b: e.dma_start(
                out=x1_d[o:o + 128, tb_ * 512:(tb_ + 1) * 512],
                in_=stagef[b][:, :])), s_sf[b], deps=[at])
            cst = plan.op(V, (lambda e, b=b: e.tensor_copy(
                x1bf[b][:, :], stagef[b][:, :])), deps=[at, meta["bf_war"].get(b)],
                inc=True)
            sq2 = plan.op(SC, (lambda e, b=b: e.activation(
                ysq[b][:, :], stagef[b][:, :], AF.Square)),
                deps=[at, meta["sq_war"].get(b)], inc=True)
            ki = meta["ki"]["n"]
            meta["ki"]["n"] += 1
            mm1 = plan.op(TE, (lambda e, b=b, ki=ki: e.matmul(
                ps_row[0:1, :], onesbc[:, 0:1], x1bf[b][:, :], start=(ki == 0),
                stop=(ki == 15), skip_group_check=True)),
                deps=[cst, m_ones] + ([bank_war.get(ps_row.name)] if ki == 0 else []),
                inc=True)
            mm2 = plan.op(TE, (lambda e, b=b, ki=ki: e.matmul(
                ps_e[0:1, :], onesbc[:, 0:1], ysq[b][:, :], start=(ki == 0),
                stop=(ki == 15), skip_group_check=True)),
                deps=[sq2, m_ones] + ([bank_war.get(ps_e.name)] if ki == 0 else []),
                inc=True)
            meta["bf_war"][b] = mm1
            meta["sq_war"][b] = mm2
            meta["sq_done"][b] = sq2
            meta["sf_war"][b] = xd
            meta["last_mm"] = (mm1, mm2, xd)
            return at

        meta2 = {"tb": tb, "tgrps": [(tb * 512, 512)], "rt_war": opj_rt_war,
                 "sf_war": opj_sf_war, "bf_war": opj_bf_war,
                 "sq_war": {0: sq_mm_last, 1: sq_mm_last},
                 "sq_done": opj_sq_done, "ki": {"n": 0}, "last_mm": None}
        slabs2 = [(o, 256, meta2) for o in range(0, H, 256)]
        opj_last = mm_stage(w2t, 32, slabs2, rhs_yp, evac_outproj,
                            extra_dep=ypp_all, slab_offs=W_OFFS['w2t'])
        prev_r2 = r2
        carry_sq = [meta2["sq_war"].get(0), meta2["sq_war"].get(1)]
        mm1, mm2, _ = meta2["last_mm"]
        ev1 = plan.op(V, (lambda e, tb=tb: e.tensor_copy(
            sum2row[0:1, tb * 512:(tb + 1) * 512], ps_row[0:1, :])),
            deps=[mm1], inc=True)
        ev2 = plan.op(V, (lambda e, tb=tb: e.tensor_copy(
            sqr2row[0:1, tb * 512:(tb + 1) * 512], ps_e[0:1, :])),
            deps=[mm2], inc=True)
        bank_war[ps_row.name] = ev1
        bank_war[ps_e.name] = ev2

    plan.barrier()
    stage_end()
    es5.close()

    es6 = ExitStack()
    m2row = sb(es6, "m2row", [1, T], F32)
    var2 = sb(es6, "var2", [1, T], F32)
    a2row = sb(es6, "a2row", [1, T], F32)
    mb2 = sb(es6, "mb2", [128, T], F32)
    ab2 = sb(es6, "ab2", [128, T], F32)
    x1t = [sb(es6, f"x1t{i}", [128, T], F32) for i in range(2)]
    jt = sb(es6, "jt", [128, T], F32)
    xb2 = sb(es6, "xb2", [128, KT * T], BF16)

    n1 = plan.op(V, lambda e: e.tensor_scalar(m2row[:, :], sum2row[:, :], 1.0 / H,
                                              None, ALU.mult), inc=True)
    n2 = plan.op(V, lambda e: e.tensor_mul(var2[:, :], m2row[:, :], m2row[:, :]),
                 deps=[n1], inc=True)
    n3 = plan.op(V, lambda e: e.scalar_tensor_tensor(
        var2[:, :], sqr2row[:, :], 1.0 / H, var2[:, :], ALU.mult, ALU.subtract),
        deps=[n2], inc=True)
    n3b = plan.op(V, lambda e: e.tensor_scalar(a2row[:, :], var2[:, :], 1.0, EPS,
                                               ALU.mult, ALU.add), deps=[n3],
                  inc=True)
    n4 = plan.op(SC, lambda e: e.activation(a2row[:, :], a2row[:, :], AF.Ln),
                 deps=[n3b], inc=True)
    n4b = plan.op(V, lambda e: e.tensor_scalar(a2row[:, :], a2row[:, :], -0.5, None,
                                               ALU.mult), deps=[n4], inc=True)
    n5 = plan.op(SC, lambda e: e.activation(a2row[:, :], a2row[:, :], AF.Exp),
                 deps=[n4b], inc=True)

    def bcast_row_t(src_row, dst_tile, dep):
        tags = []
        for t0 in (0, 512):
            bk = next_bank()
            mt = plan.op(TE, (lambda e, bk=bk, t0=t0, s=src_row: e.matmul(
                bk[:, :], onesf[0:1, :], s[0:1, t0:t0 + 512], start=True, stop=True,
                skip_group_check=True)), deps=[dep, m_ones, bank_war.get(bk.name)],
                inc=True)
            et = plan.op(V, (lambda e, bk=bk, t0=t0, d=dst_tile: e.tensor_copy(
                d[:, t0:t0 + 512], bk[:, :])), deps=[mt], inc=True)
            bank_war[bk.name] = et
            tags.append(et)
        return tags

    mb2_t = bcast_row_t(m2row, mb2, n1)
    ab2_t = bcast_row_t(a2row, ab2, n5)
    x1_war = [None, None]
    j2 = None
    for kt in range(KT):
        b = kt % 2
        xlt = plan.dma(SY, (lambda e, kt=kt, b=b: e.dma_start(
            out=x1t[b][:, :], in_=x1_d[kt * 128:(kt + 1) * 128, :])), d_x1t[b],
            deps=allw(s_sf) + [x1_war[b]])
        j1 = plan.op(V, (lambda e, b=b: e.tensor_sub(jt[:, :], x1t[b][:, :],
                                                     mb2[:, :])),
                     deps=[xlt, j2] + mb2_t, inc=True)
        j2 = plan.op(V, (lambda e, kt=kt: e.tensor_mul(
            xb2[:, kt * T:(kt + 1) * T], jt[:, :], ab2[:, :])),
            deps=[j1] + ab2_t, inc=True)
        x1_war[b] = j2

    plan.barrier()
    stage_end()

    es7 = ExitStack()
    sgf = sb(es7, "sgf", [128, 8 * 512], BF16)

    def rhs_xb2(kb, t0, tw):
        return xb2[:, kb * T + t0: kb * T + t0 + tw]

    sgf_war = {}

    def evac_ffn1(meta, o_abs, t0, tw, bk, mt):
        if meta["kind"] == "fg":
            gi = (o_abs % 512) // 128 * 2 + (t0 // 512)
            et = plan.op(SC, (lambda e, gi=gi, bk=bk: e.activation(
                sgf[:, gi * 512:(gi + 1) * 512], bk[:, :], AF.Silu)),
                deps=[mt, sgf_war.get(gi)], inc=True)
            sgf_war[gi] = et
            meta.setdefault("sg_tags", {})[gi] = et
            return et
        else:
            gi = (o_abs % 512) // 128 * 2 + (t0 // 512)
            gtag = meta["g_meta"]["sg_tags"][gi]
            s = next_stage()
            et = plan.op(V, (lambda e, gi=gi, bk=bk, s=s: e.tensor_tensor(
                stage[s][:, :], bk[:, :], sgf[:, gi * 512:(gi + 1) * 512], ALU.mult)),
                deps=[mt, gtag, stg_war[s]], inc=True)
            kb_ = (o_abs - INTER) // 128
            tb_ = t0 // 512
            dmat = plan.dma(SY, (lambda e, kb_=kb_, tb_=tb_, s=s: e.dma_start(
                out=h_d[:, tb_ * 32768 + kb_ * 512: tb_ * 32768 + (kb_ + 1) * 512],
                in_=stage[s][:, :])),
                s_stg[s], deps=[et])
            stg_war[s] = dmat
            sgf_war[gi] = et
            return et

    TG_T = [(0, 512), (512, 512)]
    slabs3 = []
    for gi in range(INTER // 512):
        mg = {"kind": "fg", "tgrps": TG_T}
        mu = {"kind": "fu", "tgrps": TG_T, "g_meta": mg}
        slabs3.append((gi * 512, 512, mg))
        slabs3.append((INTER + gi * 512, 512, mu))
    mm_stage(w3t, KT, slabs3, rhs_xb2, evac_ffn1,
             slab_offs=W_OFFS['w3t'])

    plan.barrier()
    stage_end()
    es7.close()

    es8 = ExitStack()
    hblk = sb(es8, "hblk", [128, 64 * 512], BF16)
    rtile2 = [sb(es8, f"rtile2_{i}", [128, 512], F32) for i in range(2)]
    KB2 = INTER // 128
    ffn2_last = None
    x1w = {}
    for tb in range(2):
        hl = plan.dma(SY, (lambda e, tb=tb: e.dma_start(
            out=hblk[:, :], in_=h_d[:, tb * 32768:(tb + 1) * 32768])),
            d_hblk, deps=allw(s_stg) + [ffn2_last])

        def rhs_h(kb, t0, tw):
            return hblk[:, kb * 512: kb * 512 + 512]

        def evac_ffn2(meta, o_abs, t0, tw, bk, mt, tb=tb):
            b = (o_abs // 128) % 2
            xlt = plan.dma(SY, (lambda e, o=o_abs, tb=tb, b=b: e.dma_start(
                out=rtile2[b][:, :], in_=x1_d[o:o + 128, tb * 512:(tb + 1) * 512])),
                d_x1t[b], deps=allw(s_sf) + [x1w.get(b)])
            at = plan.op(V, (lambda e, bk=bk, b=b: e.tensor_add(
                stagef[b][:, :], bk[:, :], rtile2[b][:, :])),
                deps=[mt, xlt, x1w.get(("sf", b))], inc=True)
            od = plan.dma(SY, (lambda e, o=o_abs, tb=tb, b=b: e.dma_start(
                out=out_hbm[o:o + 128, tb * 512:(tb + 1) * 512],
                in_=stagef[b][:, :])), s_sf[b], deps=[at])
            x1w[b] = at
            x1w[("sf", b)] = od
            return at

        meta4 = {"tgrps": [(tb * 512, 512)]}
        slabs4 = [(o, 128, meta4) for o in range(0, H, 128)]
        ffn2_last = mm_stage(w4t, KB2, slabs4, rhs_h, evac_ffn2,
                             extra_dep=hl, slab_offs=W_OFFS['w4t'])

    plan.barrier()
    stage_end()

    sems = {}
    sem_stack = ExitStack()
    for name in plan.sem_names:
        sems[name] = sem_stack.enter_context(nc.semaphore(name))
    global _SEMS
    _SEMS = sems

    with nc.Block() as block:
        def runner(engname):
            def run(e):
                for item in plan.ops[engname]:
                    if item[0] == "wait":
                        e.wait_ge(sems[item[1]], item[2])
                    else:
                        inst = item[1](e)
                        tag = item[2]
                        if tag is not None and inst is not None:
                            inst.then_inc(sems[tag[0]],
                                          16 if item[0] == "dma" else 1)
            return run
        block.tensor(runner(TE))
        block.vector(runner(V))
        block.scalar(runner(SC))
        block.gpsimd(runner(GP))
        block.sync(runner(SY))

    sem_stack.close()
    es8.close()
    es6.close()
    es0.close()
    return nc


_SEMS = None


def _tile_w(WT, slabs, kb_n):
    parts = []
    for (o0, w) in slabs:
        blk = WT[:, o0:o0 + w].reshape(kb_n, 128, w).transpose(1, 0, 2)
        parts.append(blk.reshape(128, kb_n * w))
    return np.ascontiguousarray(np.concatenate(parts, axis=1))


def _host_inputs(full):
    x = np.asarray(full["L0_attn_hidden_states"], np.float32)[0]
    xT = np.ascontiguousarray(x.T)
    w1t = _tile_w(np.asarray(full["L0_attn_in_proj_w"], np.float32).T,
                  SLABS_W1, KT)
    w2t = _tile_w(np.asarray(full["L0_attn_out_proj_w"], np.float32).T,
                  SLABS_W2, 32)
    w3t = _tile_w(np.asarray(full["L0_ffn_w_gate_up"], np.float32).T,
                  SLABS_W3, KT)
    w4t = _tile_w(np.asarray(full["L0_ffn_w_down"], np.float32).T,
                  SLABS_W4, INTER // 128)
    lnw = np.ascontiguousarray(
        np.asarray(full["L0_attn_ln_weight"], np.float32).reshape(KT, 128).T)
    nw = np.ascontiguousarray(
        np.asarray(full["L0_attn_norm_weight"], np.float32).reshape(I // 128, 128).T)
    dexp = np.ascontiguousarray(
        np.repeat(np.asarray(full["L0_attn_D"], np.float32), PD)
        .reshape(I // 128, 128).T)
    cwf = np.asarray(full["L0_attn_conv_w"], np.float32)
    cw = np.ascontiguousarray(
        cwf.reshape(33, 128, 4).transpose(1, 0, 2).reshape(128, 33 * 4))
    cb = np.ascontiguousarray(
        np.asarray(full["L0_attn_conv_b"], np.float32).reshape(33, 128).T)
    dtb = np.asarray(full["L0_attn_dt_bias"], np.float32).reshape(NH, 1)
    alog = np.asarray(full["L0_attn_A_log"], np.float32).reshape(NH, 1)
    import ml_dtypes
    sidx = np.arange(128)[:, None]
    lidx = np.arange(CH)[None, :]
    tri0 = (sidx <= lidx).astype(ml_dtypes.bfloat16)
    tri1 = ((sidx + 128) <= lidx).astype(ml_dtypes.bfloat16)
    tri = np.ascontiguousarray(np.concatenate([tri0, tri1], axis=1))
    identf = np.eye(128, dtype=np.float32)
    identb = np.eye(128, dtype=ml_dtypes.bfloat16)

    shared = dict(w1t=w1t, w2t=w2t, w3t=w3t, w4t=w4t, lnw=lnw, nw=nw, dexp=dexp,
                  cw=cw, cb=cb, dtb=dtb, alog=alog, tri=tri, identf=identf,
                  identb=identb)
    in_maps = []
    for r in range(NCORES):
        lo = r * T - 3
        if lo < 0:
            xs = np.concatenate(
                [np.zeros((H, 3), np.float32), xT[:, 0:T]], axis=1)
        else:
            xs = xT[:, lo:lo + TH]
        m = dict(shared)
        m["xt"] = np.ascontiguousarray(
            xs.reshape(KT, 128, TH).transpose(1, 0, 2).reshape(128, KT * TH))
        m["rmask"] = (np.arange(NCORES) < r).astype(np.float32).reshape(1, NCORES)
        in_maps.append(m)
    return in_maps


def kernel(**inputs):
    in_maps = _host_inputs(inputs)
    nc = build_nc()
    res = run_bass_kernel_spmd(nc, in_maps, core_ids=list(range(NCORES)))
    full = np.concatenate([np.asarray(res.results[r]["out"]) for r in range(NCORES)],
                          axis=1)
    return np.ascontiguousarray(full.T).reshape(1, 8192, H).astype(np.float32)


# version 07
# speedup vs baseline: 1.3915x; 1.1564x over previous
import numpy as np
import concourse.bass as bass
import concourse.mybir as mybir
from concourse.bass_utils import run_bass_kernel_spmd

F32 = mybir.dt.float32
BF16 = mybir.dt.bfloat16
AF = mybir.ActivationFunctionType
ALU = mybir.AluOpType

NCORES = 8
H = 2048
I = 4096
NH = 64
PD = 64
NS = 64
CH = 256
CONV = 4224
PROJ = 8384
T = 1024
TH = T + 3
NCHK = T // CH
KT = H // 128
EPS = 1e-5
INTER = 8192

ENGINES = ("tensor", "vector", "scalar", "gpsimd", "sync")
V, SC, TE, GP, SY = "vector", "scalar", "tensor", "gpsimd", "sync"

SLABS_W1 = ([(8320, 64)]
            + [(o, 512) for o in range(I, 8192, 512)]
            + [(8192, 128)]
            + [(o, 512) for o in range(0, I, 512)])
SLABS_W2 = [(o, 256) for o in range(0, H, 256)]
SLABS_W3 = [x for gi in range(INTER // 512)
            for x in ((gi * 512, 512), (INTER + gi * 512, 512))]
SLABS_W4 = [(o, 128) for o in range(0, H, 128)]


def _slab_offsets(slabs, kb_n):
    offs = []
    t = 0
    for (o0, w) in slabs:
        offs.append(t)
        t += kb_n * w
    return offs, t


class Plan:
    def __init__(self):
        self.ops = {e: [] for e in ENGINES}
        self.cnt = {e: 0 for e in ENGINES}
        self.waited = {e: {} for e in ENGINES}
        self.dma_fills = {}
        self.sem_names = (["m_" + e for e in ENGINES]
                          + ["b_" + e for e in ENGINES])
        self.bcnt = {e: 0 for e in ENGINES}
        self.stopped = False

    def new_dma_sem(self, name):
        assert name not in self.dma_fills
        self.dma_fills[name] = 0
        self.sem_names.append(name)
        return name

    def _wait(self, eng, sem, val):
        if self.waited[eng].get(sem, 0) >= val:
            return
        self.waited[eng][sem] = val
        self.ops[eng].append(("wait", sem, val))

    def op(self, eng, fn, deps=(), inc=False):
        if self.stopped:
            return None
        for d in deps:
            if d is not None:
                self._wait(eng, d[0], d[1])
        tag = None
        if inc:
            self.cnt[eng] += 1
            tag = ("m_" + eng, self.cnt[eng])
        self.ops[eng].append(("op", fn, tag))
        return tag

    def dma(self, queue, fn, sem, deps=()):
        if self.stopped:
            return None
        for d in deps:
            if d is not None:
                self._wait(queue, d[0], d[1])
        self.dma_fills[sem] += 16
        tag = (sem, self.dma_fills[sem])
        self.ops[queue].append(("dma", fn, tag))
        return tag

    def barrier(self):
        if self.stopped:
            return
        marks = []
        for e in ENGINES:
            if e == SY:
                continue
            self.bcnt[e] += 1
            tag = ("b_" + e, self.bcnt[e])
            self.ops[e].append(("op", (lambda eng: eng.nop()), tag))
            marks.append(tag)
            if self.cnt[e] > 0:
                marks.append(("m_" + e, self.cnt[e]))
        for s, vv in self.dma_fills.items():
            if vv > 0:
                marks.append((s, vv))
        for e in ENGINES:
            for m in marks:
                self._wait(e, m[0], m[1])


def build_nc(stop_after=None):
    from contextlib import ExitStack
    nc = bass.Bass("TRN2", target_bir_lowering=False, debug=False)

    def din(name, shape, dt=F32):
        return nc.dram_tensor(name, shape, dt, kind="ExternalInput").ap()

    xt = din("xt", [128, KT * TH])
    _o1, _t1 = _slab_offsets(SLABS_W1, KT)
    _o2, _t2 = _slab_offsets(SLABS_W2, 32)
    _o3, _t3 = _slab_offsets(SLABS_W3, KT)
    _o4, _t4 = _slab_offsets(SLABS_W4, INTER // 128)
    w1t = din("w1t", [128, _t1])
    w2t = din("w2t", [128, _t2])
    w3t = din("w3t", [128, _t3])
    w4t = din("w4t", [128, _t4])
    W_OFFS = {"w1t": _o1, "w2t": _o2, "w3t": _o3, "w4t": _o4}
    lnw_c = din("lnw", [128, KT])
    nw_c = din("nw", [128, I // 128])
    dexp_c = din("dexp", [128, I // 128])
    cw_c = din("cw", [128, 33 * 4])
    cb_c = din("cb", [128, 33])
    dtb_c = din("dtb", [NH, 1])
    alog_c = din("alog", [NH, 1])
    tri_c = din("tri", [128, 2 * CH], BF16)
    identf_c = din("identf", [128, 128])
    identb_c = din("identb", [128, 128], BF16)
    rmask_c = din("rmask", [1, NCORES])

    out_hbm = nc.dram_tensor("out", [H, T], F32, kind="ExternalOutput").ap()

    def dint(name, shape, dt):
        return nc.dram_tensor(name, shape, dt, kind="Internal").ap()

    xbc_d = dint("xbc_d", [CONV, TH], BF16)
    sg_d = dint("sg_d", [I, T], BF16)
    xs_d = dint("xs_d", [128, NCHK * 33 * CH], BF16)
    r_d = dint("r_d", [H, T], F32)
    acum_d = dint("acum_dd", [NH, T], F32)
    st_d = dint("st_d", [NCHK * NH, I], BF16)
    ccin_d = dint("ccin_d", [NH, I + 1], F32)
    ccout_d = dint("ccout_d", [NCORES * NH, I + 1], F32)
    y_d = dint("y_d", [I, T], BF16)
    x1_d = dint("x1_d", [H, T], F32)
    h_d = dint("h_d", [128, 2 * 64 * 512], BF16)

    es0 = ExitStack()

    def sb(stack, name, shape, dt):
        return stack.enter_context(nc.sbuf_tensor(name, shape, dt))

    onesf = sb(es0, "onesf", [1, 128], F32)
    onescf = sb(es0, "onescf", [128, 1], F32)
    onesbc = sb(es0, "onesbc", [128, 1], BF16)
    lnw = sb(es0, "lnw_sb", [128, KT], F32)
    nw = sb(es0, "nw_sb", [128, I // 128], F32)
    dexp = sb(es0, "dexp_sb", [128, I // 128], F32)
    cw = sb(es0, "cw_sb", [128, 33 * 4], F32)
    cb = sb(es0, "cb_sb", [128, 33], F32)
    dtb = sb(es0, "dtb_sb", [NH, 1], F32)
    alog = sb(es0, "alog_sb", [NH, 1], F32)
    tri = sb(es0, "tri_sb", [128, 2 * CH], BF16)
    idf = sb(es0, "idf_sb", [128, 128], F32)
    idb = sb(es0, "idb_sb", [128, 128], BF16)
    rmask = sb(es0, "rmask_sb", [1, NCORES], F32)
    SLABW = 8192
    slab = [sb(es0, f"slab{i}", [128, SLABW], BF16) for i in range(2)]
    stage = [sb(es0, f"stage{i}", [128, 512], BF16) for i in range(4)]
    stagef = [sb(es0, f"stagef{i}", [128, 512], F32) for i in range(2)]
    dt_sb = sb(es0, "dt_sb", [NH, T], F32)
    rowtmp = sb(es0, "rowtmp", [1, 128], F32)

    ps_mm = [es0.enter_context(nc.psum_tensor(f"ps_mm{i}", [128, 512], F32))
             for i in range(4)]
    ps_e = es0.enter_context(nc.psum_tensor("ps_e", [128, 512], F32))
    ps_tpb = es0.enter_context(nc.psum_tensor("ps_tpb", [128, 1024], BF16))
    ps_tpf = es0.enter_context(nc.psum_tensor("ps_tpf", [128, 512], F32))
    ps_row = es0.enter_context(nc.psum_tensor("ps_row", [1, 512], F32))

    plan = Plan()
    _STOP = {'n': 0}

    def stage_end():
        _STOP['n'] += 1
        if stop_after is not None and _STOP['n'] >= stop_after:
            plan.stopped = True
    nds = plan.new_dma_sem
    d_const = nds("d_const"); d_xf = nds("d_xf")
    d_slab = [nds("d_slab0"), nds("d_slab1")]
    s_stg = [nds(f"s_stg{i}") for i in range(4)]
    s_sf = [nds(f"s_sf{i}") for i in range(2)]
    s_t2f = [nds(f"s_t2f{i}") for i in range(2)]
    s_acum = nds("s_acum"); s_st = nds("s_st")
    s_ccin = nds("s_ccin"); d_cc = nds("d_cc")
    d_xbct = [nds("d_xbct0"), nds("d_xbct1")]
    d_xsc = nds("d_xsc")
    d_abc = [nds(f"d_abc{i}") for i in range(4)]
    d_lq = nds("d_lq"); d_dt = nds("d_dt"); d_stl = nds("d_stl")
    s_y = [nds("s_y0"), nds("s_y1")]
    d_yt = [nds("d_yt0"), nds("d_yt1")]
    d_sgt = [nds("d_sgt0"), nds("d_sgt1")]
    d_x1t = [nds("d_x1t0"), nds("d_x1t1")]
    d_rt = [nds("d_rt0"), nds("d_rt1")]
    d_hblk = nds("d_hblk")

    def allw(sems_):
        return [(s, plan.dma_fills[s]) for s in sems_ if plan.dma_fills[s] > 0]

    MMB = {"i": 0}

    def next_bank():
        b = ps_mm[MMB["i"] % 4]
        MMB["i"] += 1
        return b

    bank_war = {}

    for dst, src in [(lnw, lnw_c), (nw, nw_c), (dexp, dexp_c), (cw, cw_c),
                     (cb, cb_c), (dtb, dtb_c), (alog, alog_c), (tri, tri_c),
                     (idf, identf_c), (idb, identb_c), (rmask, rmask_c)]:
        plan.dma(SY, (lambda e, d=dst, s=src: e.dma_start(out=d[:, :], in_=s[:, :])),
                 d_const)
    plan.op(V, lambda e: e.memset(onesf[:, :], 1.0), inc=True)
    plan.op(V, lambda e: e.memset(onescf[:, :], 1.0), inc=True)
    m_ones = plan.op(V, lambda e: e.memset(onesbc[:, :], 1.0), inc=True)

    es1 = ExitStack()
    xfull = sb(es1, "xfull", [128, KT * TH], F32)
    xsq = [sb(es1, f"xsq{i}", [128, TH], F32) for i in range(2)]
    srow = sb(es1, "srow", [1, TH], F32)
    sqrow = sb(es1, "sqrow", [1, TH], F32)
    mrow = sb(es1, "mrow", [1, TH], F32)
    varrow = sb(es1, "varrow", [1, TH], F32)
    arow = sb(es1, "arow", [1, TH], F32)
    aorow = sb(es1, "aorow", [1, TH], F32)
    mb = sb(es1, "mb", [128, TH], F32)
    ab = sb(es1, "ab", [128, TH], F32)
    aob = sb(es1, "aob", [128, TH], F32)
    t1f = sb(es1, "t1f", [128, TH], F32)
    t2f = [sb(es1, f"t2f{i}", [128, TH], F32) for i in range(2)]
    xb = sb(es1, "xb", [128, KT * TH], BF16)

    t_x = plan.dma(SY, lambda e: e.dma_start(out=xfull[:, :], in_=xt[:, :]), d_xf)

    TG = [(0, 512), (512, 512), (1024, 3)]
    sum_dst = [ps_row, ps_mm[0], ps_mm[1]]
    sq_dst = [ps_e, ps_mm[2], ps_mm[3]]
    sqmm_tag = {}
    last_mm = None
    for kt in range(KT):
        b = kt % 2
        sq_t = plan.op(
            SC, (lambda e, b=b, kt=kt: e.activation(
                xsq[b][:, :], xfull[:, kt * TH:(kt + 1) * TH], AF.Square)),
            deps=[t_x, sqmm_tag.get(kt - 2)], inc=True)
        for gi, (t0, tw) in enumerate(TG):
            plan.op(TE, (lambda e, kt=kt, gi=gi, t0=t0, tw=tw: e.matmul(
                sum_dst[gi][0:1, 0:tw], onescf[:, 0:1],
                xfull[:, kt * TH + t0: kt * TH + t0 + tw],
                start=(kt == 0), stop=(kt == KT - 1), skip_group_check=True)),
                deps=[t_x, m_ones])
            last_mm = plan.op(TE, (lambda e, b=b, gi=gi, t0=t0, tw=tw, kt=kt:
                e.matmul(
                    sq_dst[gi][0:1, 0:tw], onescf[:, 0:1], xsq[b][:, t0:t0 + tw],
                    start=(kt == 0), stop=(kt == KT - 1), skip_group_check=True)),
                deps=[sq_t, m_ones], inc=True)
        sqmm_tag[kt] = last_mm
    ev = []
    for gi, (t0, tw) in enumerate(TG):
        ev.append(plan.op(V, (lambda e, gi=gi, t0=t0, tw=tw: e.tensor_copy(
            srow[0:1, t0:t0 + tw], sum_dst[gi][0:1, 0:tw])), deps=[last_mm], inc=True))
        ev.append(plan.op(V, (lambda e, gi=gi, t0=t0, tw=tw: e.tensor_copy(
            sqrow[0:1, t0:t0 + tw], sq_dst[gi][0:1, 0:tw])), deps=[last_mm], inc=True))
        bank_war[sum_dst[gi].name] = ev[-2]
        bank_war[sq_dst[gi].name] = ev[-1]
    m1 = plan.op(V, lambda e: e.tensor_scalar(mrow[:, :], srow[:, :], 1.0 / H, None,
                                              ALU.mult), deps=ev, inc=True)
    m2t = plan.op(V, lambda e: e.tensor_mul(varrow[:, :], mrow[:, :], mrow[:, :]),
                  deps=[m1], inc=True)
    m3 = plan.op(V, lambda e: e.scalar_tensor_tensor(
        varrow[:, :], sqrow[:, :], 1.0 / H, varrow[:, :], ALU.mult, ALU.subtract),
        deps=[m2t], inc=True)
    a0 = plan.op(V, lambda e: e.tensor_scalar(arow[:, :], varrow[:, :], 1.0 + EPS,
                                              EPS * EPS, ALU.mult, ALU.add),
                 deps=[m3], inc=True)
    a1 = plan.op(SC, lambda e: e.activation(arow[:, :], arow[:, :], AF.Ln),
                 deps=[a0], inc=True)
    a1b = plan.op(V, lambda e: e.tensor_scalar(arow[:, :], arow[:, :], -0.5, None,
                                               ALU.mult), deps=[a1], inc=True)
    a2 = plan.op(SC, lambda e: e.activation(arow[:, :], arow[:, :], AF.Exp),
                 deps=[a1b], inc=True)
    a2b = plan.op(V, lambda e: e.tensor_scalar(aorow[:, :], varrow[:, :], 1.0, EPS,
                                               ALU.mult, ALU.add), deps=[m3], inc=True)
    a3 = plan.op(SC, lambda e: e.activation(aorow[:, :], aorow[:, :], AF.Ln),
                 deps=[a2b], inc=True)
    a3b = plan.op(V, lambda e: e.tensor_scalar(aorow[:, :], aorow[:, :], -0.5, None,
                                               ALU.mult), deps=[a3], inc=True)
    a4 = plan.op(SC, lambda e: e.activation(aorow[:, :], aorow[:, :], AF.Exp),
                 deps=[a3b], inc=True)

    def bcast_row(src_row, dst_tile, dep):
        tags = []
        for (t0, tw) in TG:
            bk = next_bank()
            mt = plan.op(TE, (lambda e, bk=bk, t0=t0, tw=tw, s=src_row: e.matmul(
                bk[:, 0:tw], onesf[0:1, :], s[0:1, t0:t0 + tw],
                start=True, stop=True, skip_group_check=True)),
                deps=[dep, m_ones, bank_war.get(bk.name)], inc=True)
            et = plan.op(V, (lambda e, bk=bk, t0=t0, tw=tw, d=dst_tile: e.tensor_copy(
                d[:, t0:t0 + tw], bk[:, 0:tw])), deps=[mt], inc=True)
            bank_war[bk.name] = et
            tags.append(et)
        return tags

    mb_t = bcast_row(mrow, mb, m1)
    ab_t = bcast_row(arow, ab, a2)
    aob_t = bcast_row(aorow, aob, a4)

    r_tags = []
    xb_tags = []
    prev_dma = {}
    for kt in range(KT):
        b = kt % 2
        tt1 = plan.op(V, (lambda e, kt=kt: e.tensor_sub(
            t1f[:, :], xfull[:, kt * TH:(kt + 1) * TH], mb[:, :])),
            deps=[t_x] + mb_t, inc=True)
        xb_tags.append(plan.op(V, (lambda e, kt=kt: e.scalar_tensor_tensor(
            xb[:, kt * TH:(kt + 1) * TH], ab[:, :], lnw[:, kt:kt + 1], t1f[:, :],
            ALU.mult, ALU.mult)), deps=[tt1] + ab_t + [("d_const", 16 * 11)],
            inc=True))
        tt2 = plan.op(V, (lambda e, kt=kt, b=b: e.tensor_mul(
            t2f[b][:, :], t1f[:, :], aob[:, :])),
            deps=[tt1, prev_dma.get(b)] + aob_t, inc=True)
        tt3 = plan.op(V, (lambda e, kt=kt, b=b: e.tensor_add(
            t2f[b][:, :], t2f[b][:, :], xfull[:, kt * TH:(kt + 1) * TH])),
            deps=[tt2], inc=True)
        prev_dma[b] = plan.dma(SY, (lambda e, kt=kt, b=b: e.dma_start(
            out=r_d[kt * 128:(kt + 1) * 128, :], in_=t2f[b][:, 3:TH])),
            s_t2f[b], deps=[tt3])
        r_tags.append(prev_dma[b])

    stage_end()

    def mm_stage(w_hbm, kb_n, slabs, rhs_fn, evac_fn, extra_dep=None,
                 slab_offs=None, rhs_dep=None):
        slab_last_use = [None, None]
        slab_tag = [None, None]
        for si, (o0, wdt, meta) in enumerate(slabs):
            sbuf_i = si % 2
            soff = slab_offs[si]
            nsub = wdt // 128 if wdt >= 128 else 1
            slab_tag[sbuf_i] = plan.dma(GP, (lambda e, soff=soff, wdt=wdt, i=sbuf_i:
                e.dma_start(
                    out=slab[i][:, 0:kb_n * wdt],
                    in_=w_hbm[:, soff:soff + kb_n * wdt])),
                d_slab[sbuf_i], deps=[slab_last_use[sbuf_i], extra_dep])
            tgrps = meta["tgrps"]
            for osub in range(nsub):
                ow = min(128, wdt)
                for (t0, tw) in tgrps:
                    bk = next_bank()
                    mt = None
                    for kb in range(kb_n):
                        mt = plan.op(TE, (lambda e, bk=bk, kb=kb, osub=osub, ow=ow,
                                          wdt=wdt, t0=t0, tw=tw, i=sbuf_i: e.matmul(
                            bk[0:ow, 0:tw],
                            slab[i][:, kb * wdt + osub * 128: kb * wdt + osub * 128 + ow],
                            rhs_fn(kb, t0, tw),
                            start=(kb == 0), stop=(kb == kb_n - 1),
                            skip_group_check=True)),
                            deps=([slab_tag[sbuf_i], bank_war.get(bk.name)]
                                  if kb == 0 else [])
                                 + ([rhs_dep(kb)] if rhs_dep is not None else []),
                            inc=(kb == kb_n - 1))
                    slab_last_use[sbuf_i] = mt
                    war = evac_fn(meta, o0 + osub * 128, t0, tw, bk, mt)
                    bank_war[bk.name] = war
        return mt

    stg_war = [None] * 4
    STG = {"i": 0}

    def next_stage():
        s = STG["i"] % 4
        STG["i"] += 1
        return s

    def rhs_xb(kb, t0, tw):
        return xb[:, kb * TH + t0: kb * TH + t0 + tw]

    dt_tags = []
    xbc_tile_tags = {}

    def evac_inproj(meta, o_abs, t0, tw, bk, mt):
        kind = meta["kind"]
        if kind == "dt":
            t = plan.op(V, (lambda e, t0=t0, tw=tw, bk=bk: e.tensor_copy(
                dt_sb[:, t0 - 3: t0 - 3 + tw], bk[0:NH, 0:tw])), deps=[mt], inc=True)
            dt_tags.append(t)
            return t
        s = next_stage()
        if kind == "gate":
            et = plan.op(SC, (lambda e, s=s, bk=bk, tw=tw: e.activation(
                stage[s][:, 0:tw], bk[:, 0:tw], AF.Silu)),
                deps=[mt, stg_war[s]], inc=True)
            dmat = plan.dma(SY, (lambda e, s=s, o=o_abs, t0=t0, tw=tw: e.dma_start(
                out=sg_d[o:o + 128, t0 - 3: t0 - 3 + tw], in_=stage[s][:, 0:tw])),
                s_stg[s], deps=[et])
        else:
            et = plan.op(V, (lambda e, s=s, bk=bk, tw=tw: e.tensor_copy(
                stage[s][:, 0:tw], bk[:, 0:tw])), deps=[mt, stg_war[s]], inc=True)
            dmat = plan.dma(SY, (lambda e, s=s, o=o_abs, t0=t0, tw=tw: e.dma_start(
                out=xbc_d[o - I:o - I + 128, t0: t0 + tw], in_=stage[s][:, 0:tw])),
                s_stg[s], deps=[et])
            xbc_tile_tags.setdefault((o_abs - I) // 128, []).append(dmat)
        stg_war[s] = dmat
        return et

    TG_MAIN = [(3, 512), (515, 512)]
    TG_FULL = [(0, 512), (512, 512), (1024, 3)]
    slabs1 = ([(8320, 64, {"kind": "dt", "tgrps": TG_MAIN})]
              + [(o, 512, {"kind": "xbc", "tgrps": TG_FULL}) for o in range(I, 8192, 512)]
              + [(8192, 128, {"kind": "xbc", "tgrps": TG_FULL})]
              + [(o, 512, {"kind": "gate", "tgrps": TG_MAIN}) for o in range(0, I, 512)])
    mm_stage(w1t, KT, slabs1, rhs_xb, evac_inproj,
             slab_offs=W_OFFS['w1t'], rhs_dep=lambda kb: xb_tags[kb])

    stage_end()
    es1.close()

    es2 = ExitStack()
    cd = sb(es2, "cd", [NH, NCHK], F32)
    dtot = sb(es2, "dtot", [NH, 1], F32)
    decb = sb(es2, "decb", [NH, NCHK * NH], F32)
    alcol = sb(es2, "alcol", [128, 8 * NH], F32)
    lndtT = sb(es2, "lndtT", [128, 8 * NH], F32)
    wcol = sb(es2, "wcol", [128, 8 * NH], F32)
    bfm = sb(es2, "bfm", [NS, T], BF16)
    cfm = sb(es2, "cfm", [NS, T], BF16)
    smask = sb(es2, "smask", [128, NCHK * 2 * CH], BF16)
    es2a = ExitStack()
    dts = sb(es2a, "dts", [NH, T], F32)
    lndt = sb(es2a, "lndt", [NH, T], F32)
    dA = sb(es2a, "dA", [NH, T], F32)
    acum = sb(es2a, "acum_sb", [NH, T], F32)
    alf = sb(es2a, "alf", [NH, T], F32)
    tmpa = sb(es2a, "tmpa", [NH, T], F32)
    zero64 = sb(es2a, "zero64", [NH, CH], F32)
    expa = sb(es2a, "expa", [NH, 1], F32)

    zt = plan.op(V, lambda e: e.memset(zero64[:, :], 0.0), inc=True)
    u1 = plan.op(V, lambda e: e.tensor_scalar(tmpa[:, :], dt_sb[:, :], dtb[:, 0:1],
                                              None, ALU.add),
                 deps=dt_tags + [("d_const", 16 * 11)], inc=True)
    u2 = plan.op(V, lambda e: e.scalar_tensor_tensor(lndt[:, :], tmpa[:, :], -1.0,
                                                     tmpa[:, :], ALU.mult, ALU.min),
                 deps=[u1], inc=True)
    u3 = plan.op(SC, lambda e: e.activation(lndt[:, :], lndt[:, :], AF.Exp),
                 deps=[u2], inc=True)
    u4 = plan.op(SC, lambda e: e.activation(lndt[:, :], lndt[:, :], AF.Ln, bias=1.0),
                 deps=[u3], inc=True)
    u5 = plan.op(V, lambda e: e.tensor_scalar(dA[:, :], tmpa[:, :], 0.0, None, ALU.max),
                 deps=[u2], inc=True)
    u6 = plan.op(V, lambda e: e.tensor_add(dts[:, :], lndt[:, :], dA[:, :]),
                 deps=[u4, u5], inc=True)
    u7 = plan.op(SC, lambda e: e.activation(lndt[:, :], dts[:, :], AF.Ln),
                 deps=[u6], inc=True)
    u8 = plan.op(SC, lambda e: e.activation(expa[:, :], alog[:, 0:1], AF.Exp),
                 deps=[("d_const", 16 * 11)], inc=True)
    u9 = plan.op(V, lambda e: e.tensor_scalar(dA[:, :], dts[:, :], expa[:, 0:1], -1.0,
                                              ALU.mult, ALU.mult), deps=[u6, u8],
                 inc=True)
    sc_t = []
    for c in range(NCHK):
        sc_t.append(plan.op(V, (lambda e, c=c: e.tensor_tensor_scan(
            acum[:, c * CH:(c + 1) * CH], dA[:, c * CH:(c + 1) * CH], zero64[:, :],
            0.0, ALU.add, ALU.add)), deps=[u9, zt], inc=True))
    al_t = plan.op(V, lambda e: e.tensor_sub(alf[:, :], acum[:, :], lndt[:, :]),
                   deps=sc_t + [u7], inc=True)
    acum_dma = plan.dma(SY, lambda e: e.dma_start(out=acum_d[:, :], in_=acum[:, :]),
                        s_acum, deps=sc_t)
    cd_t = plan.op(SC, lambda e: e.activation(
        cd[:, :], acum.ap()[:, CH - 1::CH], AF.Exp), deps=sc_t, inc=True)
    d1 = plan.op(V, lambda e: e.tensor_mul(dtot[:, :], cd[:, 0:1], cd[:, 1:2]),
                 deps=[cd_t], inc=True)
    d2 = plan.op(V, lambda e: e.tensor_mul(dtot[:, :], dtot[:, :], cd[:, 2:3]),
                 deps=[d1], inc=True)
    dtot_t = plan.op(V, lambda e: e.tensor_mul(dtot[:, :], dtot[:, :], cd[:, 3:4]),
                     deps=[d2], inc=True)

    tp_war = [None]

    def transpose_f32(src_ap, np_, dst_ap, dep):
        mt = plan.op(TE, (lambda e: e.transpose(
            ps_tpf[:, 0:np_], src_ap, idf[0:np_, 0:np_])),
            deps=[dep, tp_war[0], ("d_const", 16 * 11)], inc=True)
        et = plan.op(V, (lambda e: e.tensor_copy(
            dst_ap, ps_tpf[:, 0:np_])), deps=[mt], inc=True)
        tp_war[0] = et
        return et

    w3_tags = []
    for c in range(NCHK):
        w1 = plan.op(V, (lambda e, c=c: e.tensor_scalar(
            tmpa[:, c * CH:(c + 1) * CH], acum[:, c * CH:(c + 1) * CH],
            acum[:, (c + 1) * CH - 1:(c + 1) * CH], -1.0, ALU.subtract, ALU.mult)),
            deps=sc_t, inc=True)
        w2 = plan.op(V, (lambda e, c=c: e.tensor_add(
            tmpa[:, c * CH:(c + 1) * CH], tmpa[:, c * CH:(c + 1) * CH],
            lndt[:, c * CH:(c + 1) * CH])), deps=[w1, u7], inc=True)
        w3 = plan.op(SC, (lambda e, c=c: e.activation(
            tmpa[:, c * CH:(c + 1) * CH], tmpa[:, c * CH:(c + 1) * CH], AF.Exp)),
            deps=[w2], inc=True)
        w3_tags.append(w3)
    stage_end()


    es3 = ExitStack()
    xbct = [sb(es3, f"xbct{i}", [128, TH], BF16) for i in range(2)]
    cacc = [sb(es3, f"cacc{i}", [128, T], F32) for i in range(2)]
    conv_last = [None, None]
    acc_war = [None, None]
    for it in range(33):
        b = it % 2
        ld = plan.dma(SY, (lambda e, it=it, b=b: e.dma_start(
            out=xbct[b][:, :], in_=xbc_d[it * 128:(it + 1) * 128, :])),
            d_xbct[b], deps=[conv_last[b]] + xbc_tile_tags[it])
        a = b
        t = plan.op(V, (lambda e, b=b, a=a, it=it: e.tensor_scalar(
            cacc[a][:, :], xbct[b][:, 0:T], cw[:, it * 4:it * 4 + 1], None,
            ALU.mult)), deps=[ld, acc_war[a], ("d_const", 16 * 11)], inc=True)
        for k in range(1, 4):
            t = plan.op(V, (lambda e, b=b, a=a, it=it, k=k: e.scalar_tensor_tensor(
                cacc[a][:, :], xbct[b][:, k:T + k], cw[:, it * 4 + k:it * 4 + k + 1],
                cacc[a][:, :], ALU.mult, ALU.add)), deps=[t], inc=True)
        conv_last[b] = t
        sil = []
        for s2 in range(2):
            ss = next_stage()
            et = plan.op(SC, (lambda e, a=a, s2=s2, it=it, ss=ss: e.activation(
                stage[ss][:, :], cacc[a][:, s2 * 512:(s2 + 1) * 512], AF.Silu,
                bias=cb[:, it:it + 1])), deps=[t, stg_war[ss]], inc=True)
            W33 = 33 * CH
            dmat = None
            for half in range(2):
                cc_ = s2 * 2 + half
                dmat = plan.dma(SY, (lambda e, it=it, ss=ss, cc_=cc_, half=half,
                                     W33=W33: e.dma_start(
                    out=xs_d[:, cc_ * W33 + it * CH: cc_ * W33 + (it + 1) * CH],
                    in_=stage[ss][:, half * CH:(half + 1) * CH])),
                    s_stg[ss], deps=[et])
            stg_war[ss] = dmat
            sil.append(et)
            if it == 32:
                b1 = plan.op(V, (lambda e, s2=s2, ss=ss: e.tensor_copy(
                    bfm[:, s2 * 512:(s2 + 1) * 512], stage[ss][0:NS, :])),
                    deps=[et], inc=True)
                c1 = plan.op(V, (lambda e, s2=s2, ss=ss: e.tensor_copy(
                    cfm[:, s2 * 512:(s2 + 1) * 512], stage[ss][NS:128, :])),
                    deps=[et], inc=True)
                stg_war[ss] = c1
        acc_war[a] = sil[-1]

    for tt in range(8):
        transpose_f32(alf[:, tt * 128:(tt + 1) * 128], NH,
                      alcol[:, tt * NH:(tt + 1) * NH], al_t)
        transpose_f32(lndt[:, tt * 128:(tt + 1) * 128], NH,
                      lndtT[:, tt * NH:(tt + 1) * NH], u7)
    for c in range(NCHK):
        for s in range(2):
            tt = 2 * c + s
            transpose_f32(tmpa[:, tt * 128:(tt + 1) * 128], NH,
                          wcol[:, tt * NH:(tt + 1) * NH], w3_tags[c])
    for c in range(NCHK):
        mt = plan.op(TE, (lambda e, c=c: e.transpose(
            ps_tpf[0:1, 0:NH], cd[:, c:c + 1], idf[0:NH, 0:NH])),
            deps=[cd_t, tp_war[0]], inc=True)
        et = plan.op(V, (lambda e: e.tensor_copy(
            rowtmp[0:1, 0:NH], ps_tpf[0:1, 0:NH])),
            deps=[mt], inc=True)
        tp_war[0] = et
        bk = next_bank()
        m2 = plan.op(TE, (lambda e, bk=bk: e.matmul(
            bk[0:NH, 0:NH], onesf[0:1, 0:NH], rowtmp[0:1, 0:NH],
            start=True, stop=True, skip_group_check=True)),
            deps=[et, m_ones, bank_war.get(bk.name)], inc=True)
        e2 = plan.op(V, (lambda e, c=c, bk=bk: e.tensor_copy(
            decb[:, c * NH:(c + 1) * NH], bk[0:NH, 0:NH])), deps=[m2], inc=True)
        bank_war[bk.name] = e2


    plan.barrier()
    stage_end()
    es3.close()
    es2a.close()

    es4 = ExitStack()
    xs_c = sb(es4, "xs_c", [128, 32 * CH], BF16)
    xsT = sb(es4, "xsT", [128, NCHK * 2 * I], BF16)
    btc = sb(es4, "btc", [128, 2 * NS], BF16)
    bd = [sb(es4, f"bd{i}", [128, 2 * NS], BF16) for i in range(2)]
    es4s = ExitStack()
    ststage = sb(es4s, "ststage", [NH, I], F32)
    carry = [sb(es4s, "carry0", [NH, I], F32)]

    cz = plan.op(V, lambda e: e.memset(carry[0][:, :], 0.0), inc=True)
    bd_war = [None, None]
    tpb_war = [None]

    def transpose_b(src_ap, np_, dst_ap, dep):
        mt = plan.op(TE, (lambda e: e.transpose(
            ps_tpb[:, 0:np_], src_ap, idb[0:np_, 0:np_])),
            deps=[dep, tpb_war[0]], inc=True)
        et = plan.op(V, (lambda e: e.tensor_copy(
            dst_ap, ps_tpb[:, 0:np_])), deps=[mt], inc=True)
        tpb_war[0] = et
        return et

    def load_xs_chunk(c, war, dsem):
        W33 = 33 * CH
        return plan.dma(SY, (lambda e, c=c, W33=W33: e.dma_start(
            out=xs_c[:, :],
            in_=xs_d[:, c * W33: c * W33 + 32 * CH])),
            dsem, deps=[war] + allw(s_stg))

    def build_xsT(c, dep_load):
        tags = []
        base = c * 2 * I
        for s in range(2):
            for it0 in range(0, 32, 4):
                mt = None
                for q in range(4):
                    it = it0 + q
                    mt = plan.op(TE, (lambda e, s=s, it=it, q=q: e.transpose(
                        ps_tpb[:, q * 128:(q + 1) * 128],
                        xs_c[:, it * CH + s * 128: it * CH + s * 128 + 128],
                        idb[0:128, 0:128])),
                        deps=[dep_load] + ([tpb_war[0]] if q == 0 else []),
                        inc=(q == 3))
                et = plan.op(V, (lambda e, s=s, it0=it0, base=base: e.tensor_copy(
                    xsT[:, base + s * I + it0 * 128: base + s * I + it0 * 128 + 512],
                    ps_tpb[:, 0:512])), deps=[mt], inc=True)
                tpb_war[0] = et
                tags.append(et)
        return tags

    xs_war = None
    scan_done = []
    stev_prev = None
    st_dma_last = None
    for c in range(NCHK):
        xl = load_xs_chunk(c, xs_war, d_xsc)
        xst_tags = build_xsT(c, xl)
        xs_war = xst_tags[-1]
        for s in range(2):
            bk = next_bank()
            mt = plan.op(TE, (lambda e, c=c, s=s, bk=bk: e.matmul(
                bk[:, 0:CH], bfm[:, c * CH + s * 128: c * CH + s * 128 + 128],
                cfm[:, c * CH:(c + 1) * CH], start=True, stop=True,
                skip_group_check=True)),
                deps=[bank_war.get(bk.name)], inc=True)
            et = plan.op(V, (lambda e, c=c, s=s, bk=bk: e.tensor_tensor(
                smask[:, (2 * c + s) * CH:(2 * c + s + 1) * CH], bk[:, 0:CH],
                tri[:, s * CH:(s + 1) * CH], ALU.mult)), deps=[mt], inc=True)
            bank_war[bk.name] = et
        bt_tags = []
        for s in range(2):
            bt_tags.append(transpose_b(
                bfm[:, c * CH + s * 128: c * CH + s * 128 + 128], NS,
                btc[:, s * NS:(s + 1) * NS], dep=None))
        for hg in range(8):
            mm_last = None
            for hh in range(8):
                h = hg * 8 + hh
                bdi = hh % 2
                bdt = []
                for s in range(2):
                    tt = 2 * c + s
                    bdt.append(plan.op(V, (lambda e, h=h, s=s, tt=tt, bdi=bdi:
                        e.tensor_scalar(
                            bd[bdi][:, s * NS:(s + 1) * NS],
                            btc[:, s * NS:(s + 1) * NS],
                            wcol[:, tt * NH + h: tt * NH + h + 1], None, ALU.mult)),
                        deps=[bt_tags[s], bd_war[bdi]], inc=True))
                for s in range(2):
                    mm_last = plan.op(TE, (lambda e, c=c, h=h, hh=hh, s=s, bdi=bdi:
                        e.matmul(
                            ps_e[0:NS, hh * PD:(hh + 1) * PD],
                            bd[bdi][:, s * NS:(s + 1) * NS],
                            xsT[:, c * 2 * I + s * I + h * PD:
                                c * 2 * I + s * I + (h + 1) * PD],
                            start=(hh == 0 and s == 0), stop=(hh == 7 and s == 1),
                            skip_group_check=True)),
                        deps=[bdt[s]] + ([xs_war, stev_prev] if hh == 0 and s == 0
                                         else []),
                        inc=True)
                bd_war[bdi] = mm_last
            stev_prev = plan.op(V, (lambda e, hg=hg: e.tensor_copy(
                ststage[:, hg * 512:(hg + 1) * 512], ps_e[0:NH, :])),
                deps=[mm_last, st_dma_last], inc=True)
        st_dma = plan.dma(GP, (lambda e, c=c: e.dma_start(
            out=st_d[c * NH:(c + 1) * NH, :], in_=ststage[:, :])),
            s_st, deps=[stev_prev])
        st_dma_last = st_dma
        src = carry[0]
        dst = carry[0]
        last = None
        for h in range(NH):
            last = plan.op(V, (lambda e, c=c, h=h, src=src, dst=dst:
                e.scalar_tensor_tensor(
                    dst[:, h * PD:(h + 1) * PD], src[:, h * PD:(h + 1) * PD],
                    decb[:, c * NH + h: c * NH + h + 1],
                    ststage[:, h * PD:(h + 1) * PD], ALU.mult, ALU.add)),
                deps=[stev_prev, st_dma] + ([cz] if c == 0 else [scan_done[-1]]),
                inc=True)
        scan_done.append(last)

    cc1 = plan.dma(SY, lambda e: e.dma_start(out=ccin_d[:, 0:I], in_=carry[0][:, :]),
                   s_ccin, deps=[scan_done[-1]])
    cc2 = plan.dma(SY, lambda e: e.dma_start(out=ccin_d[:, I:I + 1], in_=dtot[:, :],
                                             allow_slow_non_contiguous=True),
                   s_ccin, deps=[dtot_t])
    es4s.close()

    plan.barrier()
    stage_end()

    def do_cc(e):
        e.collective_compute(
            "AllGather", ALU.bypass, replica_groups=[list(range(NCORES))],
            ins=[ccin_d.opt()], outs=[ccout_d.opt()]).then_inc(_SEMS["d_cc"])
        return None
    plan.op(GP, do_cc, deps=[(s_ccin, plan.dma_fills[s_ccin])])
    CC_TAG = ("d_cc", 1)

    es4b = ExitStack()
    lq_st = sb(es4b, "lq_st", [NH, I], BF16)
    dtall = sb(es4b, "dtall", [NH, NCORES], F32)
    dmall = sb(es4b, "dmall", [NH, NCORES], F32)
    wfin = sb(es4b, "wfin", [NH, NCORES], F32)
    rmb = sb(es4b, "rmb", [NH, NCORES], F32)
    wbq = sb(es4b, "wbq", [NH, NH], F32)
    prevb = sb(es4b, "prevb", [NH, I], F32)
    prevbb = sb(es4b, "prevbb", [NH, I], BF16)
    stldb = sb(es4b, "stldb", [NH, I], BF16)

    pz = plan.op(V, lambda e: e.memset(prevb[:, :], 0.0), inc=True)
    dt_tags = []
    for q in range(NCORES):
        dt_tags.append(plan.dma(SY, (lambda e, q=q: e.dma_start(
            out=dtall[:, q:q + 1], in_=ccout_d[q * NH:(q + 1) * NH, I:I + 1],
            allow_slow_non_contiguous=True)),
            d_dt, deps=[CC_TAG]))
    dt_all = dt_tags[-1]
    bk = next_bank()
    mt = plan.op(TE, (lambda e, bk=bk: e.matmul(
        bk[0:NH, 0:NCORES], onesf[0:1, 0:NH], rmask[0:1, :], start=True, stop=True,
        skip_group_check=True)), deps=[bank_war.get(bk.name), m_ones,
                                       ("d_const", 16 * 11)], inc=True)
    rmb_t = plan.op(V, (lambda e, bk=bk: e.tensor_copy(rmb[:, :], bk[0:NH, 0:NCORES])),
                    deps=[mt], inc=True)
    bank_war[bk.name] = rmb_t
    q1 = plan.op(V, lambda e: e.tensor_scalar(dmall[:, :], dtall[:, :], -1.0, None,
                                              ALU.add), deps=[dt_all], inc=True)
    q2 = plan.op(V, lambda e: e.tensor_mul(dmall[:, :], dmall[:, :], rmb[:, :]),
                 deps=[q1, rmb_t], inc=True)
    q3 = plan.op(V, lambda e: e.tensor_scalar(dmall[:, :], dmall[:, :], 1.0, None,
                                              ALU.add), deps=[q2], inc=True)
    wt = plan.op(V, lambda e: e.memset(wfin[:, NCORES - 1:NCORES], 1.0), inc=True)
    for q in range(NCORES - 2, -1, -1):
        wt = plan.op(V, (lambda e, q=q: e.tensor_mul(
            wfin[:, q:q + 1], wfin[:, q + 1:q + 2], dmall[:, q + 1:q + 2])),
            deps=[wt, q3], inc=True)
    wt = plan.op(V, lambda e: e.tensor_mul(wfin[:, :], wfin[:, :], rmb[:, :]),
                 deps=[wt], inc=True)
    lq_war = None
    for q in range(NCORES):
        mt = plan.op(TE, (lambda e, q=q: e.transpose(
            ps_tpf[0:1, 0:NH], wfin[:, q:q + 1], idf[0:NH, 0:NH])),
            deps=[wt, tp_war[0]], inc=True)
        et = plan.op(V, (lambda e: e.tensor_copy(
            rowtmp[0:1, 0:NH], ps_tpf[0:1, 0:NH])), deps=[mt], inc=True)
        tp_war[0] = et
        bk = next_bank()
        m2 = plan.op(TE, (lambda e, bk=bk: e.matmul(
            bk[0:NH, 0:NH], onesf[0:1, 0:NH], rowtmp[0:1, 0:NH], start=True,
            stop=True, skip_group_check=True)),
            deps=[et, m_ones, bank_war.get(bk.name)], inc=True)
        wb_t = plan.op(V, (lambda e, bk=bk: e.tensor_copy(wbq[:, :], bk[0:NH, 0:NH])),
                       deps=[m2], inc=True)
        bank_war[bk.name] = wb_t
        lq_t = plan.dma(GP, (lambda e, q=q: e.dma_start(
            out=lq_st[:, :], in_=ccout_d[q * NH:(q + 1) * NH, 0:I])), d_lq,
            deps=[CC_TAG, lq_war])
        last = None
        for h in range(NH):
            last = plan.op(V, (lambda e, h=h: e.scalar_tensor_tensor(
                prevb[:, h * PD:(h + 1) * PD], lq_st[:, h * PD:(h + 1) * PD],
                wbq[:, h:h + 1], prevb[:, h * PD:(h + 1) * PD], ALU.mult, ALU.add)),
                deps=[lq_t, wb_t, pz], inc=True)
        lq_war = last

    plan.barrier()
    stage_end()

    es4c = ExitStack()
    abc = [sb(es4c, f"abc{i}", [128, CH], F32) for i in range(4)]
    dc_sb = [sb(es4c, f"dc{i}", [128, CH], F32) for i in range(6)]
    lb_sb = [sb(es4c, f"lb{i}", [128, CH], BF16) for i in range(6)]
    pb_sb = [sb(es4c, f"pb{i}", [128, CH], BF16) for i in range(6)]
    csd = [sb(es4c, f"csd{i}", [NS, CH], BF16) for i in range(4)]
    ysta = [sb(es4c, f"ysta{i}", [128, CH], BF16) for i in range(2)]

    prev_ready = plan.op(V, lambda e: e.tensor_copy(prevbb[:, :], prevb[:, :]),
                         inc=True)
    xs_war = None
    ysta_war = [None, None]
    abc_war = [None] * 4
    dc_war = [None] * 6
    pb_war = [None] * 6
    csd_war = [None] * 4
    offmm_last = None
    for c in range(NCHK):
        if c > 0:
            stl = plan.dma(SY, (lambda e, c=c: e.dma_start(
                out=stldb[:, :], in_=st_d[(c - 1) * NH:c * NH, :])), d_stl,
                deps=[(s_st, plan.dma_fills[s_st]), prev_ready])
            last = None
            for h in range(NH):
                last = plan.op(V, (lambda e, c=c, h=h: e.scalar_tensor_tensor(
                    prevb[:, h * PD:(h + 1) * PD], prevb[:, h * PD:(h + 1) * PD],
                    decb[:, (c - 1) * NH + h:(c - 1) * NH + h + 1],
                    stldb[:, h * PD:(h + 1) * PD], ALU.mult, ALU.add)),
                    deps=[stl, prev_ready], inc=True)
            prev_ready = plan.op(V, lambda e: e.tensor_copy(prevbb[:, :],
                                                            prevb[:, :]),
                                 deps=[last, offmm_last], inc=True)
        xl = load_xs_chunk(c, xs_war, d_xsc)
        last_et = None

        AH = 3

        def emit_abc(h, c=c):
            n = h % 4
            dq = SY if h % 2 == 0 else GP
            t = plan.dma(dq, (lambda e, c=c, h=h, n=n: e.dma_start(
                out=abc[n][:, :],
                in_=acum_d[h:h + 1, c * CH:(c + 1) * CH]
                    .partition_broadcast(128))),
                d_abc[n], deps=list(abc_war[n] or ()) + [(s_acum, 16)])
            return t

        abct = {}
        for h in range(AH):
            abct[h] = emit_abc(h)
        stA = {}
        stB = {}
        stC = {}
        stD = {}
        for hh in range(NH + 4):
            if hh < NH:
                h = hh
                n = h % 4
                a_list = []
                for s in range(2):
                    tt = 2 * c + s
                    r = (2 * h + s) % 6
                    t1 = plan.op(V, (lambda e, n=n, r=r, tt=tt, h=h:
                        e.tensor_scalar(
                            dc_sb[r][:, :], abc[n][:, :],
                            alcol[:, tt * NH + h: tt * NH + h + 1],
                            lndtT[:, tt * NH + h: tt * NH + h + 1],
                            ALU.subtract, ALU.min)),
                        deps=[abct[h], dc_war[r]], inc=True)
                    a_list.append((r, t1))
                b_list = []
                for (r, t1) in a_list:
                    t2 = plan.op(SC, (lambda e, r=r: e.activation(
                        lb_sb[r][:, :], dc_sb[r][:, :], AF.Exp)),
                        deps=[t1], inc=True)
                    dc_war[r] = t2
                    b_list.append((r, t2))
                cr = h % 4
                lvt = plan.op(SC, (lambda e, n=n, cr=cr: e.activation(
                    csd[cr][:, :], abc[n][0:NS, :], AF.Exp)),
                    deps=[abct[h], csd_war[cr]], inc=True)
                abc_war[n] = (a_list[-1][1], lvt)
                stA[h] = a_list
                stB[h] = (b_list, lvt)
                if h + AH < NH:
                    abct[h + AH] = emit_abc(h + AH)
            if 0 <= hh - 1 < NH:
                h = hh - 1
                cr = h % 4
                b_list, lvt = stB[h]
                c_list = []
                for (r, t2) in b_list:
                    s = len(c_list)
                    t3 = plan.op(V, (lambda e, r=r, c=c, s=s: e.tensor_tensor(
                        pb_sb[r][:, :], lb_sb[r][:, :],
                        smask[:, (2 * c + s) * CH:(2 * c + s + 1) * CH],
                        ALU.mult)), deps=[t2, pb_war[r]], inc=True)
                    c_list.append((r, t3))
                csd_t = plan.op(V, (lambda e, cr=cr, c=c: e.tensor_tensor(
                    csd[cr][:, :], csd[cr][:, :], cfm[:, c * CH:(c + 1) * CH],
                    ALU.mult)), deps=[lvt], inc=True)
                stC[h] = (c_list, csd_t)
                j = h // 2
                odd = h % 2
                yb = ps_mm[j % 4]
                tp = (0, 64) if odd else None
                out_sl = yb[64:128, 0:CH] if odd else yb[0:64, 0:CH]
                mms = []
                for s in range(2):
                    r, t3 = c_list[s]
                    mms.append(plan.op(TE, (lambda e, out_sl=out_sl, c=c, s=s,
                                            h=h, r=r, tp=tp: e.matmul(
                        out_sl,
                        xsT[:, c * 2 * I + s * I + h * PD:
                            c * 2 * I + s * I + (h + 1) * PD],
                        pb_sb[r][:, :], start=(s == 0), stop=False,
                        skip_group_check=True, tile_position=tp)),
                        deps=[t3] + ([bank_war.get(yb.name)]
                                     if s == 0 and not odd else []),
                        inc=True))
                    pb_war[r] = mms[-1]
                mms.append(plan.op(TE, (lambda e, out_sl=out_sl, h=h, cr=cr,
                                        tp=tp: e.matmul(
                    out_sl, prevbb[:, h * PD:(h + 1) * PD], csd[cr][:, :],
                    start=False, stop=True, skip_group_check=True,
                    tile_position=tp)),
                    deps=[csd_t, prev_ready], inc=True))
                csd_war[cr] = mms[-1]
                offmm_last = mms[-1]
                stD[h] = mms[-1]
            he = hh - 3
            if 0 <= he < NH and he % 2 == 1:
                j = he // 2
                yb = ps_mm[j % 4]
                yr = j % 2
                et = plan.op(V, (lambda e, yb=yb, j=j, yr=yr:
                    e.scalar_tensor_tensor(
                        ysta[yr][:, :], xs_c[:, j * CH:(j + 1) * CH],
                        dexp[:, j:j + 1], yb[:, 0:CH], ALU.mult, ALU.add)),
                    deps=[stD[he], ysta_war[yr], xl], inc=True)
                ydma = plan.dma(SY, (lambda e, j=j, c=c, yr=yr: e.dma_start(
                    out=y_d[j * 128:(j + 1) * 128, c * CH:(c + 1) * CH],
                    in_=ysta[yr][:, :])), s_y[yr], deps=[et])
                ysta_war[yr] = ydma
                bank_war[yb.name] = et
                last_et = et
        xs_war = last_et


    plan.barrier()
    stage_end()
    es4c.close()
    es4b.close()
    es4.close()
    es2.close()

    es5 = ExitStack()
    ytl = [sb(es5, f"ytl{i}", [128, 512], BF16) for i in range(2)]
    sgt = [sb(es5, f"sgt{i}", [128, 512], BF16) for i in range(2)]
    yp = sb(es5, "yp", [128, 32 * 512], BF16)
    ysq = [sb(es5, f"ysq{i}", [128, 512], BF16) for i in range(2)]
    rstdb = sb(es5, "rstdb", [128, 512], F32)
    rtile = [sb(es5, f"rtile{i}", [128, 512], F32) for i in range(2)]
    x1bf = [sb(es5, f"x1bf{i}", [128, 512], BF16) for i in range(2)]
    sum2row = sb(es5, "sum2row", [1, T], F32)
    sqr2row = sb(es5, "sqr2row", [1, T], F32)

    opj_last = None
    prev_r2 = None
    carry_sq = [None, None]
    opj_rt_war = {}
    opj_sf_war = {}
    opj_bf_war = {}
    opj_sq_done = {}
    for tb in range(2):
        ysq_war = list(carry_sq)
        sq_mm_last = None
        yp_tags = []
        for j in range(32):
            b = j % 2
            ylt = plan.dma(SY, (lambda e, j=j, tb=tb, b=b: e.dma_start(
                out=ytl[b][:, :], in_=y_d[j * 128:(j + 1) * 128,
                                          tb * 512:(tb + 1) * 512])),
                d_yt[b], deps=allw(s_y) + [yp_tags[-2] if j >= 2 else None])
            sgl = plan.dma(SY, (lambda e, j=j, tb=tb, b=b: e.dma_start(
                out=sgt[b][:, :], in_=sg_d[j * 128:(j + 1) * 128,
                                           tb * 512:(tb + 1) * 512])),
                d_sgt[b], deps=allw(s_stg) + [yp_tags[-2] if j >= 2 else None])
            ypt = plan.op(V, (lambda e, j=j, b=b: e.tensor_tensor(
                yp[:, j * 512:(j + 1) * 512], ytl[b][:, :], sgt[b][:, :], ALU.mult)),
                deps=[ylt, sgl, opj_last, prev_r2], inc=True)
            yp_tags.append(ypt)
            sqt = plan.op(SC, (lambda e, j=j, b=b: e.activation(
                ysq[b][:, :], yp[:, j * 512:(j + 1) * 512], AF.Square)),
                deps=[ypt, ysq_war[b]], inc=True)
            sq_mm_last = plan.op(TE, (lambda e, j=j, b=b: e.matmul(
                ps_row[0:1, :], onesbc[:, 0:1], ysq[b][:, :],
                start=(j == 0), stop=(j == 31), skip_group_check=True)),
                deps=[sqt, m_ones] + ([bank_war.get(ps_row.name)] if j == 0 else []),
                inc=True)
            ysq_war[b] = sq_mm_last
        r0 = plan.op(V, (lambda e, tb=tb: e.tensor_scalar(
            sum2row[0:1, tb * 512:(tb + 1) * 512], ps_row[0:1, :], 1.0 / I, EPS,
            ALU.mult, ALU.add)), deps=[sq_mm_last], inc=True)
        r1 = plan.op(SC, (lambda e, tb=tb: e.activation(
            sum2row[0:1, tb * 512:(tb + 1) * 512],
            sum2row[0:1, tb * 512:(tb + 1) * 512], AF.Ln)), deps=[r0], inc=True)
        r1b = plan.op(V, (lambda e, tb=tb: e.tensor_scalar(
            sum2row[0:1, tb * 512:(tb + 1) * 512],
            sum2row[0:1, tb * 512:(tb + 1) * 512], -0.5, None, ALU.mult)),
            deps=[r1], inc=True)
        r2 = plan.op(SC, (lambda e, tb=tb: e.activation(
            sum2row[0:1, tb * 512:(tb + 1) * 512],
            sum2row[0:1, tb * 512:(tb + 1) * 512], AF.Exp)), deps=[r1b], inc=True)
        bank_war[ps_row.name] = r0
        bk = next_bank()
        m3 = plan.op(TE, (lambda e, bk=bk, tb=tb: e.matmul(
            bk[:, :], onesf[0:1, :], sum2row[0:1, tb * 512:(tb + 1) * 512],
            start=True, stop=True, skip_group_check=True)),
            deps=[r2, m_ones, bank_war.get(bk.name)], inc=True)
        rb_t = plan.op(V, (lambda e, bk=bk: e.tensor_copy(rstdb[:, :], bk[:, :])),
                       deps=[m3], inc=True)
        bank_war[bk.name] = rb_t
        ypp_tags = []
        for j in range(32):
            ypp_tags.append(plan.op(V, (lambda e, j=j: e.scalar_tensor_tensor(
                yp[:, j * 512:(j + 1) * 512], yp[:, j * 512:(j + 1) * 512],
                nw[:, j:j + 1], rstdb[:, :], ALU.mult, ALU.mult)),
                deps=[yp_tags[j], rb_t], inc=True))
        ypp_all = ypp_tags[-1]

        def rhs_yp(kb, t0, tw):
            return yp[:, kb * 512: kb * 512 + 512]

        x1_evt = {"n": 0}

        def evac_outproj(meta, o_abs, t0, tw, bk2, mt2):
            tb_ = meta["tb"]
            b = x1_evt["n"] % 2
            x1_evt["n"] += 1
            rt = plan.dma(SY, (lambda e, o=o_abs, tb_=tb_, b=b: e.dma_start(
                out=rtile[b][:, :], in_=r_d[o:o + 128, tb_ * 512:(tb_ + 1) * 512])),
                d_rt[b], deps=allw(s_t2f) + [meta["rt_war"].get(b)])
            at = plan.op(V, (lambda e, bk2=bk2, b=b: e.tensor_add(
                stagef[b][:, :], bk2[:, :], rtile[b][:, :])),
                deps=[mt2, rt, meta["sf_war"].get(b), meta["sq_done"].get(b)],
                inc=True)
            meta["rt_war"][b] = at
            xd = plan.dma(SY, (lambda e, o=o_abs, tb_=tb_, b=b: e.dma_start(
                out=x1_d[o:o + 128, tb_ * 512:(tb_ + 1) * 512],
                in_=stagef[b][:, :])), s_sf[b], deps=[at])
            cst = plan.op(V, (lambda e, b=b: e.tensor_copy(
                x1bf[b][:, :], stagef[b][:, :])), deps=[at, meta["bf_war"].get(b)],
                inc=True)
            sq2 = plan.op(SC, (lambda e, b=b: e.activation(
                ysq[b][:, :], stagef[b][:, :], AF.Square)),
                deps=[at, meta["sq_war"].get(b)], inc=True)
            ki = meta["ki"]["n"]
            meta["ki"]["n"] += 1
            mm1 = plan.op(TE, (lambda e, b=b, ki=ki: e.matmul(
                ps_row[0:1, :], onesbc[:, 0:1], x1bf[b][:, :], start=(ki == 0),
                stop=(ki == 15), skip_group_check=True)),
                deps=[cst, m_ones] + ([bank_war.get(ps_row.name)] if ki == 0 else []),
                inc=True)
            mm2 = plan.op(TE, (lambda e, b=b, ki=ki: e.matmul(
                ps_e[0:1, :], onesbc[:, 0:1], ysq[b][:, :], start=(ki == 0),
                stop=(ki == 15), skip_group_check=True)),
                deps=[sq2, m_ones] + ([bank_war.get(ps_e.name)] if ki == 0 else []),
                inc=True)
            meta["bf_war"][b] = mm1
            meta["sq_war"][b] = mm2
            meta["sq_done"][b] = sq2
            meta["sf_war"][b] = xd
            meta["last_mm"] = (mm1, mm2, xd)
            return at

        meta2 = {"tb": tb, "tgrps": [(tb * 512, 512)], "rt_war": opj_rt_war,
                 "sf_war": opj_sf_war, "bf_war": opj_bf_war,
                 "sq_war": {0: sq_mm_last, 1: sq_mm_last},
                 "sq_done": opj_sq_done, "ki": {"n": 0}, "last_mm": None}
        slabs2 = [(o, 256, meta2) for o in range(0, H, 256)]
        opj_last = mm_stage(w2t, 32, slabs2, rhs_yp, evac_outproj,
                            extra_dep=ypp_all, slab_offs=W_OFFS['w2t'])
        prev_r2 = r2
        carry_sq = [meta2["sq_war"].get(0), meta2["sq_war"].get(1)]
        mm1, mm2, _ = meta2["last_mm"]
        ev1 = plan.op(V, (lambda e, tb=tb: e.tensor_copy(
            sum2row[0:1, tb * 512:(tb + 1) * 512], ps_row[0:1, :])),
            deps=[mm1], inc=True)
        ev2 = plan.op(V, (lambda e, tb=tb: e.tensor_copy(
            sqr2row[0:1, tb * 512:(tb + 1) * 512], ps_e[0:1, :])),
            deps=[mm2], inc=True)
        bank_war[ps_row.name] = ev1
        bank_war[ps_e.name] = ev2

    plan.barrier()
    stage_end()
    es5.close()

    es6 = ExitStack()
    m2row = sb(es6, "m2row", [1, T], F32)
    var2 = sb(es6, "var2", [1, T], F32)
    a2row = sb(es6, "a2row", [1, T], F32)
    mb2 = sb(es6, "mb2", [128, T], F32)
    ab2 = sb(es6, "ab2", [128, T], F32)
    x1t = [sb(es6, f"x1t{i}", [128, T], F32) for i in range(2)]
    jt = sb(es6, "jt", [128, T], F32)
    xb2 = sb(es6, "xb2", [128, KT * T], BF16)

    n1 = plan.op(V, lambda e: e.tensor_scalar(m2row[:, :], sum2row[:, :], 1.0 / H,
                                              None, ALU.mult), inc=True)
    n2 = plan.op(V, lambda e: e.tensor_mul(var2[:, :], m2row[:, :], m2row[:, :]),
                 deps=[n1], inc=True)
    n3 = plan.op(V, lambda e: e.scalar_tensor_tensor(
        var2[:, :], sqr2row[:, :], 1.0 / H, var2[:, :], ALU.mult, ALU.subtract),
        deps=[n2], inc=True)
    n3b = plan.op(V, lambda e: e.tensor_scalar(a2row[:, :], var2[:, :], 1.0, EPS,
                                               ALU.mult, ALU.add), deps=[n3],
                  inc=True)
    n4 = plan.op(SC, lambda e: e.activation(a2row[:, :], a2row[:, :], AF.Ln),
                 deps=[n3b], inc=True)
    n4b = plan.op(V, lambda e: e.tensor_scalar(a2row[:, :], a2row[:, :], -0.5, None,
                                               ALU.mult), deps=[n4], inc=True)
    n5 = plan.op(SC, lambda e: e.activation(a2row[:, :], a2row[:, :], AF.Exp),
                 deps=[n4b], inc=True)

    def bcast_row_t(src_row, dst_tile, dep):
        tags = []
        for t0 in (0, 512):
            bk = next_bank()
            mt = plan.op(TE, (lambda e, bk=bk, t0=t0, s=src_row: e.matmul(
                bk[:, :], onesf[0:1, :], s[0:1, t0:t0 + 512], start=True, stop=True,
                skip_group_check=True)), deps=[dep, m_ones, bank_war.get(bk.name)],
                inc=True)
            et = plan.op(V, (lambda e, bk=bk, t0=t0, d=dst_tile: e.tensor_copy(
                d[:, t0:t0 + 512], bk[:, :])), deps=[mt], inc=True)
            bank_war[bk.name] = et
            tags.append(et)
        return tags

    mb2_t = bcast_row_t(m2row, mb2, n1)
    ab2_t = bcast_row_t(a2row, ab2, n5)
    x1_war = [None, None]
    j2 = None
    for kt in range(KT):
        b = kt % 2
        xlt = plan.dma(SY, (lambda e, kt=kt, b=b: e.dma_start(
            out=x1t[b][:, :], in_=x1_d[kt * 128:(kt + 1) * 128, :])), d_x1t[b],
            deps=allw(s_sf) + [x1_war[b]])
        j1 = plan.op(V, (lambda e, b=b: e.tensor_sub(jt[:, :], x1t[b][:, :],
                                                     mb2[:, :])),
                     deps=[xlt, j2] + mb2_t, inc=True)
        j2 = plan.op(V, (lambda e, kt=kt: e.tensor_mul(
            xb2[:, kt * T:(kt + 1) * T], jt[:, :], ab2[:, :])),
            deps=[j1] + ab2_t, inc=True)
        x1_war[b] = j2

    plan.barrier()
    stage_end()

    es7 = ExitStack()
    sgf = sb(es7, "sgf", [128, 8 * 512], BF16)

    def rhs_xb2(kb, t0, tw):
        return xb2[:, kb * T + t0: kb * T + t0 + tw]

    sgf_war = {}

    def evac_ffn1(meta, o_abs, t0, tw, bk, mt):
        if meta["kind"] == "fg":
            gi = (o_abs % 512) // 128 * 2 + (t0 // 512)
            et = plan.op(SC, (lambda e, gi=gi, bk=bk: e.activation(
                sgf[:, gi * 512:(gi + 1) * 512], bk[:, :], AF.Silu)),
                deps=[mt, sgf_war.get(gi)], inc=True)
            sgf_war[gi] = et
            meta.setdefault("sg_tags", {})[gi] = et
            return et
        else:
            gi = (o_abs % 512) // 128 * 2 + (t0 // 512)
            gtag = meta["g_meta"]["sg_tags"][gi]
            s = next_stage()
            et = plan.op(V, (lambda e, gi=gi, bk=bk, s=s: e.tensor_tensor(
                stage[s][:, :], bk[:, :], sgf[:, gi * 512:(gi + 1) * 512], ALU.mult)),
                deps=[mt, gtag, stg_war[s]], inc=True)
            kb_ = (o_abs - INTER) // 128
            tb_ = t0 // 512
            dmat = plan.dma(SY, (lambda e, kb_=kb_, tb_=tb_, s=s: e.dma_start(
                out=h_d[:, tb_ * 32768 + kb_ * 512: tb_ * 32768 + (kb_ + 1) * 512],
                in_=stage[s][:, :])),
                s_stg[s], deps=[et])
            stg_war[s] = dmat
            sgf_war[gi] = et
            return et

    TG_T = [(0, 512), (512, 512)]
    slabs3 = []
    for gi in range(INTER // 512):
        mg = {"kind": "fg", "tgrps": TG_T}
        mu = {"kind": "fu", "tgrps": TG_T, "g_meta": mg}
        slabs3.append((gi * 512, 512, mg))
        slabs3.append((INTER + gi * 512, 512, mu))
    mm_stage(w3t, KT, slabs3, rhs_xb2, evac_ffn1,
             slab_offs=W_OFFS['w3t'])

    plan.barrier()
    stage_end()
    es7.close()

    es8 = ExitStack()
    hblk = sb(es8, "hblk", [128, 64 * 512], BF16)
    rtile2 = [sb(es8, f"rtile2_{i}", [128, 512], F32) for i in range(2)]
    KB2 = INTER // 128
    ffn2_last = None
    x1w = {}
    for tb in range(2):
        hl = plan.dma(SY, (lambda e, tb=tb: e.dma_start(
            out=hblk[:, :], in_=h_d[:, tb * 32768:(tb + 1) * 32768])),
            d_hblk, deps=allw(s_stg) + [ffn2_last])

        def rhs_h(kb, t0, tw):
            return hblk[:, kb * 512: kb * 512 + 512]

        def evac_ffn2(meta, o_abs, t0, tw, bk, mt, tb=tb):
            b = (o_abs // 128) % 2
            xlt = plan.dma(SY, (lambda e, o=o_abs, tb=tb, b=b: e.dma_start(
                out=rtile2[b][:, :], in_=x1_d[o:o + 128, tb * 512:(tb + 1) * 512])),
                d_x1t[b], deps=allw(s_sf) + [x1w.get(b)])
            at = plan.op(V, (lambda e, bk=bk, b=b: e.tensor_add(
                stagef[b][:, :], bk[:, :], rtile2[b][:, :])),
                deps=[mt, xlt, x1w.get(("sf", b))], inc=True)
            od = plan.dma(SY, (lambda e, o=o_abs, tb=tb, b=b: e.dma_start(
                out=out_hbm[o:o + 128, tb * 512:(tb + 1) * 512],
                in_=stagef[b][:, :])), s_sf[b], deps=[at])
            x1w[b] = at
            x1w[("sf", b)] = od
            return at

        meta4 = {"tgrps": [(tb * 512, 512)]}
        slabs4 = [(o, 128, meta4) for o in range(0, H, 128)]
        ffn2_last = mm_stage(w4t, KB2, slabs4, rhs_h, evac_ffn2,
                             extra_dep=hl, slab_offs=W_OFFS['w4t'])

    plan.barrier()
    stage_end()

    sems = {}
    sem_stack = ExitStack()
    for name in plan.sem_names:
        sems[name] = sem_stack.enter_context(nc.semaphore(name))
    global _SEMS
    _SEMS = sems

    with nc.Block() as block:
        def runner(engname):
            def run(e):
                for item in plan.ops[engname]:
                    if item[0] == "wait":
                        e.wait_ge(sems[item[1]], item[2])
                    else:
                        inst = item[1](e)
                        tag = item[2]
                        if tag is not None and inst is not None:
                            inst.then_inc(sems[tag[0]],
                                          16 if item[0] == "dma" else 1)
            return run
        block.tensor(runner(TE))
        block.vector(runner(V))
        block.scalar(runner(SC))
        block.gpsimd(runner(GP))
        block.sync(runner(SY))

    sem_stack.close()
    es8.close()
    es6.close()
    es0.close()
    return nc


_SEMS = None


def _tile_w(WT, slabs, kb_n):
    parts = []
    for (o0, w) in slabs:
        blk = WT[:, o0:o0 + w].reshape(kb_n, 128, w).transpose(1, 0, 2)
        parts.append(blk.reshape(128, kb_n * w))
    return np.ascontiguousarray(np.concatenate(parts, axis=1))


def _host_inputs(full):
    x = np.asarray(full["L0_attn_hidden_states"], np.float32)[0]
    xT = np.ascontiguousarray(x.T)
    w1t = _tile_w(np.asarray(full["L0_attn_in_proj_w"], np.float32).T,
                  SLABS_W1, KT)
    w2t = _tile_w(np.asarray(full["L0_attn_out_proj_w"], np.float32).T,
                  SLABS_W2, 32)
    w3t = _tile_w(np.asarray(full["L0_ffn_w_gate_up"], np.float32).T,
                  SLABS_W3, KT)
    w4t = _tile_w(np.asarray(full["L0_ffn_w_down"], np.float32).T,
                  SLABS_W4, INTER // 128)
    lnw = np.ascontiguousarray(
        np.asarray(full["L0_attn_ln_weight"], np.float32).reshape(KT, 128).T)
    nw = np.ascontiguousarray(
        np.asarray(full["L0_attn_norm_weight"], np.float32).reshape(I // 128, 128).T)
    dexp = np.ascontiguousarray(
        np.repeat(np.asarray(full["L0_attn_D"], np.float32), PD)
        .reshape(I // 128, 128).T)
    cwf = np.asarray(full["L0_attn_conv_w"], np.float32)
    cw = np.ascontiguousarray(
        cwf.reshape(33, 128, 4).transpose(1, 0, 2).reshape(128, 33 * 4))
    cb = np.ascontiguousarray(
        np.asarray(full["L0_attn_conv_b"], np.float32).reshape(33, 128).T)
    dtb = np.asarray(full["L0_attn_dt_bias"], np.float32).reshape(NH, 1)
    alog = np.asarray(full["L0_attn_A_log"], np.float32).reshape(NH, 1)
    import ml_dtypes
    sidx = np.arange(128)[:, None]
    lidx = np.arange(CH)[None, :]
    tri0 = (sidx <= lidx).astype(ml_dtypes.bfloat16)
    tri1 = ((sidx + 128) <= lidx).astype(ml_dtypes.bfloat16)
    tri = np.ascontiguousarray(np.concatenate([tri0, tri1], axis=1))
    identf = np.eye(128, dtype=np.float32)
    identb = np.eye(128, dtype=ml_dtypes.bfloat16)

    shared = dict(w1t=w1t, w2t=w2t, w3t=w3t, w4t=w4t, lnw=lnw, nw=nw, dexp=dexp,
                  cw=cw, cb=cb, dtb=dtb, alog=alog, tri=tri, identf=identf,
                  identb=identb)
    in_maps = []
    for r in range(NCORES):
        lo = r * T - 3
        if lo < 0:
            xs = np.concatenate(
                [np.zeros((H, 3), np.float32), xT[:, 0:T]], axis=1)
        else:
            xs = xT[:, lo:lo + TH]
        m = dict(shared)
        m["xt"] = np.ascontiguousarray(
            xs.reshape(KT, 128, TH).transpose(1, 0, 2).reshape(128, KT * TH))
        m["rmask"] = (np.arange(NCORES) < r).astype(np.float32).reshape(1, NCORES)
        in_maps.append(m)
    return in_maps


def kernel(**inputs):
    in_maps = _host_inputs(inputs)
    nc = build_nc()
    res = run_bass_kernel_spmd(nc, in_maps, core_ids=list(range(NCORES)))
    full = np.concatenate([np.asarray(res.results[r]["out"]) for r in range(NCORES)],
                          axis=1)
    return np.ascontiguousarray(full.T).reshape(1, 8192, H).astype(np.float32)


# version 08
# speedup vs baseline: 1.5691x; 1.1276x over previous
import numpy as np
import concourse.bass as bass
import concourse.mybir as mybir
from concourse.bass_utils import run_bass_kernel_spmd

F32 = mybir.dt.float32
BF16 = mybir.dt.bfloat16
AF = mybir.ActivationFunctionType
ALU = mybir.AluOpType

NCORES = 8
H = 2048
I = 4096
NH = 64
PD = 64
NS = 64
CH = 256
CONV = 4224
PROJ = 8384
T = 1024
TH = T + 3
NCHK = T // CH
KT = H // 128
EPS = 1e-5
INTER = 8192

ENGINES = ("tensor", "vector", "scalar", "gpsimd", "sync")
V, SC, TE, GP, SY = "vector", "scalar", "tensor", "gpsimd", "sync"

SLABS_W1 = ([(8320, 64)]
            + [(o, 512) for o in range(I, 8192, 512)]
            + [(8192, 128)]
            + [(o, 512) for o in range(0, I, 512)])
SLABS_W2 = [(o, 256) for o in range(0, H, 256)]
SLABS_W3 = [x for gi in range(INTER // 512)
            for x in ((gi * 512, 512), (INTER + gi * 512, 512))]
SLABS_W4 = [(o, 128) for o in range(0, H, 128)]


def _slab_offsets(slabs, kb_n):
    offs = []
    t = 0
    for (o0, w) in slabs:
        offs.append(t)
        t += kb_n * w
    return offs, t


class Plan:
    def __init__(self):
        self.ops = {e: [] for e in ENGINES}
        self.cnt = {e: 0 for e in ENGINES}
        self.waited = {e: {} for e in ENGINES}
        self.dma_fills = {}
        self.sem_names = (["m_" + e for e in ENGINES]
                          + ["b_" + e for e in ENGINES])
        self.bcnt = {e: 0 for e in ENGINES}
        self.stopped = False

    def new_dma_sem(self, name):
        assert name not in self.dma_fills
        self.dma_fills[name] = 0
        self.sem_names.append(name)
        return name

    def _wait(self, eng, sem, val):
        if self.waited[eng].get(sem, 0) >= val:
            return
        self.waited[eng][sem] = val
        self.ops[eng].append(("wait", sem, val))

    def op(self, eng, fn, deps=(), inc=False):
        if self.stopped:
            return None
        for d in deps:
            if d is not None:
                self._wait(eng, d[0], d[1])
        tag = None
        if inc:
            self.cnt[eng] += 1
            tag = ("m_" + eng, self.cnt[eng])
        self.ops[eng].append(("op", fn, tag))
        return tag

    def dma(self, queue, fn, sem, deps=()):
        if self.stopped:
            return None
        for d in deps:
            if d is not None:
                self._wait(queue, d[0], d[1])
        self.dma_fills[sem] += 16
        tag = (sem, self.dma_fills[sem])
        self.ops[queue].append(("dma", fn, tag))
        return tag

    def barrier(self):
        if self.stopped:
            return
        marks = []
        for e in ENGINES:
            if e == SY:
                continue
            self.bcnt[e] += 1
            tag = ("b_" + e, self.bcnt[e])
            self.ops[e].append(("op", (lambda eng: eng.nop()), tag))
            marks.append(tag)
            if self.cnt[e] > 0:
                marks.append(("m_" + e, self.cnt[e]))
        for s, vv in self.dma_fills.items():
            if vv > 0:
                marks.append((s, vv))
        for e in ENGINES:
            for m in marks:
                self._wait(e, m[0], m[1])


def build_nc(stop_after=None):
    from contextlib import ExitStack
    nc = bass.Bass("TRN2", target_bir_lowering=False, debug=False,
                   detect_race_conditions=False)

    def din(name, shape, dt=F32):
        return nc.dram_tensor(name, shape, dt, kind="ExternalInput").ap()

    xt = din("xt", [128, KT * TH])
    _o1, _t1 = _slab_offsets(SLABS_W1, KT)
    _o2, _t2 = _slab_offsets(SLABS_W2, 32)
    _o3, _t3 = _slab_offsets(SLABS_W3, KT)
    _o4, _t4 = _slab_offsets(SLABS_W4, INTER // 128)
    w1t = din("w1t", [128, _t1])
    w2t = din("w2t", [128, _t2])
    w3t = din("w3t", [128, _t3])
    w4t = din("w4t", [128, _t4])
    W_OFFS = {"w1t": _o1, "w2t": _o2, "w3t": _o3, "w4t": _o4}
    lnw_c = din("lnw", [128, KT])
    nw_c = din("nw", [128, I // 128])
    dexp_c = din("dexp", [128, I // 128])
    cw_c = din("cw", [128, 33 * 4])
    cb_c = din("cb", [128, 33])
    dtb_c = din("dtb", [NH, 1])
    alog_c = din("alog", [NH, 1])
    tri_c = din("tri", [128, 2 * CH], BF16)
    identf_c = din("identf", [128, 128])
    identb_c = din("identb", [128, 128], BF16)
    rmask_c = din("rmask", [1, NCORES])

    out_hbm = nc.dram_tensor("out", [H, T], F32, kind="ExternalOutput").ap()

    def dint(name, shape, dt):
        return nc.dram_tensor(name, shape, dt, kind="Internal").ap()

    xbc_d = dint("xbc_d", [CONV, TH], BF16)
    sg_d = dint("sg_d", [I, T], BF16)
    xs_d = dint("xs_d", [128, NCHK * 33 * CH], BF16)
    r_d = dint("r_d", [H, T], F32)
    acum_d = dint("acum_dd", [NH, T], F32)
    st_d = dint("st_d", [NCHK * NH, I], BF16)
    ccin_d = dint("ccin_d", [NH, I + 1], F32)
    ccout_d = dint("ccout_d", [NCORES * NH, I + 1], F32)
    y_d = dint("y_d", [I, T], BF16)
    x1_d = dint("x1_d", [H, T], F32)
    h_d = dint("h_d", [128, 2 * 64 * 512], BF16)

    es0 = ExitStack()

    def sb(stack, name, shape, dt):
        return stack.enter_context(nc.sbuf_tensor(name, shape, dt))

    onesf = sb(es0, "onesf", [1, 128], F32)
    onescf = sb(es0, "onescf", [128, 1], F32)
    onesbc = sb(es0, "onesbc", [128, 1], BF16)
    lnw = sb(es0, "lnw_sb", [128, KT], F32)
    nw = sb(es0, "nw_sb", [128, I // 128], F32)
    dexp = sb(es0, "dexp_sb", [128, I // 128], F32)
    cw = sb(es0, "cw_sb", [128, 33 * 4], F32)
    cb = sb(es0, "cb_sb", [128, 33], F32)
    dtb = sb(es0, "dtb_sb", [NH, 1], F32)
    alog = sb(es0, "alog_sb", [NH, 1], F32)
    tri = sb(es0, "tri_sb", [128, 2 * CH], BF16)
    idf = sb(es0, "idf_sb", [128, 128], F32)
    idb = sb(es0, "idb_sb", [128, 128], BF16)
    rmask = sb(es0, "rmask_sb", [1, NCORES], F32)
    SLABW = 8192
    slab = [sb(es0, f"slab{i}", [128, SLABW], BF16) for i in range(2)]
    stage = [sb(es0, f"stage{i}", [128, 512], BF16) for i in range(4)]
    stagef = [sb(es0, f"stagef{i}", [128, 512], F32) for i in range(2)]
    dt_sb = sb(es0, "dt_sb", [NH, T], F32)
    rowtmp = sb(es0, "rowtmp", [1, 128], F32)

    ps_mm = [es0.enter_context(nc.psum_tensor(f"ps_mm{i}", [128, 512], F32))
             for i in range(4)]
    ps_e = es0.enter_context(nc.psum_tensor("ps_e", [128, 512], F32))
    ps_tpb = es0.enter_context(nc.psum_tensor("ps_tpb", [128, 1024], BF16))
    ps_tpf = es0.enter_context(nc.psum_tensor("ps_tpf", [128, 512], F32))
    ps_row = es0.enter_context(nc.psum_tensor("ps_row", [1, 512], F32))

    plan = Plan()
    _STOP = {'n': 0}

    def stage_end():
        _STOP['n'] += 1
        if stop_after is not None and _STOP['n'] >= stop_after:
            plan.stopped = True
    nds = plan.new_dma_sem
    d_const = nds("d_const"); d_xf = nds("d_xf")
    d_slab = [nds("d_slab0"), nds("d_slab1")]
    s_stg = [nds(f"s_stg{i}") for i in range(4)]
    s_sf = [nds(f"s_sf{i}") for i in range(2)]
    s_t2f = [nds(f"s_t2f{i}") for i in range(2)]
    s_acum = nds("s_acum"); s_st = nds("s_st")
    s_ccin = nds("s_ccin"); d_cc = nds("d_cc")
    d_xbct = [nds("d_xbct0"), nds("d_xbct1")]
    d_xsc = nds("d_xsc")
    d_abc = [nds(f"d_abc{i}") for i in range(4)]
    d_lq = nds("d_lq"); d_dt = nds("d_dt"); d_stl = nds("d_stl")
    s_y = [nds("s_y0"), nds("s_y1")]
    d_yt = [nds("d_yt0"), nds("d_yt1")]
    d_sgt = [nds("d_sgt0"), nds("d_sgt1")]
    d_x1t = [nds("d_x1t0"), nds("d_x1t1")]
    d_rt = [nds("d_rt0"), nds("d_rt1")]
    d_hblk = nds("d_hblk")

    def allw(sems_):
        return [(s, plan.dma_fills[s]) for s in sems_ if plan.dma_fills[s] > 0]

    MMB = {"i": 0}

    def next_bank():
        b = ps_mm[MMB["i"] % 4]
        MMB["i"] += 1
        return b

    bank_war = {}

    for dst, src in [(lnw, lnw_c), (nw, nw_c), (dexp, dexp_c), (cw, cw_c),
                     (cb, cb_c), (dtb, dtb_c), (alog, alog_c), (tri, tri_c),
                     (idf, identf_c), (idb, identb_c), (rmask, rmask_c)]:
        plan.dma(SY, (lambda e, d=dst, s=src: e.dma_start(out=d[:, :], in_=s[:, :])),
                 d_const)
    plan.op(V, lambda e: e.memset(onesf[:, :], 1.0), inc=True)
    plan.op(V, lambda e: e.memset(onescf[:, :], 1.0), inc=True)
    m_ones = plan.op(V, lambda e: e.memset(onesbc[:, :], 1.0), inc=True)

    es1 = ExitStack()
    xfull = sb(es1, "xfull", [128, KT * TH], F32)
    xsq = [sb(es1, f"xsq{i}", [128, TH], F32) for i in range(2)]
    srow = sb(es1, "srow", [1, TH], F32)
    sqrow = sb(es1, "sqrow", [1, TH], F32)
    mrow = sb(es1, "mrow", [1, TH], F32)
    varrow = sb(es1, "varrow", [1, TH], F32)
    arow = sb(es1, "arow", [1, TH], F32)
    aorow = sb(es1, "aorow", [1, TH], F32)
    mb = sb(es1, "mb", [128, TH], F32)
    ab = sb(es1, "ab", [128, TH], F32)
    aob = sb(es1, "aob", [128, TH], F32)
    t1f = sb(es1, "t1f", [128, TH], F32)
    t2f = [sb(es1, f"t2f{i}", [128, TH], F32) for i in range(2)]
    xb = sb(es1, "xb", [128, KT * TH], BF16)

    t_x = plan.dma(SY, lambda e: e.dma_start(out=xfull[:, :], in_=xt[:, :]), d_xf)

    TG = [(0, 512), (512, 512), (1024, 3)]
    sum_dst = [ps_row, ps_mm[0], ps_mm[1]]
    sq_dst = [ps_e, ps_mm[2], ps_mm[3]]
    sqmm_tag = {}
    last_mm = None
    for kt in range(KT):
        b = kt % 2
        sq_t = plan.op(
            SC, (lambda e, b=b, kt=kt: e.activation(
                xsq[b][:, :], xfull[:, kt * TH:(kt + 1) * TH], AF.Square)),
            deps=[t_x, sqmm_tag.get(kt - 2)], inc=True)
        for gi, (t0, tw) in enumerate(TG):
            plan.op(TE, (lambda e, kt=kt, gi=gi, t0=t0, tw=tw: e.matmul(
                sum_dst[gi][0:1, 0:tw], onescf[:, 0:1],
                xfull[:, kt * TH + t0: kt * TH + t0 + tw],
                start=(kt == 0), stop=(kt == KT - 1), skip_group_check=True)),
                deps=[t_x, m_ones])
            last_mm = plan.op(TE, (lambda e, b=b, gi=gi, t0=t0, tw=tw, kt=kt:
                e.matmul(
                    sq_dst[gi][0:1, 0:tw], onescf[:, 0:1], xsq[b][:, t0:t0 + tw],
                    start=(kt == 0), stop=(kt == KT - 1), skip_group_check=True)),
                deps=[sq_t, m_ones], inc=True)
        sqmm_tag[kt] = last_mm
    ev = []
    for gi, (t0, tw) in enumerate(TG):
        ev.append(plan.op(V, (lambda e, gi=gi, t0=t0, tw=tw: e.tensor_copy(
            srow[0:1, t0:t0 + tw], sum_dst[gi][0:1, 0:tw])), deps=[last_mm], inc=True))
        ev.append(plan.op(V, (lambda e, gi=gi, t0=t0, tw=tw: e.tensor_copy(
            sqrow[0:1, t0:t0 + tw], sq_dst[gi][0:1, 0:tw])), deps=[last_mm], inc=True))
        bank_war[sum_dst[gi].name] = ev[-2]
        bank_war[sq_dst[gi].name] = ev[-1]
    m1 = plan.op(V, lambda e: e.tensor_scalar(mrow[:, :], srow[:, :], 1.0 / H, None,
                                              ALU.mult), deps=ev, inc=True)
    m2t = plan.op(V, lambda e: e.tensor_mul(varrow[:, :], mrow[:, :], mrow[:, :]),
                  deps=[m1], inc=True)
    m3 = plan.op(V, lambda e: e.scalar_tensor_tensor(
        varrow[:, :], sqrow[:, :], 1.0 / H, varrow[:, :], ALU.mult, ALU.subtract),
        deps=[m2t], inc=True)
    a0 = plan.op(V, lambda e: e.tensor_scalar(arow[:, :], varrow[:, :], 1.0 + EPS,
                                              EPS * EPS, ALU.mult, ALU.add),
                 deps=[m3], inc=True)
    a1 = plan.op(SC, lambda e: e.activation(arow[:, :], arow[:, :], AF.Ln),
                 deps=[a0], inc=True)
    a1b = plan.op(V, lambda e: e.tensor_scalar(arow[:, :], arow[:, :], -0.5, None,
                                               ALU.mult), deps=[a1], inc=True)
    a2 = plan.op(SC, lambda e: e.activation(arow[:, :], arow[:, :], AF.Exp),
                 deps=[a1b], inc=True)
    a2b = plan.op(V, lambda e: e.tensor_scalar(aorow[:, :], varrow[:, :], 1.0, EPS,
                                               ALU.mult, ALU.add), deps=[m3], inc=True)
    a3 = plan.op(SC, lambda e: e.activation(aorow[:, :], aorow[:, :], AF.Ln),
                 deps=[a2b], inc=True)
    a3b = plan.op(V, lambda e: e.tensor_scalar(aorow[:, :], aorow[:, :], -0.5, None,
                                               ALU.mult), deps=[a3], inc=True)
    a4 = plan.op(SC, lambda e: e.activation(aorow[:, :], aorow[:, :], AF.Exp),
                 deps=[a3b], inc=True)

    def bcast_row(src_row, dst_tile, dep):
        tags = []
        for (t0, tw) in TG:
            bk = next_bank()
            mt = plan.op(TE, (lambda e, bk=bk, t0=t0, tw=tw, s=src_row: e.matmul(
                bk[:, 0:tw], onesf[0:1, :], s[0:1, t0:t0 + tw],
                start=True, stop=True, skip_group_check=True)),
                deps=[dep, m_ones, bank_war.get(bk.name)], inc=True)
            et = plan.op(V, (lambda e, bk=bk, t0=t0, tw=tw, d=dst_tile: e.tensor_copy(
                d[:, t0:t0 + tw], bk[:, 0:tw])), deps=[mt], inc=True)
            bank_war[bk.name] = et
            tags.append(et)
        return tags

    mb_t = bcast_row(mrow, mb, m1)
    ab_t = bcast_row(arow, ab, a2)
    aob_t = bcast_row(aorow, aob, a4)

    r_tags = []
    xb_tags = []
    prev_dma = {}
    for kt in range(KT):
        b = kt % 2
        tt1 = plan.op(V, (lambda e, kt=kt: e.tensor_sub(
            t1f[:, :], xfull[:, kt * TH:(kt + 1) * TH], mb[:, :])),
            deps=[t_x] + mb_t, inc=True)
        xb_tags.append(plan.op(V, (lambda e, kt=kt: e.scalar_tensor_tensor(
            xb[:, kt * TH:(kt + 1) * TH], ab[:, :], lnw[:, kt:kt + 1], t1f[:, :],
            ALU.mult, ALU.mult)), deps=[tt1] + ab_t + [("d_const", 16 * 11)],
            inc=True))
        tt2 = plan.op(V, (lambda e, kt=kt, b=b: e.tensor_mul(
            t2f[b][:, :], t1f[:, :], aob[:, :])),
            deps=[tt1, prev_dma.get(b)] + aob_t, inc=True)
        tt3 = plan.op(V, (lambda e, kt=kt, b=b: e.tensor_add(
            t2f[b][:, :], t2f[b][:, :], xfull[:, kt * TH:(kt + 1) * TH])),
            deps=[tt2], inc=True)
        prev_dma[b] = plan.dma(SY, (lambda e, kt=kt, b=b: e.dma_start(
            out=r_d[kt * 128:(kt + 1) * 128, :], in_=t2f[b][:, 3:TH])),
            s_t2f[b], deps=[tt3])
        r_tags.append(prev_dma[b])

    stage_end()

    def mm_stage(w_hbm, kb_n, slabs, rhs_fn, evac_fn, extra_dep=None,
                 slab_offs=None, rhs_dep=None):
        slab_last_use = [None, None]
        slab_tag = [None, None]
        for si, (o0, wdt, meta) in enumerate(slabs):
            sbuf_i = si % 2
            soff = slab_offs[si]
            nsub = wdt // 128 if wdt >= 128 else 1
            slab_tag[sbuf_i] = plan.dma(GP, (lambda e, soff=soff, wdt=wdt, i=sbuf_i:
                e.dma_start(
                    out=slab[i][:, 0:kb_n * wdt],
                    in_=w_hbm[:, soff:soff + kb_n * wdt])),
                d_slab[sbuf_i], deps=[slab_last_use[sbuf_i], extra_dep])
            tgrps = meta["tgrps"]
            for osub in range(nsub):
                ow = min(128, wdt)
                for (t0, tw) in tgrps:
                    bk = next_bank()
                    mt = None
                    for kb in range(kb_n):
                        mt = plan.op(TE, (lambda e, bk=bk, kb=kb, osub=osub, ow=ow,
                                          wdt=wdt, t0=t0, tw=tw, i=sbuf_i: e.matmul(
                            bk[0:ow, 0:tw],
                            slab[i][:, kb * wdt + osub * 128: kb * wdt + osub * 128 + ow],
                            rhs_fn(kb, t0, tw),
                            start=(kb == 0), stop=(kb == kb_n - 1),
                            skip_group_check=True)),
                            deps=([slab_tag[sbuf_i], bank_war.get(bk.name)]
                                  if kb == 0 else [])
                                 + ([rhs_dep(kb)] if rhs_dep is not None else []),
                            inc=(kb == kb_n - 1))
                    slab_last_use[sbuf_i] = mt
                    war = evac_fn(meta, o0 + osub * 128, t0, tw, bk, mt)
                    bank_war[bk.name] = war
        return mt

    stg_war = [None] * 4
    STG = {"i": 0}

    def next_stage():
        s = STG["i"] % 4
        STG["i"] += 1
        return s

    def rhs_xb(kb, t0, tw):
        return xb[:, kb * TH + t0: kb * TH + t0 + tw]

    dt_tags = []
    xbc_tile_tags = {}

    def evac_inproj(meta, o_abs, t0, tw, bk, mt):
        kind = meta["kind"]
        if kind == "dt":
            t = plan.op(V, (lambda e, t0=t0, tw=tw, bk=bk: e.tensor_copy(
                dt_sb[:, t0 - 3: t0 - 3 + tw], bk[0:NH, 0:tw])), deps=[mt], inc=True)
            dt_tags.append(t)
            return t
        s = next_stage()
        if kind == "gate":
            et = plan.op(SC, (lambda e, s=s, bk=bk, tw=tw: e.activation(
                stage[s][:, 0:tw], bk[:, 0:tw], AF.Silu)),
                deps=[mt, stg_war[s]], inc=True)
            dmat = plan.dma(SY, (lambda e, s=s, o=o_abs, t0=t0, tw=tw: e.dma_start(
                out=sg_d[o:o + 128, t0 - 3: t0 - 3 + tw], in_=stage[s][:, 0:tw])),
                s_stg[s], deps=[et])
        else:
            et = plan.op(V, (lambda e, s=s, bk=bk, tw=tw: e.tensor_copy(
                stage[s][:, 0:tw], bk[:, 0:tw])), deps=[mt, stg_war[s]], inc=True)
            dmat = plan.dma(SY, (lambda e, s=s, o=o_abs, t0=t0, tw=tw: e.dma_start(
                out=xbc_d[o - I:o - I + 128, t0: t0 + tw], in_=stage[s][:, 0:tw])),
                s_stg[s], deps=[et])
            xbc_tile_tags.setdefault((o_abs - I) // 128, []).append(dmat)
        stg_war[s] = dmat
        return et

    TG_MAIN = [(3, 512), (515, 512)]
    TG_FULL = [(0, 512), (512, 512), (1024, 3)]
    slabs1 = ([(8320, 64, {"kind": "dt", "tgrps": TG_MAIN})]
              + [(o, 512, {"kind": "xbc", "tgrps": TG_FULL}) for o in range(I, 8192, 512)]
              + [(8192, 128, {"kind": "xbc", "tgrps": TG_FULL})]
              + [(o, 512, {"kind": "gate", "tgrps": TG_MAIN}) for o in range(0, I, 512)])
    N_XBC_SLABS = 10
    mt_xbc_last = mm_stage(w1t, KT, slabs1[:N_XBC_SLABS], rhs_xb, evac_inproj,
                           slab_offs=W_OFFS['w1t'][:N_XBC_SLABS],
                           rhs_dep=lambda kb: xb_tags[kb])

    stage_end()
    es1.close()

    es2 = ExitStack()
    cd = sb(es2, "cd", [NH, NCHK], F32)
    dtot = sb(es2, "dtot", [NH, 1], F32)
    decb = sb(es2, "decb", [NH, NCHK * NH], F32)
    alcol = sb(es2, "alcol", [128, 8 * NH], F32)
    lndtT = sb(es2, "lndtT", [128, 8 * NH], F32)
    wcol = sb(es2, "wcol", [128, 8 * NH], F32)
    bfm = sb(es2, "bfm", [NS, T], BF16)
    cfm = sb(es2, "cfm", [NS, T], BF16)
    smask = sb(es2, "smask", [128, NCHK * 2 * CH], BF16)
    es2a = ExitStack()
    dts = sb(es2a, "dts", [NH, T], F32)
    lndt = sb(es2a, "lndt", [NH, T], F32)
    dA = sb(es2a, "dA", [NH, T], F32)
    acum = sb(es2a, "acum_sb", [NH, T], F32)
    alf = sb(es2a, "alf", [NH, T], F32)
    tmpa = sb(es2a, "tmpa", [NH, T], F32)
    zero64 = sb(es2a, "zero64", [NH, CH], F32)
    expa = sb(es2a, "expa", [NH, 1], F32)

    zt = plan.op(V, lambda e: e.memset(zero64[:, :], 0.0), inc=True)
    u1 = plan.op(V, lambda e: e.tensor_scalar(tmpa[:, :], dt_sb[:, :], dtb[:, 0:1],
                                              None, ALU.add),
                 deps=dt_tags + [("d_const", 16 * 11)], inc=True)
    u2 = plan.op(V, lambda e: e.scalar_tensor_tensor(lndt[:, :], tmpa[:, :], -1.0,
                                                     tmpa[:, :], ALU.mult, ALU.min),
                 deps=[u1], inc=True)
    u3 = plan.op(SC, lambda e: e.activation(lndt[:, :], lndt[:, :], AF.Exp),
                 deps=[u2], inc=True)
    u4 = plan.op(SC, lambda e: e.activation(lndt[:, :], lndt[:, :], AF.Ln, bias=1.0),
                 deps=[u3], inc=True)
    u5 = plan.op(V, lambda e: e.tensor_scalar(dA[:, :], tmpa[:, :], 0.0, None, ALU.max),
                 deps=[u2], inc=True)
    u6 = plan.op(V, lambda e: e.tensor_add(dts[:, :], lndt[:, :], dA[:, :]),
                 deps=[u4, u5], inc=True)
    u7 = plan.op(SC, lambda e: e.activation(lndt[:, :], dts[:, :], AF.Ln),
                 deps=[u6], inc=True)
    u8 = plan.op(SC, lambda e: e.activation(expa[:, :], alog[:, 0:1], AF.Exp),
                 deps=[("d_const", 16 * 11)], inc=True)
    u9 = plan.op(V, lambda e: e.tensor_scalar(dA[:, :], dts[:, :], expa[:, 0:1], -1.0,
                                              ALU.mult, ALU.mult), deps=[u6, u8],
                 inc=True)
    sc_t = []
    for c in range(NCHK):
        sc_t.append(plan.op(V, (lambda e, c=c: e.tensor_tensor_scan(
            acum[:, c * CH:(c + 1) * CH], dA[:, c * CH:(c + 1) * CH], zero64[:, :],
            0.0, ALU.add, ALU.add)), deps=[u9, zt], inc=True))
    al_t = plan.op(V, lambda e: e.tensor_sub(alf[:, :], acum[:, :], lndt[:, :]),
                   deps=sc_t + [u7], inc=True)
    acum_dma = plan.dma(SY, lambda e: e.dma_start(out=acum_d[:, :], in_=acum[:, :]),
                        s_acum, deps=sc_t)
    cd_t = plan.op(SC, lambda e: e.activation(
        cd[:, :], acum.ap()[:, CH - 1::CH], AF.Exp), deps=sc_t, inc=True)
    d1 = plan.op(V, lambda e: e.tensor_mul(dtot[:, :], cd[:, 0:1], cd[:, 1:2]),
                 deps=[cd_t], inc=True)
    d2 = plan.op(V, lambda e: e.tensor_mul(dtot[:, :], dtot[:, :], cd[:, 2:3]),
                 deps=[d1], inc=True)
    dtot_t = plan.op(V, lambda e: e.tensor_mul(dtot[:, :], dtot[:, :], cd[:, 3:4]),
                     deps=[d2], inc=True)

    tp_war = [None]

    def transpose_f32(src_ap, np_, dst_ap, dep):
        mt = plan.op(TE, (lambda e: e.transpose(
            ps_tpf[:, 0:np_], src_ap, idf[0:np_, 0:np_])),
            deps=[dep, tp_war[0], ("d_const", 16 * 11)], inc=True)
        et = plan.op(V, (lambda e: e.tensor_copy(
            dst_ap, ps_tpf[:, 0:np_])), deps=[mt], inc=True)
        tp_war[0] = et
        return et

    w3_tags = []
    for c in range(NCHK):
        w1 = plan.op(V, (lambda e, c=c: e.tensor_scalar(
            tmpa[:, c * CH:(c + 1) * CH], acum[:, c * CH:(c + 1) * CH],
            acum[:, (c + 1) * CH - 1:(c + 1) * CH], -1.0, ALU.subtract, ALU.mult)),
            deps=sc_t, inc=True)
        w2 = plan.op(V, (lambda e, c=c: e.tensor_add(
            tmpa[:, c * CH:(c + 1) * CH], tmpa[:, c * CH:(c + 1) * CH],
            lndt[:, c * CH:(c + 1) * CH])), deps=[w1, u7], inc=True)
        w3 = plan.op(SC, (lambda e, c=c: e.activation(
            tmpa[:, c * CH:(c + 1) * CH], tmpa[:, c * CH:(c + 1) * CH], AF.Exp)),
            deps=[w2], inc=True)
        w3_tags.append(w3)
    stage_end()


    es3 = ExitStack()
    xbct = [sb(es3, f"xbct{i}", [128, TH], BF16) for i in range(2)]
    cacc = [sb(es3, f"cacc{i}", [128, T], F32) for i in range(2)]
    cstage = [sb(es3, f"cstage{i}", [128, 512], BF16) for i in range(4)]
    conv_last = [None, None]
    acc_war = [None, None]
    c_stg_war = [None] * 4
    ld_tags = {}

    def conv_ld(it):
        b = it % 2
        ld_tags[it] = plan.dma(SY, (lambda e, it=it, b=b: e.dma_start(
            out=xbct[b][:, :], in_=xbc_d[it * 128:(it + 1) * 128, :])),
            d_xbct[b], deps=[conv_last[b]] + xbc_tile_tags[it])

    conv_ld(0)
    conv_ld(1)
    for it in range(33):
        b = it % 2
        a = b
        t = plan.op(V, (lambda e, b=b, a=a, it=it: e.tensor_scalar(
            cacc[a][:, :], xbct[b][:, 0:T], cw[:, it * 4:it * 4 + 1], None,
            ALU.mult)), deps=[ld_tags[it], acc_war[a], ("d_const", 16 * 11)],
            inc=True)
        for k in range(1, 4):
            t = plan.op(V, (lambda e, b=b, a=a, it=it, k=k: e.scalar_tensor_tensor(
                cacc[a][:, :], xbct[b][:, k:T + k], cw[:, it * 4 + k:it * 4 + k + 1],
                cacc[a][:, :], ALU.mult, ALU.add)), deps=[t], inc=True)
        conv_last[b] = t
        if it + 2 < 33:
            conv_ld(it + 2)
        sil = []
        for s2 in range(2):
            ss = (2 * it + s2) % 4
            et = plan.op(SC, (lambda e, a=a, s2=s2, it=it, ss=ss: e.activation(
                cstage[ss][:, :], cacc[a][:, s2 * 512:(s2 + 1) * 512], AF.Silu,
                bias=cb[:, it:it + 1])), deps=[t, c_stg_war[ss]], inc=True)
            W33 = 33 * CH
            dmat = plan.dma(SY, (lambda e, it=it, ss=ss, s2=s2, W33=W33:
                e.dma_start(
                    out=xs_d.rearrange("p (cc w) -> p cc w", cc=4)
                        [:, 2 * s2:2 * s2 + 2, it * CH:(it + 1) * CH],
                    in_=cstage[ss][:, :].rearrange("p (h w) -> p h w", h=2))),
                s_stg[ss], deps=[et])
            c_stg_war[ss] = dmat
            sil.append(et)
            if it == 32:
                b1 = plan.op(V, (lambda e, s2=s2, ss=ss: e.tensor_copy(
                    bfm[:, s2 * 512:(s2 + 1) * 512], cstage[ss][0:NS, :])),
                    deps=[et], inc=True)
                c1 = plan.op(V, (lambda e, s2=s2, ss=ss: e.tensor_copy(
                    cfm[:, s2 * 512:(s2 + 1) * 512], cstage[ss][NS:128, :])),
                    deps=[et], inc=True)
                c_stg_war[ss] = c1
        acc_war[a] = sil[-1]

    mm_stage(w1t, KT, slabs1[N_XBC_SLABS:], rhs_xb, evac_inproj,
             slab_offs=W_OFFS['w1t'][N_XBC_SLABS:],
             rhs_dep=lambda kb: xb_tags[kb], extra_dep=mt_xbc_last)

    for tt in range(8):
        transpose_f32(alf[:, tt * 128:(tt + 1) * 128], NH,
                      alcol[:, tt * NH:(tt + 1) * NH], al_t)
        transpose_f32(lndt[:, tt * 128:(tt + 1) * 128], NH,
                      lndtT[:, tt * NH:(tt + 1) * NH], u7)
    for c in range(NCHK):
        for s in range(2):
            tt = 2 * c + s
            transpose_f32(tmpa[:, tt * 128:(tt + 1) * 128], NH,
                          wcol[:, tt * NH:(tt + 1) * NH], w3_tags[c])
    for c in range(NCHK):
        mt = plan.op(TE, (lambda e, c=c: e.transpose(
            ps_tpf[0:1, 0:NH], cd[:, c:c + 1], idf[0:NH, 0:NH])),
            deps=[cd_t, tp_war[0]], inc=True)
        et = plan.op(V, (lambda e: e.tensor_copy(
            rowtmp[0:1, 0:NH], ps_tpf[0:1, 0:NH])),
            deps=[mt], inc=True)
        tp_war[0] = et
        bk = next_bank()
        m2 = plan.op(TE, (lambda e, bk=bk: e.matmul(
            bk[0:NH, 0:NH], onesf[0:1, 0:NH], rowtmp[0:1, 0:NH],
            start=True, stop=True, skip_group_check=True)),
            deps=[et, m_ones, bank_war.get(bk.name)], inc=True)
        e2 = plan.op(V, (lambda e, c=c, bk=bk: e.tensor_copy(
            decb[:, c * NH:(c + 1) * NH], bk[0:NH, 0:NH])), deps=[m2], inc=True)
        bank_war[bk.name] = e2


    plan.barrier()
    stage_end()
    es3.close()
    es2a.close()

    es4 = ExitStack()
    xs_c = sb(es4, "xs_c", [128, 32 * CH], BF16)
    xsT = sb(es4, "xsT", [128, NCHK * 2 * I], BF16)
    btc = sb(es4, "btc", [128, 2 * NS], BF16)
    bd = [sb(es4, f"bd{i}", [128, 2 * NS], BF16) for i in range(2)]
    es4s = ExitStack()
    ststage = sb(es4s, "ststage", [NH, I], F32)
    carry = [sb(es4s, "carry0", [NH, I], F32)]

    cz = plan.op(V, lambda e: e.memset(carry[0][:, :], 0.0), inc=True)
    bd_war = [None, None]
    tpb_war = [None]

    def transpose_b(src_ap, np_, dst_ap, dep):
        mt = plan.op(TE, (lambda e: e.transpose(
            ps_tpb[:, 0:np_], src_ap, idb[0:np_, 0:np_])),
            deps=[dep, tpb_war[0]], inc=True)
        et = plan.op(V, (lambda e: e.tensor_copy(
            dst_ap, ps_tpb[:, 0:np_])), deps=[mt], inc=True)
        tpb_war[0] = et
        return et

    def load_xs_chunk(c, war, dsem):
        W33 = 33 * CH
        return plan.dma(SY, (lambda e, c=c, W33=W33: e.dma_start(
            out=xs_c[:, :],
            in_=xs_d[:, c * W33: c * W33 + 32 * CH])),
            dsem, deps=[war] + allw(s_stg))

    def build_xsT(c, dep_load):
        tags = []
        base = c * 2 * I
        for s in range(2):
            for it0 in range(0, 32, 4):
                mt = None
                for q in range(4):
                    it = it0 + q
                    mt = plan.op(TE, (lambda e, s=s, it=it, q=q: e.transpose(
                        ps_tpb[:, q * 128:(q + 1) * 128],
                        xs_c[:, it * CH + s * 128: it * CH + s * 128 + 128],
                        idb[0:128, 0:128])),
                        deps=[dep_load] + ([tpb_war[0]] if q == 0 else []),
                        inc=(q == 3))
                et = plan.op(V, (lambda e, s=s, it0=it0, base=base: e.tensor_copy(
                    xsT[:, base + s * I + it0 * 128: base + s * I + it0 * 128 + 512],
                    ps_tpb[:, 0:512])), deps=[mt], inc=True)
                tpb_war[0] = et
                tags.append(et)
        return tags

    xs_war = None
    scan_done = []
    stev_prev = None
    st_dma_last = None
    for c in range(NCHK):
        xl = load_xs_chunk(c, xs_war, d_xsc)
        xst_tags = build_xsT(c, xl)
        xs_war = xst_tags[-1]
        for s in range(2):
            bk = next_bank()
            mt = plan.op(TE, (lambda e, c=c, s=s, bk=bk: e.matmul(
                bk[:, 0:CH], bfm[:, c * CH + s * 128: c * CH + s * 128 + 128],
                cfm[:, c * CH:(c + 1) * CH], start=True, stop=True,
                skip_group_check=True)),
                deps=[bank_war.get(bk.name)], inc=True)
            et = plan.op(V, (lambda e, c=c, s=s, bk=bk: e.tensor_tensor(
                smask[:, (2 * c + s) * CH:(2 * c + s + 1) * CH], bk[:, 0:CH],
                tri[:, s * CH:(s + 1) * CH], ALU.mult)), deps=[mt], inc=True)
            bank_war[bk.name] = et
        bt_tags = []
        for s in range(2):
            bt_tags.append(transpose_b(
                bfm[:, c * CH + s * 128: c * CH + s * 128 + 128], NS,
                btc[:, s * NS:(s + 1) * NS], dep=None))
        for hg in range(8):
            mm_last = None
            for hh in range(8):
                h = hg * 8 + hh
                bdi = hh % 2
                bdt = []
                for s in range(2):
                    tt = 2 * c + s
                    bdt.append(plan.op(V, (lambda e, h=h, s=s, tt=tt, bdi=bdi:
                        e.tensor_scalar(
                            bd[bdi][:, s * NS:(s + 1) * NS],
                            btc[:, s * NS:(s + 1) * NS],
                            wcol[:, tt * NH + h: tt * NH + h + 1], None, ALU.mult)),
                        deps=[bt_tags[s], bd_war[bdi]], inc=True))
                for s in range(2):
                    mm_last = plan.op(TE, (lambda e, c=c, h=h, hh=hh, s=s, bdi=bdi:
                        e.matmul(
                            ps_e[0:NS, hh * PD:(hh + 1) * PD],
                            bd[bdi][:, s * NS:(s + 1) * NS],
                            xsT[:, c * 2 * I + s * I + h * PD:
                                c * 2 * I + s * I + (h + 1) * PD],
                            start=(hh == 0 and s == 0), stop=(hh == 7 and s == 1),
                            skip_group_check=True)),
                        deps=[bdt[s]] + ([xs_war, stev_prev] if hh == 0 and s == 0
                                         else []),
                        inc=True)
                bd_war[bdi] = mm_last
            stev_prev = plan.op(V, (lambda e, hg=hg: e.tensor_copy(
                ststage[:, hg * 512:(hg + 1) * 512], ps_e[0:NH, :])),
                deps=[mm_last, st_dma_last], inc=True)
        st_dma = plan.dma(GP, (lambda e, c=c: e.dma_start(
            out=st_d[c * NH:(c + 1) * NH, :], in_=ststage[:, :])),
            s_st, deps=[stev_prev])
        st_dma_last = st_dma
        src = carry[0]
        dst = carry[0]
        last = None
        for h in range(NH):
            last = plan.op(V, (lambda e, c=c, h=h, src=src, dst=dst:
                e.scalar_tensor_tensor(
                    dst[:, h * PD:(h + 1) * PD], src[:, h * PD:(h + 1) * PD],
                    decb[:, c * NH + h: c * NH + h + 1],
                    ststage[:, h * PD:(h + 1) * PD], ALU.mult, ALU.add)),
                deps=[stev_prev, st_dma] + ([cz] if c == 0 else [scan_done[-1]]),
                inc=True)
        scan_done.append(last)

    cc1 = plan.dma(SY, lambda e: e.dma_start(out=ccin_d[:, 0:I], in_=carry[0][:, :]),
                   s_ccin, deps=[scan_done[-1]])
    cc2 = plan.dma(SY, lambda e: e.dma_start(out=ccin_d[:, I:I + 1], in_=dtot[:, :],
                                             allow_slow_non_contiguous=True),
                   s_ccin, deps=[dtot_t])
    es4s.close()

    plan.barrier()
    stage_end()

    def do_cc(e):
        e.collective_compute(
            "AllGather", ALU.bypass, replica_groups=[list(range(NCORES))],
            ins=[ccin_d.opt()], outs=[ccout_d.opt()]).then_inc(_SEMS["d_cc"])
        return None
    plan.op(GP, do_cc, deps=[(s_ccin, plan.dma_fills[s_ccin])])
    CC_TAG = ("d_cc", 1)

    es4b = ExitStack()
    lq_st = sb(es4b, "lq_st", [NH, I], BF16)
    dtall = sb(es4b, "dtall", [NH, NCORES], F32)
    dmall = sb(es4b, "dmall", [NH, NCORES], F32)
    wfin = sb(es4b, "wfin", [NH, NCORES], F32)
    rmb = sb(es4b, "rmb", [NH, NCORES], F32)
    wbq = sb(es4b, "wbq", [NH, NH], F32)
    prevb = sb(es4b, "prevb", [NH, I], F32)
    prevbb = sb(es4b, "prevbb", [NH, I], BF16)
    stldb = sb(es4b, "stldb", [NH, I], BF16)

    pz = plan.op(V, lambda e: e.memset(prevb[:, :], 0.0), inc=True)
    dt_tags = []
    for q in range(NCORES):
        dt_tags.append(plan.dma(SY, (lambda e, q=q: e.dma_start(
            out=dtall[:, q:q + 1], in_=ccout_d[q * NH:(q + 1) * NH, I:I + 1],
            allow_slow_non_contiguous=True)),
            d_dt, deps=[CC_TAG]))
    dt_all = dt_tags[-1]
    bk = next_bank()
    mt = plan.op(TE, (lambda e, bk=bk: e.matmul(
        bk[0:NH, 0:NCORES], onesf[0:1, 0:NH], rmask[0:1, :], start=True, stop=True,
        skip_group_check=True)), deps=[bank_war.get(bk.name), m_ones,
                                       ("d_const", 16 * 11)], inc=True)
    rmb_t = plan.op(V, (lambda e, bk=bk: e.tensor_copy(rmb[:, :], bk[0:NH, 0:NCORES])),
                    deps=[mt], inc=True)
    bank_war[bk.name] = rmb_t
    q1 = plan.op(V, lambda e: e.tensor_scalar(dmall[:, :], dtall[:, :], -1.0, None,
                                              ALU.add), deps=[dt_all], inc=True)
    q2 = plan.op(V, lambda e: e.tensor_mul(dmall[:, :], dmall[:, :], rmb[:, :]),
                 deps=[q1, rmb_t], inc=True)
    q3 = plan.op(V, lambda e: e.tensor_scalar(dmall[:, :], dmall[:, :], 1.0, None,
                                              ALU.add), deps=[q2], inc=True)
    wt = plan.op(V, lambda e: e.memset(wfin[:, NCORES - 1:NCORES], 1.0), inc=True)
    for q in range(NCORES - 2, -1, -1):
        wt = plan.op(V, (lambda e, q=q: e.tensor_mul(
            wfin[:, q:q + 1], wfin[:, q + 1:q + 2], dmall[:, q + 1:q + 2])),
            deps=[wt, q3], inc=True)
    wt = plan.op(V, lambda e: e.tensor_mul(wfin[:, :], wfin[:, :], rmb[:, :]),
                 deps=[wt], inc=True)
    lq_war = None
    for q in range(NCORES):
        mt = plan.op(TE, (lambda e, q=q: e.transpose(
            ps_tpf[0:1, 0:NH], wfin[:, q:q + 1], idf[0:NH, 0:NH])),
            deps=[wt, tp_war[0]], inc=True)
        et = plan.op(V, (lambda e: e.tensor_copy(
            rowtmp[0:1, 0:NH], ps_tpf[0:1, 0:NH])), deps=[mt], inc=True)
        tp_war[0] = et
        bk = next_bank()
        m2 = plan.op(TE, (lambda e, bk=bk: e.matmul(
            bk[0:NH, 0:NH], onesf[0:1, 0:NH], rowtmp[0:1, 0:NH], start=True,
            stop=True, skip_group_check=True)),
            deps=[et, m_ones, bank_war.get(bk.name)], inc=True)
        wb_t = plan.op(V, (lambda e, bk=bk: e.tensor_copy(wbq[:, :], bk[0:NH, 0:NH])),
                       deps=[m2], inc=True)
        bank_war[bk.name] = wb_t
        lq_t = plan.dma(GP, (lambda e, q=q: e.dma_start(
            out=lq_st[:, :], in_=ccout_d[q * NH:(q + 1) * NH, 0:I])), d_lq,
            deps=[CC_TAG, lq_war])
        last = None
        for h in range(NH):
            last = plan.op(V, (lambda e, h=h: e.scalar_tensor_tensor(
                prevb[:, h * PD:(h + 1) * PD], lq_st[:, h * PD:(h + 1) * PD],
                wbq[:, h:h + 1], prevb[:, h * PD:(h + 1) * PD], ALU.mult, ALU.add)),
                deps=[lq_t, wb_t, pz], inc=True)
        lq_war = last

    plan.barrier()
    stage_end()

    es4c = ExitStack()
    abc = [sb(es4c, f"abc{i}", [128, CH], F32) for i in range(4)]
    dc_sb = [sb(es4c, f"dc{i}", [128, CH], F32) for i in range(6)]
    lb_sb = [sb(es4c, f"lb{i}", [128, CH], BF16) for i in range(6)]
    pb_sb = [sb(es4c, f"pb{i}", [128, CH], BF16) for i in range(6)]
    csd = [sb(es4c, f"csd{i}", [NS, CH], BF16) for i in range(4)]
    ysta = [sb(es4c, f"ysta{i}", [128, CH], BF16) for i in range(2)]

    prev_ready = plan.op(V, lambda e: e.tensor_copy(prevbb[:, :], prevb[:, :]),
                         inc=True)
    xs_war = None
    ysta_war = [None, None]
    abc_war = [None] * 4
    dc_war = [None] * 6
    pb_war = [None] * 6
    csd_war = [None] * 4
    offmm_last = None
    for c in range(NCHK):
        if c > 0:
            stl = plan.dma(SY, (lambda e, c=c: e.dma_start(
                out=stldb[:, :], in_=st_d[(c - 1) * NH:c * NH, :])), d_stl,
                deps=[(s_st, plan.dma_fills[s_st]), prev_ready])
            last = None
            for h in range(NH):
                last = plan.op(V, (lambda e, c=c, h=h: e.scalar_tensor_tensor(
                    prevb[:, h * PD:(h + 1) * PD], prevb[:, h * PD:(h + 1) * PD],
                    decb[:, (c - 1) * NH + h:(c - 1) * NH + h + 1],
                    stldb[:, h * PD:(h + 1) * PD], ALU.mult, ALU.add)),
                    deps=[stl, prev_ready], inc=True)
            prev_ready = plan.op(V, lambda e: e.tensor_copy(prevbb[:, :],
                                                            prevb[:, :]),
                                 deps=[last, offmm_last], inc=True)
        xl = load_xs_chunk(c, xs_war, d_xsc)
        last_et = None

        AH = 3

        def emit_abc(h, c=c):
            n = h % 4
            dq = SY if h % 2 == 0 else GP
            t = plan.dma(dq, (lambda e, c=c, h=h, n=n: e.dma_start(
                out=abc[n][:, :],
                in_=acum_d[h:h + 1, c * CH:(c + 1) * CH]
                    .partition_broadcast(128))),
                d_abc[n], deps=list(abc_war[n] or ()) + [(s_acum, 16)])
            return t

        abct = {}
        for h in range(AH):
            abct[h] = emit_abc(h)
        stA = {}
        stB = {}
        stC = {}
        stD = {}
        for hh in range(NH + 4):
            if hh < NH:
                h = hh
                n = h % 4
                a_list = []
                for s in range(2):
                    tt = 2 * c + s
                    r = (2 * h + s) % 6
                    t1 = plan.op(V, (lambda e, n=n, r=r, tt=tt, h=h:
                        e.tensor_scalar(
                            dc_sb[r][:, :], abc[n][:, :],
                            alcol[:, tt * NH + h: tt * NH + h + 1],
                            lndtT[:, tt * NH + h: tt * NH + h + 1],
                            ALU.subtract, ALU.min)),
                        deps=[abct[h], dc_war[r]], inc=True)
                    a_list.append((r, t1))
                b_list = []
                for (r, t1) in a_list:
                    t2 = plan.op(SC, (lambda e, r=r: e.activation(
                        lb_sb[r][:, :], dc_sb[r][:, :], AF.Exp)),
                        deps=[t1], inc=True)
                    dc_war[r] = t2
                    b_list.append((r, t2))
                cr = h % 4
                lvt = plan.op(SC, (lambda e, n=n, cr=cr: e.activation(
                    csd[cr][:, :], abc[n][0:NS, :], AF.Exp)),
                    deps=[abct[h], csd_war[cr]], inc=True)
                abc_war[n] = (a_list[-1][1], lvt)
                stA[h] = a_list
                stB[h] = (b_list, lvt)
                if h + AH < NH:
                    abct[h + AH] = emit_abc(h + AH)
            if 0 <= hh - 1 < NH:
                h = hh - 1
                cr = h % 4
                b_list, lvt = stB[h]
                c_list = []
                for (r, t2) in b_list:
                    s = len(c_list)
                    t3 = plan.op(V, (lambda e, r=r, c=c, s=s: e.tensor_tensor(
                        pb_sb[r][:, :], lb_sb[r][:, :],
                        smask[:, (2 * c + s) * CH:(2 * c + s + 1) * CH],
                        ALU.mult)), deps=[t2, pb_war[r]], inc=True)
                    c_list.append((r, t3))
                csd_t = plan.op(V, (lambda e, cr=cr, c=c: e.tensor_tensor(
                    csd[cr][:, :], csd[cr][:, :], cfm[:, c * CH:(c + 1) * CH],
                    ALU.mult)), deps=[lvt], inc=True)
                stC[h] = (c_list, csd_t)
                j = h // 2
                odd = h % 2
                yb = ps_mm[j % 4]
                tp = (0, 64) if odd else None
                out_sl = yb[64:128, 0:CH] if odd else yb[0:64, 0:CH]
                mms = []
                for s in range(2):
                    r, t3 = c_list[s]
                    mms.append(plan.op(TE, (lambda e, out_sl=out_sl, c=c, s=s,
                                            h=h, r=r, tp=tp: e.matmul(
                        out_sl,
                        xsT[:, c * 2 * I + s * I + h * PD:
                            c * 2 * I + s * I + (h + 1) * PD],
                        pb_sb[r][:, :], start=(s == 0), stop=False,
                        skip_group_check=True, tile_position=tp)),
                        deps=[t3] + ([bank_war.get(yb.name)]
                                     if s == 0 and not odd else []),
                        inc=True))
                    pb_war[r] = mms[-1]
                mms.append(plan.op(TE, (lambda e, out_sl=out_sl, h=h, cr=cr,
                                        tp=tp: e.matmul(
                    out_sl, prevbb[:, h * PD:(h + 1) * PD], csd[cr][:, :],
                    start=False, stop=True, skip_group_check=True,
                    tile_position=tp)),
                    deps=[csd_t, prev_ready], inc=True))
                csd_war[cr] = mms[-1]
                offmm_last = mms[-1]
                stD[h] = mms[-1]
            he = hh - 3
            if 0 <= he < NH and he % 2 == 1:
                j = he // 2
                yb = ps_mm[j % 4]
                yr = j % 2
                et = plan.op(V, (lambda e, yb=yb, j=j, yr=yr:
                    e.scalar_tensor_tensor(
                        ysta[yr][:, :], xs_c[:, j * CH:(j + 1) * CH],
                        dexp[:, j:j + 1], yb[:, 0:CH], ALU.mult, ALU.add)),
                    deps=[stD[he], ysta_war[yr], xl], inc=True)
                ydma = plan.dma(SY, (lambda e, j=j, c=c, yr=yr: e.dma_start(
                    out=y_d[j * 128:(j + 1) * 128, c * CH:(c + 1) * CH],
                    in_=ysta[yr][:, :])), s_y[yr], deps=[et])
                ysta_war[yr] = ydma
                bank_war[yb.name] = et
                last_et = et
        xs_war = last_et


    plan.barrier()
    stage_end()
    es4c.close()
    es4b.close()
    es4.close()
    es2.close()

    es5 = ExitStack()
    ytl = [sb(es5, f"ytl{i}", [128, 512], BF16) for i in range(2)]
    sgt = [sb(es5, f"sgt{i}", [128, 512], BF16) for i in range(2)]
    yp = sb(es5, "yp", [128, 32 * 512], BF16)
    ysq = [sb(es5, f"ysq{i}", [128, 512], BF16) for i in range(2)]
    rstdb = sb(es5, "rstdb", [128, 512], F32)
    rtile = [sb(es5, f"rtile{i}", [128, 512], F32) for i in range(2)]
    x1bf = [sb(es5, f"x1bf{i}", [128, 512], BF16) for i in range(2)]
    sum2row = sb(es5, "sum2row", [1, T], F32)
    sqr2row = sb(es5, "sqr2row", [1, T], F32)

    opj_last = None
    prev_r2 = None
    carry_sq = [None, None]
    opj_rt_war = {}
    opj_sf_war = {}
    opj_bf_war = {}
    opj_sq_done = {}
    for tb in range(2):
        ysq_war = list(carry_sq)
        sq_mm_last = None
        yp_tags = []
        for j in range(32):
            b = j % 2
            ylt = plan.dma(SY, (lambda e, j=j, tb=tb, b=b: e.dma_start(
                out=ytl[b][:, :], in_=y_d[j * 128:(j + 1) * 128,
                                          tb * 512:(tb + 1) * 512])),
                d_yt[b], deps=allw(s_y) + [yp_tags[-2] if j >= 2 else None])
            sgl = plan.dma(SY, (lambda e, j=j, tb=tb, b=b: e.dma_start(
                out=sgt[b][:, :], in_=sg_d[j * 128:(j + 1) * 128,
                                           tb * 512:(tb + 1) * 512])),
                d_sgt[b], deps=allw(s_stg) + [yp_tags[-2] if j >= 2 else None])
            ypt = plan.op(V, (lambda e, j=j, b=b: e.tensor_tensor(
                yp[:, j * 512:(j + 1) * 512], ytl[b][:, :], sgt[b][:, :], ALU.mult)),
                deps=[ylt, sgl, opj_last, prev_r2], inc=True)
            yp_tags.append(ypt)
            sqt = plan.op(SC, (lambda e, j=j, b=b: e.activation(
                ysq[b][:, :], yp[:, j * 512:(j + 1) * 512], AF.Square)),
                deps=[ypt, ysq_war[b]], inc=True)
            sq_mm_last = plan.op(TE, (lambda e, j=j, b=b: e.matmul(
                ps_row[0:1, :], onesbc[:, 0:1], ysq[b][:, :],
                start=(j == 0), stop=(j == 31), skip_group_check=True)),
                deps=[sqt, m_ones] + ([bank_war.get(ps_row.name)] if j == 0 else []),
                inc=True)
            ysq_war[b] = sq_mm_last
        r0 = plan.op(V, (lambda e, tb=tb: e.tensor_scalar(
            sum2row[0:1, tb * 512:(tb + 1) * 512], ps_row[0:1, :], 1.0 / I, EPS,
            ALU.mult, ALU.add)), deps=[sq_mm_last], inc=True)
        r1 = plan.op(SC, (lambda e, tb=tb: e.activation(
            sum2row[0:1, tb * 512:(tb + 1) * 512],
            sum2row[0:1, tb * 512:(tb + 1) * 512], AF.Ln)), deps=[r0], inc=True)
        r1b = plan.op(V, (lambda e, tb=tb: e.tensor_scalar(
            sum2row[0:1, tb * 512:(tb + 1) * 512],
            sum2row[0:1, tb * 512:(tb + 1) * 512], -0.5, None, ALU.mult)),
            deps=[r1], inc=True)
        r2 = plan.op(SC, (lambda e, tb=tb: e.activation(
            sum2row[0:1, tb * 512:(tb + 1) * 512],
            sum2row[0:1, tb * 512:(tb + 1) * 512], AF.Exp)), deps=[r1b], inc=True)
        bank_war[ps_row.name] = r0
        bk = next_bank()
        m3 = plan.op(TE, (lambda e, bk=bk, tb=tb: e.matmul(
            bk[:, :], onesf[0:1, :], sum2row[0:1, tb * 512:(tb + 1) * 512],
            start=True, stop=True, skip_group_check=True)),
            deps=[r2, m_ones, bank_war.get(bk.name)], inc=True)
        rb_t = plan.op(V, (lambda e, bk=bk: e.tensor_copy(rstdb[:, :], bk[:, :])),
                       deps=[m3], inc=True)
        bank_war[bk.name] = rb_t
        ypp_tags = []
        for j in range(32):
            ypp_tags.append(plan.op(V, (lambda e, j=j: e.scalar_tensor_tensor(
                yp[:, j * 512:(j + 1) * 512], yp[:, j * 512:(j + 1) * 512],
                nw[:, j:j + 1], rstdb[:, :], ALU.mult, ALU.mult)),
                deps=[yp_tags[j], rb_t], inc=True))
        ypp_all = ypp_tags[-1]

        def rhs_yp(kb, t0, tw):
            return yp[:, kb * 512: kb * 512 + 512]

        x1_evt = {"n": 0}

        def evac_outproj(meta, o_abs, t0, tw, bk2, mt2):
            tb_ = meta["tb"]
            b = x1_evt["n"] % 2
            x1_evt["n"] += 1
            rt = plan.dma(SY, (lambda e, o=o_abs, tb_=tb_, b=b: e.dma_start(
                out=rtile[b][:, :], in_=r_d[o:o + 128, tb_ * 512:(tb_ + 1) * 512])),
                d_rt[b], deps=allw(s_t2f) + [meta["rt_war"].get(b)])
            at = plan.op(V, (lambda e, bk2=bk2, b=b: e.tensor_add(
                stagef[b][:, :], bk2[:, :], rtile[b][:, :])),
                deps=[mt2, rt, meta["sf_war"].get(b), meta["sq_done"].get(b)],
                inc=True)
            meta["rt_war"][b] = at
            xd = plan.dma(SY, (lambda e, o=o_abs, tb_=tb_, b=b: e.dma_start(
                out=x1_d[o:o + 128, tb_ * 512:(tb_ + 1) * 512],
                in_=stagef[b][:, :])), s_sf[b], deps=[at])
            cst = plan.op(V, (lambda e, b=b: e.tensor_copy(
                x1bf[b][:, :], stagef[b][:, :])), deps=[at, meta["bf_war"].get(b)],
                inc=True)
            sq2 = plan.op(SC, (lambda e, b=b: e.activation(
                ysq[b][:, :], stagef[b][:, :], AF.Square)),
                deps=[at, meta["sq_war"].get(b)], inc=True)
            ki = meta["ki"]["n"]
            meta["ki"]["n"] += 1
            mm1 = plan.op(TE, (lambda e, b=b, ki=ki: e.matmul(
                ps_row[0:1, :], onesbc[:, 0:1], x1bf[b][:, :], start=(ki == 0),
                stop=(ki == 15), skip_group_check=True)),
                deps=[cst, m_ones] + ([bank_war.get(ps_row.name)] if ki == 0 else []),
                inc=True)
            mm2 = plan.op(TE, (lambda e, b=b, ki=ki: e.matmul(
                ps_e[0:1, :], onesbc[:, 0:1], ysq[b][:, :], start=(ki == 0),
                stop=(ki == 15), skip_group_check=True)),
                deps=[sq2, m_ones] + ([bank_war.get(ps_e.name)] if ki == 0 else []),
                inc=True)
            meta["bf_war"][b] = mm1
            meta["sq_war"][b] = mm2
            meta["sq_done"][b] = sq2
            meta["sf_war"][b] = xd
            meta["last_mm"] = (mm1, mm2, xd)
            return at

        meta2 = {"tb": tb, "tgrps": [(tb * 512, 512)], "rt_war": opj_rt_war,
                 "sf_war": opj_sf_war, "bf_war": opj_bf_war,
                 "sq_war": {0: sq_mm_last, 1: sq_mm_last},
                 "sq_done": opj_sq_done, "ki": {"n": 0}, "last_mm": None}
        slabs2 = [(o, 256, meta2) for o in range(0, H, 256)]
        opj_last = mm_stage(w2t, 32, slabs2, rhs_yp, evac_outproj,
                            extra_dep=ypp_all, slab_offs=W_OFFS['w2t'])
        prev_r2 = r2
        carry_sq = [meta2["sq_war"].get(0), meta2["sq_war"].get(1)]
        mm1, mm2, _ = meta2["last_mm"]
        ev1 = plan.op(V, (lambda e, tb=tb: e.tensor_copy(
            sum2row[0:1, tb * 512:(tb + 1) * 512], ps_row[0:1, :])),
            deps=[mm1], inc=True)
        ev2 = plan.op(V, (lambda e, tb=tb: e.tensor_copy(
            sqr2row[0:1, tb * 512:(tb + 1) * 512], ps_e[0:1, :])),
            deps=[mm2], inc=True)
        bank_war[ps_row.name] = ev1
        bank_war[ps_e.name] = ev2

    plan.barrier()
    stage_end()
    es5.close()

    es6 = ExitStack()
    m2row = sb(es6, "m2row", [1, T], F32)
    var2 = sb(es6, "var2", [1, T], F32)
    a2row = sb(es6, "a2row", [1, T], F32)
    mb2 = sb(es6, "mb2", [128, T], F32)
    ab2 = sb(es6, "ab2", [128, T], F32)
    x1t = [sb(es6, f"x1t{i}", [128, T], F32) for i in range(2)]
    jt = sb(es6, "jt", [128, T], F32)
    xb2 = sb(es6, "xb2", [128, KT * T], BF16)

    n1 = plan.op(V, lambda e: e.tensor_scalar(m2row[:, :], sum2row[:, :], 1.0 / H,
                                              None, ALU.mult), inc=True)
    n2 = plan.op(V, lambda e: e.tensor_mul(var2[:, :], m2row[:, :], m2row[:, :]),
                 deps=[n1], inc=True)
    n3 = plan.op(V, lambda e: e.scalar_tensor_tensor(
        var2[:, :], sqr2row[:, :], 1.0 / H, var2[:, :], ALU.mult, ALU.subtract),
        deps=[n2], inc=True)
    n3b = plan.op(V, lambda e: e.tensor_scalar(a2row[:, :], var2[:, :], 1.0, EPS,
                                               ALU.mult, ALU.add), deps=[n3],
                  inc=True)
    n4 = plan.op(SC, lambda e: e.activation(a2row[:, :], a2row[:, :], AF.Ln),
                 deps=[n3b], inc=True)
    n4b = plan.op(V, lambda e: e.tensor_scalar(a2row[:, :], a2row[:, :], -0.5, None,
                                               ALU.mult), deps=[n4], inc=True)
    n5 = plan.op(SC, lambda e: e.activation(a2row[:, :], a2row[:, :], AF.Exp),
                 deps=[n4b], inc=True)

    def bcast_row_t(src_row, dst_tile, dep):
        tags = []
        for t0 in (0, 512):
            bk = next_bank()
            mt = plan.op(TE, (lambda e, bk=bk, t0=t0, s=src_row: e.matmul(
                bk[:, :], onesf[0:1, :], s[0:1, t0:t0 + 512], start=True, stop=True,
                skip_group_check=True)), deps=[dep, m_ones, bank_war.get(bk.name)],
                inc=True)
            et = plan.op(V, (lambda e, bk=bk, t0=t0, d=dst_tile: e.tensor_copy(
                d[:, t0:t0 + 512], bk[:, :])), deps=[mt], inc=True)
            bank_war[bk.name] = et
            tags.append(et)
        return tags

    mb2_t = bcast_row_t(m2row, mb2, n1)
    ab2_t = bcast_row_t(a2row, ab2, n5)
    x1_war = [None, None]
    j2 = None
    for kt in range(KT):
        b = kt % 2
        xlt = plan.dma(SY, (lambda e, kt=kt, b=b: e.dma_start(
            out=x1t[b][:, :], in_=x1_d[kt * 128:(kt + 1) * 128, :])), d_x1t[b],
            deps=allw(s_sf) + [x1_war[b]])
        j1 = plan.op(V, (lambda e, b=b: e.tensor_sub(jt[:, :], x1t[b][:, :],
                                                     mb2[:, :])),
                     deps=[xlt, j2] + mb2_t, inc=True)
        j2 = plan.op(V, (lambda e, kt=kt: e.tensor_mul(
            xb2[:, kt * T:(kt + 1) * T], jt[:, :], ab2[:, :])),
            deps=[j1] + ab2_t, inc=True)
        x1_war[b] = j2

    plan.barrier()
    stage_end()

    es7 = ExitStack()
    sgf = sb(es7, "sgf", [128, 8 * 512], BF16)

    def rhs_xb2(kb, t0, tw):
        return xb2[:, kb * T + t0: kb * T + t0 + tw]

    sgf_war = {}

    def evac_ffn1(meta, o_abs, t0, tw, bk, mt):
        if meta["kind"] == "fg":
            gi = (o_abs % 512) // 128 * 2 + (t0 // 512)
            et = plan.op(SC, (lambda e, gi=gi, bk=bk: e.activation(
                sgf[:, gi * 512:(gi + 1) * 512], bk[:, :], AF.Silu)),
                deps=[mt, sgf_war.get(gi)], inc=True)
            sgf_war[gi] = et
            meta.setdefault("sg_tags", {})[gi] = et
            return et
        else:
            gi = (o_abs % 512) // 128 * 2 + (t0 // 512)
            gtag = meta["g_meta"]["sg_tags"][gi]
            s = next_stage()
            et = plan.op(V, (lambda e, gi=gi, bk=bk, s=s: e.tensor_tensor(
                stage[s][:, :], bk[:, :], sgf[:, gi * 512:(gi + 1) * 512], ALU.mult)),
                deps=[mt, gtag, stg_war[s]], inc=True)
            kb_ = (o_abs - INTER) // 128
            tb_ = t0 // 512
            dmat = plan.dma(SY, (lambda e, kb_=kb_, tb_=tb_, s=s: e.dma_start(
                out=h_d[:, tb_ * 32768 + kb_ * 512: tb_ * 32768 + (kb_ + 1) * 512],
                in_=stage[s][:, :])),
                s_stg[s], deps=[et])
            stg_war[s] = dmat
            sgf_war[gi] = et
            return et

    TG_T = [(0, 512), (512, 512)]
    slabs3 = []
    for gi in range(INTER // 512):
        mg = {"kind": "fg", "tgrps": TG_T}
        mu = {"kind": "fu", "tgrps": TG_T, "g_meta": mg}
        slabs3.append((gi * 512, 512, mg))
        slabs3.append((INTER + gi * 512, 512, mu))
    mm_stage(w3t, KT, slabs3, rhs_xb2, evac_ffn1,
             slab_offs=W_OFFS['w3t'])

    plan.barrier()
    stage_end()
    es7.close()

    es8 = ExitStack()
    hblk = sb(es8, "hblk", [128, 64 * 512], BF16)
    rtile2 = [sb(es8, f"rtile2_{i}", [128, 512], F32) for i in range(2)]
    KB2 = INTER // 128
    ffn2_last = None
    x1w = {}
    for tb in range(2):
        hl = plan.dma(SY, (lambda e, tb=tb: e.dma_start(
            out=hblk[:, :], in_=h_d[:, tb * 32768:(tb + 1) * 32768])),
            d_hblk, deps=allw(s_stg) + [ffn2_last])

        def rhs_h(kb, t0, tw):
            return hblk[:, kb * 512: kb * 512 + 512]

        def evac_ffn2(meta, o_abs, t0, tw, bk, mt, tb=tb):
            b = (o_abs // 128) % 2
            xlt = plan.dma(SY, (lambda e, o=o_abs, tb=tb, b=b: e.dma_start(
                out=rtile2[b][:, :], in_=x1_d[o:o + 128, tb * 512:(tb + 1) * 512])),
                d_x1t[b], deps=allw(s_sf) + [x1w.get(b)])
            at = plan.op(V, (lambda e, bk=bk, b=b: e.tensor_add(
                stagef[b][:, :], bk[:, :], rtile2[b][:, :])),
                deps=[mt, xlt, x1w.get(("sf", b))], inc=True)
            od = plan.dma(SY, (lambda e, o=o_abs, tb=tb, b=b: e.dma_start(
                out=out_hbm[o:o + 128, tb * 512:(tb + 1) * 512],
                in_=stagef[b][:, :])), s_sf[b], deps=[at])
            x1w[b] = at
            x1w[("sf", b)] = od
            return at

        meta4 = {"tgrps": [(tb * 512, 512)]}
        slabs4 = [(o, 128, meta4) for o in range(0, H, 128)]
        ffn2_last = mm_stage(w4t, KB2, slabs4, rhs_h, evac_ffn2,
                             extra_dep=hl, slab_offs=W_OFFS['w4t'])

    plan.barrier()
    stage_end()

    sems = {}
    sem_stack = ExitStack()
    for name in plan.sem_names:
        sems[name] = sem_stack.enter_context(nc.semaphore(name))
    global _SEMS
    _SEMS = sems

    with nc.Block() as block:
        def runner(engname):
            def run(e):
                for item in plan.ops[engname]:
                    if item[0] == "wait":
                        e.wait_ge(sems[item[1]], item[2])
                    else:
                        inst = item[1](e)
                        tag = item[2]
                        if tag is not None and inst is not None:
                            inst.then_inc(sems[tag[0]],
                                          16 if item[0] == "dma" else 1)
            return run
        block.tensor(runner(TE))
        block.vector(runner(V))
        block.scalar(runner(SC))
        block.gpsimd(runner(GP))
        block.sync(runner(SY))

    sem_stack.close()
    es8.close()
    es6.close()
    es0.close()
    return nc


_SEMS = None


def _tile_w(WT, slabs, kb_n):
    parts = []
    for (o0, w) in slabs:
        blk = WT[:, o0:o0 + w].reshape(kb_n, 128, w).transpose(1, 0, 2)
        parts.append(blk.reshape(128, kb_n * w))
    return np.ascontiguousarray(np.concatenate(parts, axis=1))


def _host_inputs(full):
    x = np.asarray(full["L0_attn_hidden_states"], np.float32)[0]
    xT = np.ascontiguousarray(x.T)
    w1t = _tile_w(np.asarray(full["L0_attn_in_proj_w"], np.float32).T,
                  SLABS_W1, KT)
    w2t = _tile_w(np.asarray(full["L0_attn_out_proj_w"], np.float32).T,
                  SLABS_W2, 32)
    w3t = _tile_w(np.asarray(full["L0_ffn_w_gate_up"], np.float32).T,
                  SLABS_W3, KT)
    w4t = _tile_w(np.asarray(full["L0_ffn_w_down"], np.float32).T,
                  SLABS_W4, INTER // 128)
    lnw = np.ascontiguousarray(
        np.asarray(full["L0_attn_ln_weight"], np.float32).reshape(KT, 128).T)
    nw = np.ascontiguousarray(
        np.asarray(full["L0_attn_norm_weight"], np.float32).reshape(I // 128, 128).T)
    dexp = np.ascontiguousarray(
        np.repeat(np.asarray(full["L0_attn_D"], np.float32), PD)
        .reshape(I // 128, 128).T)
    cwf = np.asarray(full["L0_attn_conv_w"], np.float32)
    cw = np.ascontiguousarray(
        cwf.reshape(33, 128, 4).transpose(1, 0, 2).reshape(128, 33 * 4))
    cb = np.ascontiguousarray(
        np.asarray(full["L0_attn_conv_b"], np.float32).reshape(33, 128).T)
    dtb = np.asarray(full["L0_attn_dt_bias"], np.float32).reshape(NH, 1)
    alog = np.asarray(full["L0_attn_A_log"], np.float32).reshape(NH, 1)
    import ml_dtypes
    sidx = np.arange(128)[:, None]
    lidx = np.arange(CH)[None, :]
    tri0 = (sidx <= lidx).astype(ml_dtypes.bfloat16)
    tri1 = ((sidx + 128) <= lidx).astype(ml_dtypes.bfloat16)
    tri = np.ascontiguousarray(np.concatenate([tri0, tri1], axis=1))
    identf = np.eye(128, dtype=np.float32)
    identb = np.eye(128, dtype=ml_dtypes.bfloat16)

    shared = dict(w1t=w1t, w2t=w2t, w3t=w3t, w4t=w4t, lnw=lnw, nw=nw, dexp=dexp,
                  cw=cw, cb=cb, dtb=dtb, alog=alog, tri=tri, identf=identf,
                  identb=identb)
    in_maps = []
    for r in range(NCORES):
        lo = r * T - 3
        if lo < 0:
            xs = np.concatenate(
                [np.zeros((H, 3), np.float32), xT[:, 0:T]], axis=1)
        else:
            xs = xT[:, lo:lo + TH]
        m = dict(shared)
        m["xt"] = np.ascontiguousarray(
            xs.reshape(KT, 128, TH).transpose(1, 0, 2).reshape(128, KT * TH))
        m["rmask"] = (np.arange(NCORES) < r).astype(np.float32).reshape(1, NCORES)
        in_maps.append(m)
    return in_maps


def kernel(**inputs):
    in_maps = _host_inputs(inputs)
    nc = build_nc()
    res = run_bass_kernel_spmd(nc, in_maps, core_ids=list(range(NCORES)))
    full = np.concatenate([np.asarray(res.results[r]["out"]) for r in range(NCORES)],
                          axis=1)
    return np.ascontiguousarray(full.T).reshape(1, 8192, H).astype(np.float32)
